# version 06
# baseline (speedup 1.0000x reference)
import numpy as np
import concourse.bass as bass
from concourse.bass_utils import run_bass_kernel_spmd
import concourse.mybir as mybir

F32 = mybir.dt.float32
AF = mybir.ActivationFunctionType
ALU = mybir.AluOpType
AX = mybir.AxisListType

ENGS = ("sync", "tensor", "vector", "scalar", "gpsimd")

SCALARS = {
    "eps6": 1e-6, "eps5": 1e-5, "eps8": 1e-8, "eps12": 1e-12,
    "inv256": 1.0 / 256, "inv1024": 1.0 / 1024,
}
CNT_ENG = {"mm": "tensor", "vv": "vector", "ss": "scalar", "gg": "gpsimd"}


class Geo:
    def __init__(self, TCORE=4096, TILE=512):
        self.D = 1024
        self.H = 4
        self.DK = 256
        self.TCORE = TCORE
        self.TILE = TILE
        self.NSUB = TILE // 128
        self.NT = TCORE // TILE
        self.NCH = TILE // 32
        assert TILE % 128 == 0 and TCORE % TILE == 0


class Counter:
    def __init__(self, nc, name, dma=False):
        self.sem = nc.alloc_semaphore(name)
        self.total = 0
        self.step = 16 if dma else 1

    def bump(self):
        self.total += self.step
        return self.total


class PsumPool:

    def __init__(self, banks, base):
        self.banks = banks
        self.base = base

    def __getitem__(self, key):
        if not isinstance(key, tuple):
            key = (key, slice(None))
        ps, cs = key
        c0 = 0 if cs.start is None else cs.start
        c1 = 512 * 4 if cs.stop is None else cs.stop
        b = c0 // 512
        assert c1 - c0 <= 512 and (c1 - 1) // 512 == b, ("bank cross", c0, c1)
        return self.banks[self.base + b][ps, c0 - b * 512: c1 - b * 512]


class Ctx:
    def __init__(self, nc, geo):
        self.nc = nc
        self.g = geo
        self.stages = []
        self.cur = None
        self.t = {}
        self.dram = {}
        self._counters = {}

    def sbuf(self, name, shape, dtype=F32):
        h = self.nc.alloc_sbuf_tensor(name, shape, dtype)
        self.t[name] = h
        return h

    def psum(self, name, shape, dtype=F32):
        h = self.nc.alloc_psum_tensor(name, shape, dtype)
        self.t[name] = h
        return h

    def counter(self, name, dma=False):
        if name not in self._counters:
            self._counters[name] = Counter(self.nc, name, dma)
        return self._counters[name]

    def din(self, name, shape, dtype=F32):
        h = self.nc.dram_tensor(name, list(shape), dtype, kind="ExternalInput")
        self.dram[name] = h
        return h

    def dout(self, name, shape, dtype=F32):
        h = self.nc.dram_tensor(name, list(shape), dtype, kind="ExternalOutput")
        self.dram[name] = h
        return h

    def phase_begin(self):
        self._ph_base = self.nc.sbuf_base
        self._ph_names = set(self.t.keys())

    def phase_end(self):
        self.nc.sbuf_base = self._ph_base
        for n in list(self.t):
            if n not in self._ph_names:
                del self.t[n]

    def alloc_shared(self, consts_w):
        self.sbuf("consts_sb", [128, consts_w])
        self.sbuf("wbuf", [128, 8 * 1024])
        self.sbuf("scr", [128, 512])
        banks = [self.psum("PB%d" % i, [128, 512]) for i in range(8)]
        self.t["P03"] = PsumPool(banks, 0)
        self.t["P47"] = PsumPool(banks, 4)

    def stage(self, name):
        self.cur = {e: [] for e in ENGS}
        self.cur["_name"] = name
        self.stages.append(self.cur)

    def op(self, eng, fn):
        self.cur[eng].append(fn)

    def run(self, eng, build, waits=(), inc=None):
        val = None
        if inc is not None:
            val = inc.bump()
        def fn(e, build=build, waits=tuple(waits), inc=inc):
            for c, v in waits:
                if v > 0:
                    e.wait_ge(c.sem, v)
            ins = build(e)
            if inc is not None:
                ins.then_inc(inc.sem, inc.step)
        self.op(eng, fn)
        return val

    def lock(self, e, eng, ins):
        c = self.counter("c_lk_" + eng)
        ins.then_inc(c.sem, 1)
        c.total += 1
        e.wait_ge(c.sem, c.total)
        return ins

    def dma(self, eng, cnt, out_ap, in_ap, waits=()):
        return self.run(eng,
                        lambda e: e.dma_start(out=out_ap, in_=in_ap),
                        waits=waits, inc=cnt)

    def wait(self, eng, cnt, val=None):
        v = cnt.total if val is None else val
        self.op(eng, lambda e, v=v: e.wait_ge(cnt.sem, v))

    def emit(self):
        done = {e: self.counter("done_" + e) for e in ENGS}
        stages = self.stages
        cum = {e: [0] for e in ENGS}
        for st in stages:
            for e in ENGS:
                cum[e].append(cum[e][-1] + (1 if st[e] else 0))

        def engine_body(ename):
            def body(eng):
                for i, st in enumerate(stages):
                    if not st[ename]:
                        continue
                    for o in ENGS:
                        if o != ename and cum[o][i] > 0:
                            eng.wait_ge(done[o].sem, cum[o][i])
                    for fn in st[ename]:
                        fn(eng)
                    if ename == "tensor":
                        eng.nop().then_inc(done[ename].sem, 1)
                    else:
                        eng.drain().then_inc(done[ename].sem, 1)
                for o in ENGS:
                    if o != ename and cum[o][-1] > 0:
                        eng.wait_ge(done[o].sem, cum[o][-1])
            return body

        with self.nc.Block() as block:
            block.sync(engine_body("sync"))
            block.tensor(engine_body("tensor"))
            block.vector(engine_body("vector"))
            block.scalar(engine_body("scalar"))
            block.gpsimd(engine_body("gpsimd"))



def lhsT_of(w):
    return np.ascontiguousarray(np.asarray(w).T.astype(np.float32))


def conv_w_layout(wc):
    wc = np.asarray(wc, dtype=np.float32)
    K = wc.shape[1]
    img = np.zeros((128, 8 * K), np.float32)
    for fc in range(8):
        img[:, fc * K:(fc + 1) * K] = wc[fc * 128:(fc + 1) * 128, :]
    return img


def consts_image():
    off = {}
    cols = []

    def add(name, arr):
        off[name] = sum(c.shape[1] for c in cols)
        cols.append(arr.astype(np.float32))

    add("ident", np.eye(128, dtype=np.float32))
    add("trilS", np.tile(np.tril(np.ones((32, 32), np.float32), -1), (4, 1)))
    add("triuI", np.tile(np.triu(np.ones((32, 32), np.float32), 0), (4, 1)))
    add("eye2", np.tile(2.0 * np.eye(32, dtype=np.float32), (4, 1)))
    add("eye1", np.tile(np.eye(32, dtype=np.float32), (4, 1)))
    add("ones", np.ones((128, 32), np.float32))
    for i, (nm, v) in enumerate(SCALARS.items()):
        add("s_" + nm, np.full((128, 1), v, np.float32))
    img = np.concatenate(cols, axis=1)
    return img, off



class L0:
    def __init__(self, ctx, coff, consts_w, dbg=()):
        self.ctx = ctx
        self.g = ctx.g
        self.coff = coff
        self.dbg = set(dbg)
        g = self.g
        T = g.TILE
        def get(name, shape):
            return ctx.t[name] if name in ctx.t else ctx.sbuf(name, shape)
        self.consts = get("consts_sb", [128, consts_w])
        self.x_tm = ctx.sbuf("x_tm", [128, g.NSUB * 1024])
        self.x_fm = ctx.sbuf("x_fm", [128, 8 * T])
        self.wbuf = get("wbuf", [128, 8 * 1024])
        self.bw_sb = ctx.sbuf("bw_sb", [128, 8 * 4])
        self.wconv = ctx.sbuf("wconv", [128, 3 * 8 * 4])
        self.raw = ctx.sbuf("raw", [128, 8 * (3 + T)])
        self.tail = ctx.sbuf("tail", [128, 3 * 8 * 3])
        self.k_fm = ctx.sbuf("k_fm", [128, 8 * T])
        self.VS = 63 + T
        self.v_fm = ctx.sbuf("v_fm", [128, 8 * self.VS])
        self.k_tm = ctx.sbuf("k_tm", [128, g.NSUB * 1024])
        self.v_tm = ctx.sbuf("v_tm", [128, g.NSUB * 1024])
        self.kb = ctx.sbuf("kb", [128, 1024])
        self.vb = ctx.sbuf("vb", [128, 1024])
        self.beta = ctx.sbuf("beta", [128, g.NSUB * 4])
        self.nrm = ctx.sbuf("nrm", [128, 3 * g.NSUB * 4])
        self.invT = ctx.sbuf("invT", [128, 16 * 32])
        self.nwt = ctx.sbuf("nwt", [128, 5 * 32])
        self.u_sb = ctx.sbuf("u_sb", [128, g.NSUB * 1024])
        self.w_sb = ctx.sbuf("w_sb", [128, g.NSUB * 4 * 256])
        self.uadj = ctx.sbuf("uadj", [128, 1024])
        self.S = ctx.sbuf("S", [128, 2048])
        self.scr = get("scr", [128, 512])
        self.pass2 = False
        self.P03 = ctx.t["P03"]
        self.P47 = ctx.t["P47"]
        self.PB = self.P03.banks
        c = ctx.counter
        self.dma = c("c_dma", dma=True)
        self.dmo = c("c_dmo", dma=True)
        self.mm = c("c_mm")
        self.vv = c("c_vv")
        self.ss = c("c_ss")
        self.gg = c("c_gg")
        self._tp_marks = []

    def C(self, name, w=32):
        o = self.coff[name]
        return self.consts[:, o:o + w]

    def transpose128(self, pairs, evac_engs=("vector", "scalar")):
        ctx = self.ctx
        ident = self.C("ident", 128)
        for idx0, (src, dst) in enumerate(pairs):
            idx = len(self._tp_marks)
            bo = (idx % 4) * 512
            waits = []
            if idx >= 4:
                cnt, mark = self._tp_marks[idx - 4]
                waits.append((cnt, mark))
            mmv = ctx.run("tensor",
                          lambda e, src=src, bo=bo:
                          e.transpose(self.P47[:, bo:bo + 128], src, ident),
                          waits=waits, inc=self.mm)
            eng = evac_engs[idx % len(evac_engs)]
            cnt = self.vv if eng == "vector" else self.ss
            if eng == "vector":
                mark = ctx.run(eng,
                               lambda e, dst=dst, bo=bo:
                               e.tensor_copy(dst, self.P47[:, bo:bo + 128]),
                               waits=[(self.mm, mmv)], inc=cnt)
            else:
                mark = ctx.run(eng,
                               lambda e, dst=dst, bo=bo:
                               e.copy(dst, self.P47[:, bo:bo + 128]),
                               waits=[(self.mm, mmv)], inc=cnt)
            self._tp_marks.append((cnt, mark))

    def load_w(self, wname, n_out=1024):
        ctx = self.ctx
        for kc in range(8):
            ctx.dma("sync", self.dma,
                    self.wbuf[:, kc * n_out:(kc + 1) * n_out],
                    ctx.dram[wname].ap()[kc * 128:(kc + 1) * 128, :])
        ctx.wait("sync", self.dma)

    def proj_fm(self, evac, ntok=None, n_out=1024):
        ctx, T = self.ctx, self.g.TILE
        nt = T if ntok is None else ntok
        noc = n_out // 128
        marks = []
        for oc in range(noc):
            reg = 512 * (oc % 2)
            waits = [(self.vv, marks[oc - 2])] if oc >= 2 else []
            for kc in range(8):
                mmv = ctx.run("tensor",
                              lambda e, kc=kc, oc=oc, reg=reg, nt=nt,
                              n_out=n_out:
                              e.matmul(self.P03[:, reg:reg + nt],
                                       self.wbuf[:, kc * n_out + oc * 128:
                                                 kc * n_out + oc * 128 + 128],
                                       self.x_fm[:, kc * T:kc * T + nt],
                                       start=(kc == 0), stop=(kc == 7)),
                              waits=(waits if kc == 0 else []), inc=self.mm)
            b = evac(oc)
            marks.append(ctx.run("vector", lambda e, b=b, reg=reg: b(e, reg),
                                 waits=[(self.mm, self.mm.total)], inc=self.vv))

    def conv4(self, out_fm, wcol_base, eng_name, ntok=None, out_stride=None,
              out_off=0):
        ctx, T = self.ctx, self.g.TILE
        nt = T if ntok is None else ntok
        ostr = T if out_stride is None else out_stride
        for fc in range(8):
            base = fc * (3 + T)
            for j in range(4):
                def c_fn(e, fc=fc, j=j, base=base, nt=nt, out_fm=out_fm,
                         wcol_base=wcol_base, ostr=ostr, out_off=out_off):
                    src = self.raw[:, base + j: base + j + nt]
                    dst = out_fm[:, fc * ostr + out_off:
                                 fc * ostr + out_off + nt]
                    wj = self.wconv[:, wcol_base + fc * 4 + j:
                                    wcol_base + fc * 4 + j + 1]
                    if j == 0:
                        e.tensor_scalar(out=dst, in0=src, scalar1=wj,
                                        scalar2=None, op0=ALU.mult)
                    else:
                        e.scalar_tensor_tensor(out=dst, in0=src, scalar=wj,
                                               in1=dst, op0=ALU.mult,
                                               op1=ALU.add)
                ctx.op(eng_name, c_fn)

    def margins_in(self, tail_base, eng="vector"):
        ctx, T = self.ctx, self.g.TILE
        for fc in range(8):
            ctx.op(eng, lambda e, fc=fc, tb=tail_base:
                   e.tensor_copy(self.raw[:, fc * (3 + T): fc * (3 + T) + 3],
                                 self.tail[:, tb + fc * 3: tb + fc * 3 + 3]))

    def tails_out(self, tail_base, ntok=None, eng="vector"):
        ctx, T = self.ctx, self.g.TILE
        nt = T if ntok is None else ntok
        for fc in range(8):
            ctx.op(eng, lambda e, fc=fc, tb=tail_base, nt=nt:
                   e.tensor_copy(self.tail[:, tb + fc * 3: tb + fc * 3 + 3],
                                 self.raw[:, fc * (3 + T) + nt:
                                          fc * (3 + T) + nt + 3]))

    def silu_fm(self, buf, ntok=None, stride=None, off=0):
        ctx, T = self.ctx, self.g.TILE
        nt = T if ntok is None else ntok
        st = T if stride is None else stride
        for fc in range(8):
            ctx.op("scalar", lambda e, fc=fc, nt=nt, st=st, off=off:
                   e.activation(out=buf[:, fc * st + off:fc * st + off + nt],
                                in_=buf[:, fc * st + off:fc * st + off + nt],
                                func=AF.Silu))

    def alloc_pass2(self):
        ctx, g = self.ctx, self.g
        T = g.TILE
        self.pass2 = True
        self.q_fm = ctx.sbuf("q_fm", [128, 8 * T])
        self.qnrm = ctx.sbuf("qnrm", [128, 2 * g.NSUB * 4])
        self.at_sb = ctx.sbuf("at_sb", [128, g.NSUB * 4 * 32])
        self.delta_tm = ctx.sbuf("delta_tm", [128, g.NSUB * 1024])
        self.fir_w = ctx.sbuf("fir_w", [128, 8 * 98])
        self.short_fm = ctx.sbuf("short_fm", [128, 8 * T])
        self.long_fm = ctx.sbuf("long_fm", [128, 8 * T])
        self.wide_fm = ctx.sbuf("wide_fm", [128, 8 * T])
        self.slw_tm = ctx.sbuf("slw_tm", [128, 3 * 1024])
        self.stat_tm = ctx.sbuf("stat_tm", [128, 64])
        self.gstat_fm = ctx.sbuf("gstat_fm", [128, T])
        self.hdn_fm = ctx.sbuf("hdn_fm", [128, g.NSUB * 2048])
        self.wsl = ctx.sbuf("wsl", [128, 2 * 1152])
        self.g2_sb = ctx.sbuf("g2_sb", [128, 16 * 20])
        self.g1b_sb = ctx.sbuf("g1b_sb", [128, 16])
        self.g2b_sb = ctx.sbuf("g2b_sb", [32, 4])
        self.ltexp_sb = ctx.sbuf("ltexp_sb", [32, 2])
        self.fsc_sb = ctx.sbuf("fsc_sb", [128, 12])
        self.onorm_sb = ctx.sbuf("onorm_sb", [128, 1024])
        self.wsm = ctx.sbuf("wsm", [128, 20])
        self.x1_tm = ctx.sbuf("x1_tm", [128, g.NSUB * 1024])

    def build_pass(self, with_outputs, s_init_name=None, s_init_mask=None):
        ctx, g, T = self.ctx, self.g, self.g.TILE
        if with_outputs and not self.pass2:
            self.alloc_pass2()
        ctx.stage("init" + ("_p2" if with_outputs else ""))
        ctx.dma("sync", self.dma, self.bw_sb[:], ctx.dram["bw_img"].ap())
        ctx.dma("sync", self.dma, self.wconv[:], ctx.dram["wconv_img"].ap())
        ctx.dma("sync", self.dma, self.consts[:], ctx.dram["consts_img"].ap())
        if s_init_name is not None:
            sap = ctx.dram[s_init_name]
            if list(sap.shape) == [128, 2048]:
                ctx.dma("sync", self.dma, self.S[:], sap.ap())
            else:
                ctx.dma("sync", self.dma,
                        self.S[:].rearrange("p (h c v) -> p h c v", h=4, c=2),
                        sap.ap().rearrange("h (c p) v -> p h c v", p=128))
        ctx.wait("sync", self.dma)
        if with_outputs:
            self._l0_init_post()
        if s_init_name is None:
            ctx.op("vector", lambda e: e.memset(self.S[:], 0.0))
        elif s_init_mask is not None:
            ctx.stage("sinit_mask" + ("_p2" if with_outputs else ""))
            ctx.op("vector", lambda e: e.tensor_scalar(
                out=self.S[:], in0=self.S[:], scalar1=s_init_mask,
                scalar2=None, op0=ALU.mult))
        ctx.op("vector", lambda e: e.memset(self.tail[:], 0.0))
        ctx.op("vector", lambda e: e.memset(self.raw[:], 0.0))
        ctx.op("vector", lambda e: e.memset(self.x_fm[:], 0.0))
        ctx.op("vector", lambda e: e.memset(self.uadj[:], 0.0))

        self.halo_pretile(with_outputs)
        for t in range(g.NT):
            self.tile_front(t, with_outputs)
            for s in range(g.NSUB):
                self.prep_subtile(t, s, with_outputs)
            for cj in range(g.NCH):
                self.scan_step(t, cj, with_outputs)
            if with_outputs:
                self._l0_post(t)
            self.dump_dbg(t)

        ctx.stage("s_out")
        if "S_local" in ctx.dram:
            ctx.dma("sync", self.dmo,
                    ctx.dram["S_local"].ap().rearrange(
                        "h (c p) v -> p h c v", p=128),
                    self.S[:].rearrange("p (h c v) -> p h c v", h=4, c=2))
        ctx.wait("sync", self.dmo)

    def halo_pretile(self, with_outputs):
        ctx, g, T = self.ctx, self.g, self.g.TILE
        ctx.stage("halo_load")
        ctx.dma("sync", self.dma, self.x_tm[:, 0:1024],
                ctx.dram["x_halo"].ap())
        ctx.wait("sync", self.dma)
        ctx.stage("halo_tp")
        self.transpose128(
            [(self.x_tm[:, fc * 128:(fc + 1) * 128],
              self.x_fm[:, fc * T:fc * T + 128]) for fc in range(8)])
        plist = [("wk_img", 24), ("wv_img", 48)]
        if with_outputs:
            plist = [("wq_img", 0)] + plist
        for wname, tb in plist:
            ctx.stage("halo_w_" + wname)
            self.load_w(wname)
            ctx.stage("halo_" + wname)
            self.proj_fm(lambda oc: (lambda e, reg, oc=oc:
                                     e.tensor_copy(
                                         self.raw[:, oc * (3 + T) + 3:
                                                  oc * (3 + T) + 3 + 128],
                                         self.P03[:, reg:reg + 128])),
                         ntok=128)
            ctx.stage("halo_tail_" + wname)
            self.tails_out(tb, ntok=128)
            if wname == "wv_img" and with_outputs:
                ctx.stage("halo_vhist")
                self.conv4(self.v_fm, 64, "vector", ntok=128,
                           out_stride=self.VS, out_off=63)
                ctx.stage("halo_vhist_silu")
                self.silu_fm(self.v_fm, ntok=128, stride=self.VS, off=63)
                ctx.stage("halo_vhist2")
                for fc in range(8):
                    ctx.op("vector", lambda e, fc=fc:
                           e.tensor_copy(
                               self.v_fm[:, fc * self.VS:fc * self.VS + 63],
                               self.v_fm[:, fc * self.VS + 63 + 128 - 63:
                                         fc * self.VS + 63 + 128]))

    def tile_front(self, t, with_outputs):
        ctx, g, T = self.ctx, self.g, self.g.TILE
        x = ctx.dram["x"]
        for s in range(g.NSUB):
            ctx.stage("x_t%d_s%d" % (t, s))
            ctx.dma("sync", self.dma, self.x_tm[:, s * 1024:(s + 1) * 1024],
                    x.ap()[t * T + s * 128: t * T + s * 128 + 128, :])
            ctx.wait("sync", self.dma)
            ctx.stage("x_tp_t%d_s%d" % (t, s))
            self.transpose128(
                [(self.x_tm[:, s * 1024 + fc * 128:s * 1024 + (fc + 1) * 128],
                  self.x_fm[:, fc * T + s * 128:fc * T + s * 128 + 128])
                 for fc in range(8)])

        ctx.stage("beta_t%d" % t)
        for s in range(g.NSUB):
            reg = 1024 + 512 * (s % 2)
            for kc in range(8):
                ctx.run("tensor",
                        lambda e, s=s, kc=kc, reg=reg:
                        e.matmul(self.P03[:, reg:reg + 4],
                                 self.x_fm[:, kc * T + s * 128:
                                           kc * T + s * 128 + 128],
                                 self.bw_sb[:, kc * 4:(kc + 1) * 4],
                                 start=(kc == 0), stop=(kc == 7)),
                        inc=self.mm)
            ctx.run("scalar",
                    lambda e, s=s, reg=reg:
                    e.activation(out=self.beta[:, s * 4:(s + 1) * 4],
                                 in_=self.P03[:, reg:reg + 4],
                                 func=AF.Sigmoid),
                    waits=[(self.mm, self.mm.total)], inc=self.ss)

        plist = [("wk_img", self.k_fm, 24, 32, "vector", T, 0),
                 ("wv_img", self.v_fm, 48, 64, "vector", self.VS, 63)]
        if with_outputs:
            plist = [("wq_img", self.q_fm, 0, 0, "vector", T, 0)] + plist
        for wname, out_buf, tb, wcol, conv_eng, ostr, ooff in plist:
            ctx.stage("ldw_%s_t%d" % (wname, t))
            self.load_w(wname)
            ctx.stage("proj_%s_t%d" % (wname, t))
            self.proj_fm(lambda oc: (lambda e, reg, oc=oc:
                                     e.tensor_copy(
                                         self.raw[:, oc * (3 + T) + 3:
                                                  oc * (3 + T) + 3 + T],
                                         self.P03[:, reg:reg + T])))
            ctx.stage("conv_m_%s_t%d" % (wname, t))
            self.margins_in(tb)
            ctx.stage("conv_t_%s_t%d" % (wname, t))
            self.tails_out(tb)
            if wname == "wv_img" and t > 0:
                ctx.stage("vmargin_t%d" % t)
                for fc in range(8):
                    ctx.op("vector", lambda e, fc=fc:
                           e.tensor_copy(
                               self.v_fm[:, fc * self.VS:fc * self.VS + 63],
                               self.v_fm[:, fc * self.VS + T:
                                         fc * self.VS + T + 63]))
            ctx.stage("conv_%s_t%d" % (wname, t))
            self.conv4(out_buf, wcol, conv_eng, out_stride=ostr, out_off=ooff)
            ctx.stage("silu_%s_t%d" % (wname, t))
            self.silu_fm(out_buf, stride=ostr, off=ooff)

        ctx.stage("kv_tp_t%d" % t)
        pairs = []
        for buf_fm, buf_tm, st_, of_ in (
                (self.k_fm, self.k_tm, T, 0),
                (self.v_fm, self.v_tm, self.VS, 63)):
            for s in range(self.g.NSUB):
                for fc in range(8):
                    pairs.append((buf_fm[:, fc * st_ + of_ + s * 128:
                                         fc * st_ + of_ + s * 128 + 128],
                                  buf_tm[:, s * 1024 + fc * 128:
                                         s * 1024 + fc * 128 + 128]))
        self.transpose128(pairs)

        ctx.stage("norm_t%d" % t)
        NS4 = self.g.NSUB * 4
        for s in range(self.g.NSUB):
            for h in range(4):
                ctx.op("scalar", lambda e, s=s, h=h:
                       e.activation(out=self.scr[:, 0:256],
                                    in_=self.k_tm[:, s * 1024 + h * 256:
                                                  s * 1024 + h * 256 + 256],
                                    func=AF.Square,
                                    accum_out=self.nrm[:, s * 4 + h:
                                                       s * 4 + h + 1]))
        ctx.op("scalar", lambda e:
               e.activation(out=self.nrm[:, NS4:2 * NS4],
                            in_=self.nrm[:, 0:NS4], func=AF.Sqrt,
                            bias=self.C("s_eps6", 1)))
        if with_outputs:
            for s in range(self.g.NSUB):
                for h in range(4):
                    ctx.stage("qn_t%d_s%d_h%d" % (t, s, h))
                    for dkc in range(2):
                        fc = 2 * h + dkc
                        self.transpose128(
                            [(self.q_fm[:, fc * T + s * 128:
                                        fc * T + s * 128 + 128],
                              self.scr[:, dkc * 128:dkc * 128 + 128])],
                            evac_engs=("vector",))
                    vmark = self._tp_marks[-1][1]
                    ctx.run("scalar",
                            lambda e, s=s, h=h:
                            e.activation(out=self.scr[:, 256:512],
                                         in_=self.scr[:, 0:256],
                                         func=AF.Square,
                                         accum_out=self.qnrm[:, s * 4 + h:
                                                             s * 4 + h + 1]),
                            waits=[(self.vv, vmark)], inc=self.ss)
            ctx.stage("qn_fin_t%d" % t)
            ctx.op("scalar", lambda e:
                   e.activation(out=self.qnrm[:, NS4:2 * NS4],
                                in_=self.qnrm[:, 0:NS4], func=AF.Sqrt,
                                bias=self.C("s_eps6", 1)))
            ctx.stage("qn_fin2_t%d" % t)
            ctx.op("vector", lambda e:
                   e.reciprocal(self.qnrm[:, NS4:2 * NS4],
                                self.qnrm[:, NS4:2 * NS4]))
        ctx.stage("norm2_t%d" % t)
        def fin2(e):
            ctx.lock(e, "vector",
                     e.reciprocal(self.nrm[:, NS4:2 * NS4],
                                  self.nrm[:, NS4:2 * NS4]))
            e.tensor_mul(self.nrm[:, 2 * NS4:3 * NS4],
                         self.nrm[:, NS4:2 * NS4], self.beta[:, 0:NS4])
        ctx.op("vector", fin2)

    def prep_subtile(self, t, s, with_outputs):
        ctx, g, T = self.ctx, self.g, self.g.TILE
        NS4 = g.NSUB * 4
        ctx.stage("kscale_t%d_s%d" % (t, s))
        for h in range(4):
            def sc_fn(e, s=s, h=h, NS4=NS4):
                ksl = self.k_tm[:, s * 1024 + h * 256: s * 1024 + h * 256 + 256]
                e.tensor_scalar(out=self.kb[:, h * 256:(h + 1) * 256], in0=ksl,
                                scalar1=self.nrm[:, 2 * NS4 + s * 4 + h:
                                                 2 * NS4 + s * 4 + h + 1],
                                scalar2=None, op0=ALU.mult)
                e.tensor_scalar(out=ksl, in0=ksl,
                                scalar1=self.nrm[:, NS4 + s * 4 + h:
                                                 NS4 + s * 4 + h + 1],
                                scalar2=None, op0=ALU.mult)
                vsl = self.v_tm[:, s * 1024 + h * 256: s * 1024 + h * 256 + 256]
                e.tensor_scalar(out=self.vb[:, h * 256:(h + 1) * 256], in0=vsl,
                                scalar1=self.beta[:, s * 4 + h: s * 4 + h + 1],
                                scalar2=None, op0=ALU.mult)
            ctx.op("vector", sc_fn)

        ctx.stage("invTclr_t%d_s%d" % (t, s))
        ctx.op("vector", lambda e: e.memset(self.invT[:], 0.0))
        for h in range(4):
            ctx.stage("prep_t%d_s%d_h%d" % (t, s, h))
            self.prep_sh(t, s, h, with_outputs)

        ctx.stage("u_t%d_s%d" % (t, s))
        um = {}
        for h in range(4):
            um[h] = ctx.run("tensor",
                            lambda e, h=h:
                            e.matmul(self.PB[4 + h][:, 0:256],
                                     self.invT[:, h * 128:h * 128 + 128],
                                     self.vb[:, h * 256:(h + 1) * 256],
                                     start=True, stop=True),
                            inc=self.mm)
        for h in range(4):
            ctx.run("vector",
                    lambda e, h=h, s=s:
                    e.tensor_copy(self.u_sb[:, s * 1024 + h * 256:
                                            s * 1024 + h * 256 + 256],
                                  self.PB[4 + h][:, 0:256]),
                    waits=[(self.mm, um[h])], inc=self.vv)

    def prep_sh(self, t, s, h, with_outputs):
        ctx, g, T = self.ctx, self.g, self.g.TILE
        NS4 = g.NSUB * 4
        C = self.C
        nwt = self.nwt
        A = nwt[:, 0:32]
        AT = nwt[:, 32:64]
        X = nwt[:, 64:96]
        Z = nwt[:, 96:128]
        SC = nwt[:, 128:160]

        for j in range(4):
            for dkc in range(2):
                fc = 2 * h + dkc
                col = fc * T + s * 128 + j * 32
                ctx.run("tensor",
                        lambda e, j=j, col=col, dkc=dkc:
                        e.matmul(self.PB[4 + j][32 * j:32 * j + 32, 0:32],
                                 self.k_fm[:, col:col + 32],
                                 self.k_fm[:, col:col + 32],
                                 start=(dkc == 0), stop=(dkc == 1),
                                 tile_position=(0, 32 * j)),
                        inc=self.mm)
        gmm = self.mm.total

        def a_build(e, s=s, h=h, NS4=NS4):
            for j in range(4):
                e.tensor_scalar(out=A[32 * j:32 * j + 32, :],
                                in0=self.PB[4 + j][32 * j:32 * j + 32, 0:32],
                                scalar1=self.nrm[32 * j:32 * j + 32,
                                                 2 * NS4 + s * 4 + h:
                                                 2 * NS4 + s * 4 + h + 1],
                                scalar2=None, op0=ALU.mult)
            ctx.lock(e, "vector",
                     e.tensor_scalar(out=SC, in0=C("ones"),
                                     scalar1=self.nrm[:, NS4 + s * 4 + h:
                                                      NS4 + s * 4 + h + 1],
                                     scalar2=None, op0=ALU.mult))
            ctx.lock(e, "vector", e.transpose(Z, SC))
            ctx.lock(e, "vector", e.tensor_mul(A, A, Z))
            ctx.lock(e, "vector", e.tensor_mul(A, A, C("trilS")))
            e.transpose(AT, A)
            ctx.lock(e, "vector", e.tensor_sub(X, C("eye1"), A))
            return e.engine_nop()
        ctx.run("vector", a_build, waits=[(self.mm, gmm)], inc=self.vv)

        for it in range(4):
            yv = ctx.run("vector", lambda e: e.transpose(SC, X), inc=self.vv)
            for j in range(4):
                ctx.run("tensor",
                        lambda e, j=j:
                        e.matmul(self.PB[4 + j][32 * j:32 * j + 32, 0:32],
                                 AT[32 * j:32 * j + 32, :],
                                 X[32 * j:32 * j + 32, :],
                                 start=True, stop=True,
                                 tile_position=(32 * j, 32 * j)),
                        waits=([(self.vv, yv)] if j == 0 else []),
                        inc=self.mm)
            pv = self.mm.total

            def z_build(e):
                ctx.lock(e, "vector", e.tensor_sub(Z, C("eye2"), X))
                for j in range(4):
                    e.tensor_sub(Z[32 * j:32 * j + 32, :],
                                 Z[32 * j:32 * j + 32, :],
                                 self.PB[4 + j][32 * j:32 * j + 32, 0:32])
                return e.engine_nop()
            zv = ctx.run("vector", z_build, waits=[(self.mm, pv)], inc=self.vv)

            for j in range(4):
                ctx.run("tensor",
                        lambda e, j=j:
                        e.matmul(self.PB[4 + j][32 * j:32 * j + 32, 0:32],
                                 SC[32 * j:32 * j + 32, :],
                                 Z[32 * j:32 * j + 32, :],
                                 start=True, stop=True,
                                 tile_position=(32 * j, 32 * j)),
                        waits=([(self.vv, zv)] if j == 0 else []),
                        inc=self.mm)
            xv = self.mm.total
            last = (it == 3)

            def x_build(e, last=last, h=h):
                for j in range(4):
                    ins = e.tensor_copy(X[32 * j:32 * j + 32, :],
                                        self.PB[4 + j][32 * j:32 * j + 32,
                                                       0:32])
                ctx.lock(e, "vector", ins)
                if last:
                    ctx.lock(e, "vector", e.transpose(SC, X))
                    for j in range(4):
                        ins = e.tensor_copy(
                            self.invT[32 * j:32 * j + 32,
                                      (h * 4 + j) * 32:(h * 4 + j) * 32 + 32],
                            SC[32 * j:32 * j + 32, :])
                    ctx.lock(e, "vector", ins)
                return e.engine_nop()
            ctx.run("vector", x_build, waits=[(self.mm, xv)], inc=self.vv)

        ctx.stage("prep_w_t%d_s%d_h%d" % (t, s, h))
        wm = {}
        for dkc in range(2):
            wm[dkc] = ctx.run("tensor",
                    lambda e, dkc=dkc, h=h:
                    e.matmul(self.PB[4 + dkc][:, 0:128],
                             self.kb[:, h * 256 + dkc * 128:
                                     h * 256 + dkc * 128 + 128],
                             self.invT[:, h * 128:h * 128 + 128],
                             start=True, stop=True),
                    inc=self.mm)
        for dkc in range(2):
            ctx.run("vector",
                    lambda e, s=s, h=h, dkc=dkc:
                    e.tensor_copy(
                        self.w_sb[:, (s * 4 + h) * 256 + dkc * 128:
                                  (s * 4 + h) * 256 + dkc * 128 + 128],
                        self.PB[4 + dkc][:, 0:128]),
                    waits=[(self.mm, wm[dkc])], inc=self.vv)

        if with_outputs:
            self.prep_attnT(t, s, h)

    def prep_attnT(self, t, s, h):
        ctx, g, T = self.ctx, self.g, self.g.TILE
        NS4 = g.NSUB * 4
        ctx.stage("prep_at_t%d_s%d_h%d" % (t, s, h))
        for j in range(4):
            for dkc in range(2):
                fc = 2 * h + dkc
                col = fc * T + s * 128 + j * 32
                ctx.run("tensor",
                        lambda e, j=j, col=col, dkc=dkc:
                        e.matmul(self.PB[4 + j][32 * j:32 * j + 32, 0:32],
                                 self.k_fm[:, col:col + 32],
                                 self.q_fm[:, col:col + 32],
                                 start=(dkc == 0), stop=(dkc == 1),
                                 tile_position=(0, 32 * j)),
                        inc=self.mm)

        def at_build(e, s=s, h=h, NS4=NS4):
            for j in range(4):
                ins = e.tensor_scalar(
                    out=self.at_sb[32 * j:32 * j + 32,
                                   (s * 4 + h) * 32:(s * 4 + h) * 32 + 32],
                    in0=self.PB[4 + j][32 * j:32 * j + 32, 0:32],
                    scalar1=self.nrm[32 * j:32 * j + 32,
                                     NS4 + s * 4 + h:NS4 + s * 4 + h + 1],
                    scalar2=None, op0=ALU.mult)
            ctx.lock(e, "vector", ins)
            at = self.at_sb[:, (s * 4 + h) * 32:(s * 4 + h) * 32 + 32]
            return e.tensor_mul(at, at, self.C("triuI"))
        ctx.run("vector", at_build, waits=[(self.mm, self.mm.total)],
                inc=self.vv)

    def scan_step(self, t, cj, with_outputs):
        ctx, g, T = self.ctx, self.g, self.g.TILE
        s, jr = cj // 4, cj % 4
        p0 = 32 * jr
        ctx.stage("scan_t%d_c%d" % (t, cj))

        for h in range(4):
            for dkc in range(2):
                ctx.run("tensor",
                        lambda e, h=h, dkc=dkc, s=s, jr=jr, p0=p0:
                        e.matmul(self.PB[4 + h][p0:p0 + 32, 0:256],
                                 self.w_sb[:, (s * 4 + h) * 256 + dkc * 128 +
                                           jr * 32:
                                           (s * 4 + h) * 256 + dkc * 128 +
                                           jr * 32 + 32],
                                 self.S[:, (h * 2 + dkc) * 256:
                                        (h * 2 + dkc) * 256 + 256],
                                 start=(dkc == 0), stop=(dkc == 1),
                                 tile_position=(0, p0)),
                        inc=self.mm)
        pwsv = self.mm.total

        def ua(e, s=s, p0=p0):
            for h in range(4):
                e.tensor_sub(self.uadj[p0:p0 + 32, h * 256:(h + 1) * 256],
                             self.u_sb[p0:p0 + 32,
                                       s * 1024 + h * 256:
                                       s * 1024 + h * 256 + 256],
                             self.PB[4 + h][p0:p0 + 32, 0:256])
            return e.engine_nop()
        uav = ctx.run("vector", ua, waits=[(self.mm, pwsv)], inc=self.vv)

        if with_outputs:
            self.scan_outputs(t, cj, uav)
            supd_w = [(self.vv, self.vv.total)]
        else:
            supd_w = [(self.vv, uav)]

        first = True
        for h in range(4):
            for dkc in range(2):
                ctx.run("tensor",
                        lambda e, h=h, dkc=dkc, s=s:
                        e.matmul(self.PB[h * 2 + dkc][:, 0:256],
                                 self.k_tm[:, s * 1024 + h * 256 + dkc * 128:
                                           s * 1024 + h * 256 + dkc * 128 +
                                           128],
                                 self.uadj[:, h * 256:(h + 1) * 256],
                                 start=True, stop=True),
                        waits=(supd_w if first else []),
                        inc=self.mm)
                first = False
        summ = self.mm.total

        def s_add(e):
            for b in range(8):
                ins = e.tensor_add(self.S[:, b * 256:(b + 1) * 256],
                                   self.S[:, b * 256:(b + 1) * 256],
                                   self.PB[b][:, 0:256])
            return ins
        ctx.run("vector", s_add, waits=[(self.mm, summ)], inc=self.vv)
        ctx.run("gpsimd", lambda e, p0=p0:
                e.memset(self.uadj[p0:p0 + 32, :], 0.0),
                waits=[(self.mm, summ)], inc=self.gg)

    def scan_outputs(self, t, cj, uav):
        ctx, g, T = self.ctx, self.g, self.g.TILE
        s, jr = cj // 4, cj % 4
        p0 = 32 * jr
        NS4 = g.NSUB * 4
        q_fm = self.q_fm
        marks = {}
        for h in range(4):
            for dkc in range(2):
                fc = 2 * h + dkc
                col = fc * T + s * 128 + jr * 32
                ctx.run("tensor",
                        lambda e, h=h, dkc=dkc, col=col, p0=p0:
                        e.matmul(self.PB[4 + h][p0:p0 + 32, 0:256],
                                 q_fm[:, col:col + 32],
                                 self.S[:, (h * 2 + dkc) * 256:
                                        (h * 2 + dkc) * 256 + 256],
                                 start=(dkc == 0), stop=False,
                                 tile_position=(0, p0)),
                        waits=([(self.vv, uav)] if h == 0 and dkc == 0
                               else []),
                        inc=self.mm)
            marks[h] = ctx.run("tensor",
                    lambda e, h=h, s=s, p0=p0:
                    e.matmul(self.PB[4 + h][p0:p0 + 32, 0:256],
                             self.at_sb[:, (s * 4 + h) * 32:
                                        (s * 4 + h) * 32 + 32],
                             self.uadj[:, h * 256:(h + 1) * 256],
                             start=False, stop=True,
                             tile_position=(0, p0)),
                    inc=self.mm)
        for h in range(4):
            ctx.run("vector",
                    lambda e, h=h, s=s, p0=p0, NS4=NS4:
                    e.tensor_scalar(
                        out=self.delta_tm[p0:p0 + 32,
                                          s * 1024 + h * 256:
                                          s * 1024 + h * 256 + 256],
                        in0=self.PB[4 + h][p0:p0 + 32, 0:256],
                        scalar1=self.qnrm[p0:p0 + 32,
                                          NS4 + s * 4 + h:
                                          NS4 + s * 4 + h + 1],
                        scalar2=None, op0=ALU.mult),
                    waits=[(self.mm, marks[h])], inc=self.vv)

    def dump_dbg(self, t):
        ctx, g = self.ctx, self.g
        if not self.dbg:
            return
        ctx.stage("dbg_t%d" % t)
        for name in sorted(self.dbg):
            buf = self.ctx.t.get(name)
            d = ctx.dram.get("dbg_" + name)
            if buf is None or d is None:
                continue
            w = list(d.shape)[2]
            ctx.dma("sync", self.dmo, d.ap()[t, :, :], buf[:, 0:w])
        ctx.wait("sync", self.dmo)



def _l0_init_post(self):
    ctx = self.ctx
    ctx.dma("sync", self.dma, self.fir_w[:], ctx.dram["fir_img"].ap())
    ctx.dma("sync", self.dma, self.g2_sb[:], ctx.dram["g2_img"].ap())
    ctx.dma("sync", self.dma, self.g1b_sb[:], ctx.dram["g1b_img"].ap())
    ctx.dma("sync", self.dma, self.g2b_sb[0:20, 0:2], ctx.dram["g2b_img"].ap())
    ctx.dma("sync", self.dma, self.ltexp_sb[0:20, 0:1], ctx.dram["lt_img"].ap())
    ctx.dma("sync", self.dma, self.fsc_sb[:, 0:4], ctx.dram["flr_img"].ap())
    ctx.dma("sync", self.dma, self.onorm_sb[:], ctx.dram["onorm_img"].ap())
    ctx.wait("sync", self.dma)
    ctx.stage("init_post")
    ctx.op("scalar", lambda e: e.activation(
        out=self.ltexp_sb[0:20, 1:2], in_=self.ltexp_sb[0:20, 0:1],
        func=AF.Exp, scale=-1.0))
    ctx.op("scalar", lambda e: e.activation(
        out=self.fsc_sb[:, 4:8], in_=self.fsc_sb[:, 0:4], func=AF.Sigmoid))
    ctx.stage("init_post2")
    def fsc2(e):
        e.tensor_scalar(out=self.fsc_sb[:, 8:12], in0=self.fsc_sb[:, 4:8],
                        scalar1=-4.0, scalar2=1.0, op0=ALU.mult, op1=ALU.add)
        ctx.lock(e, "vector",
                 e.tensor_add(self.g2b_sb[0:20, 2:3], self.g2b_sb[0:20, 0:1],
                              self.g2b_sb[0:20, 1:2]))
        return e.tensor_scalar(out=self.g2b_sb[0:20, 2:3],
                               in0=self.g2b_sb[0:20, 2:3],
                               scalar1=self.ltexp_sb[0:20, 1:2],
                               scalar2=None, op0=ALU.mult)
    ctx.op("vector", fsc2)
    ctx.op("vector", lambda e: e.memset(self.gstat_fm[:], 0.0))
    ctx.op("vector", lambda e: e.memset(self.stat_tm[:, 60:64], 0.0))


def _l0_fir(self, t):
    ctx, T, VS = self.ctx, self.g.TILE, self.VS
    ctx.stage("fir_t%d" % t)
    for K, woff, out_fm, eng in ((3, 0, self.short_fm, "vector"),
                                 (31, 3, self.long_fm, "vector"),
                                 (64, 34, self.wide_fm, "vector")):
        for fc in range(8):
            for j in range(K):
                def c_fn(e, fc=fc, j=j, K=K, woff=woff, out_fm=out_fm):
                    src = self.v_fm[:, fc * VS + 64 - K + j:
                                    fc * VS + 64 - K + j + T]
                    dst = out_fm[:, fc * T: fc * T + T]
                    wj = self.fir_w[:, fc * 98 + woff + j:
                                    fc * 98 + woff + j + 1]
                    if j == 0:
                        e.tensor_scalar(out=dst, in0=src, scalar1=wj,
                                        scalar2=None, op0=ALU.mult)
                    else:
                        e.scalar_tensor_tensor(out=dst, in0=src, scalar=wj,
                                               in1=dst, op0=ALU.mult,
                                               op1=ALU.add)
                ctx.op(eng, c_fn)


def _l0_stats_gate_sub(self, t, s):
    ctx, g, T = self.ctx, self.g, self.g.TILE
    ctx.stage("slwtp_t%d_s%d" % (t, s))
    pairs = []
    for b, buf in enumerate((self.short_fm, self.long_fm, self.wide_fm)):
        for fc in range(8):
            pairs.append((buf[:, fc * T + s * 128:fc * T + s * 128 + 128],
                          self.slw_tm[:, b * 1024 + fc * 128:
                                      b * 1024 + (fc + 1) * 128]))
    self.transpose128(pairs)

    ctx.stage("stats_t%d_s%d" % (t, s))
    branches = [self.slw_tm[:, 0:1024], self.slw_tm[:, 1024:2048],
                self.slw_tm[:, 2048:3072],
                self.delta_tm[:, s * 1024:(s + 1) * 1024],
                self.v_tm[:, s * 1024:(s + 1) * 1024]]
    for b, buf in enumerate(branches):
        for h in range(4):
            sl = buf[:, h * 256:(h + 1) * 256]
            ctx.op("scalar", lambda e, sl=sl, b=b, h=h:
                   e.activation(out=self.scr[:, 0:256], in_=sl, func=AF.Copy,
                                accum_out=self.stat_tm[:, b * 12 + h:
                                                       b * 12 + h + 1]))
            ctx.op("scalar", lambda e, sl=sl, b=b, h=h:
                   e.activation(out=self.scr[:, 256:512], in_=sl,
                                func=AF.Square,
                                accum_out=self.stat_tm[:, b * 12 + 4 + h:
                                                       b * 12 + 4 + h + 1]))
            ctx.op("vector", lambda e, sl=sl, b=b, h=h:
                   e.reduce_max(self.stat_tm[:, b * 12 + 8 + h:
                                             b * 12 + 8 + h + 1],
                                sl, axis=AX.X, apply_absolute_value=True))
    ctx.stage("statsf_t%d_s%d" % (t, s))
    sv = self.stat_tm[:, 0:60].rearrange("p (b k) -> p b k", b=5)

    def fin(e):
        e.tensor_scalar(out=sv[:, :, 0:4], in0=sv[:, :, 0:4],
                        scalar1=self.C("s_inv256", 1), scalar2=None,
                        op0=ALU.mult)
        ctx.lock(e, "vector",
                 e.tensor_scalar(out=sv[:, :, 4:8], in0=sv[:, :, 4:8],
                                 scalar1=self.C("s_inv256", 1), scalar2=None,
                                 op0=ALU.mult))
        return e.tensor_scalar(out=sv[:, :, 4:8], in0=sv[:, :, 4:8],
                               scalar1=self.C("s_eps8", 1), scalar2=None,
                               op0=ALU.max)
    ctx.op("vector", fin)
    ctx.stage("statsf2_t%d_s%d" % (t, s))
    ctx.op("scalar", lambda e:
           e.activation(out=sv[:, :, 4:8], in_=sv[:, :, 4:8], func=AF.Sqrt))
    ctx.stage("gstat_t%d_s%d" % (t, s))
    mmv = ctx.run("tensor",
                  lambda e, s=s:
                  e.transpose(self.P47[0:64, 0:128], self.stat_tm[:, 0:64],
                              self.C("ident", 128)),
                  inc=self.mm)
    ctx.run("vector",
            lambda e, s=s:
            e.tensor_copy(self.gstat_fm[0:64, s * 128:s * 128 + 128],
                          self.P47[0:64, 0:128]),
            waits=[(self.mm, mmv)], inc=self.vv)


def _l0_gate(self, t):
    ctx, g, T = self.ctx, self.g, self.g.TILE
    NSUB = g.NSUB
    for oc in range(16):
        ctx.stage("g1w_t%d_o%d" % (t, oc))
        slab = self.wsl[:, (oc % 2) * 1152:(oc % 2) * 1152 + 1152]
        ctx.dma("sync", self.dma, slab,
                ctx.dram["g1_img"].ap()[oc, :, :])
        ctx.wait("sync", self.dma)
        ctx.stage("g1_t%d_o%d" % (t, oc))
        for s in range(NSUB):
            for kc in range(9):
                rhs = (self.x_fm[:, kc * T + s * 128:kc * T + s * 128 + 128]
                       if kc < 8 else
                       self.gstat_fm[:, s * 128:s * 128 + 128])
                ctx.run("tensor",
                        lambda e, kc=kc, s=s, slab=slab, rhs=rhs:
                        e.matmul(self.P03[:, (s % 2) * 512:(s % 2) * 512 + 128],
                                 slab[:, kc * 128:(kc + 1) * 128],
                                 rhs, start=(kc == 0), stop=(kc == 8)),
                        inc=self.mm)
            ctx.run("scalar",
                    lambda e, oc=oc, s=s:
                    e.activation(out=self.hdn_fm[:, s * 2048 + oc * 128:
                                                 s * 2048 + oc * 128 + 128],
                                 in_=self.P03[:, (s % 2) * 512:
                                              (s % 2) * 512 + 128],
                                 func=AF.Gelu,
                                 bias=self.g1b_sb[:, oc:oc + 1]),
                    waits=[(self.mm, self.mm.total)], inc=self.ss)
    for s in range(NSUB):
        ctx.stage("g2_t%d_s%d" % (t, s))
        for kc in range(16):
            ctx.run("tensor",
                    lambda e, kc=kc, s=s:
                    e.matmul(self.P47[0:20, 256:384],
                             self.g2_sb[:, kc * 20:(kc + 1) * 20],
                             self.hdn_fm[:, s * 2048 + kc * 128:
                                         s * 2048 + (kc + 1) * 128],
                             start=(kc == 0), stop=(kc == 15)),
                    inc=self.mm)
        ctx.run("scalar",
                lambda e:
                e.activation(out=self.scr[0:20, 0:128],
                             in_=self.P47[0:20, 256:384], func=AF.Identity,
                             scale=self.ltexp_sb[0:20, 1:2],
                             bias=self.g2b_sb[0:20, 2:3]),
                waits=[(self.mm, self.mm.total)], inc=self.ss)
        ctx.stage("g2tp_t%d_s%d" % (t, s))
        mmv = ctx.run("tensor",
                      lambda e:
                      e.transpose(self.P47[0:128, 512:532],
                                  self.scr[0:20, 0:128],
                                  self.C("ident", 128)[0:20, 0:20]),
                      inc=self.mm)
        ctx.run("vector",
                lambda e:
                e.tensor_copy(self.wsm[:, 0:20], self.P47[0:128, 512:532]),
                waits=[(self.mm, mmv)], inc=self.vv)
        ctx.stage("sm_t%d_s%d" % (t, s))
        for h in range(4):
            def sm1(e, h=h):
                return e.reduce_max(self.scr[:, 0:1],
                                    self.wsm[:, 5 * h:5 * h + 5],
                                    axis=AX.X, negate=True)
            ctx.run("vector", sm1, inc=self.vv)
            ctx.run("scalar",
                    lambda e, h=h:
                    e.activation(out=self.wsm[:, 5 * h:5 * h + 5],
                                 in_=self.wsm[:, 5 * h:5 * h + 5],
                                 func=AF.Exp, bias=self.scr[:, 0:1],
                                 accum_out=self.scr[:, 1:2]),
                    waits=[(self.vv, self.vv.total)], inc=self.ss)
            def sm3(e, h=h):
                ctx.lock(e, "vector",
                         e.reciprocal(self.scr[:, 2:3], self.scr[:, 1:2]))
                ctx.lock(e, "vector",
                         e.tensor_scalar(out=self.wsm[:, 5 * h:5 * h + 5],
                                         in0=self.wsm[:, 5 * h:5 * h + 5],
                                         scalar1=self.scr[:, 2:3],
                                         scalar2=None, op0=ALU.mult))
                ctx.lock(e, "vector",
                         e.tensor_scalar(out=self.wsm[:, 5 * h:5 * h + 5],
                                         in0=self.wsm[:, 5 * h:5 * h + 5],
                                         scalar1=self.fsc_sb[:, 8 + h:9 + h],
                                         scalar2=None, op0=ALU.mult))
                return e.tensor_scalar(out=self.wsm[:, 5 * h:5 * h + 4],
                                       in0=self.wsm[:, 5 * h:5 * h + 4],
                                       scalar1=self.fsc_sb[:, 4 + h:5 + h],
                                       scalar2=None, op0=ALU.add)
            ctx.run("vector", sm3, waits=[(self.ss, self.ss.total)],
                    inc=self.vv)
        self._l0_mix(t, s)


def _l0_mix(self, t, s):
    ctx, g = self.ctx, self.g
    ctx.stage("mix_t%d_s%d" % (t, s))
    branches = [self.slw_tm[:, 0:1024], self.slw_tm[:, 1024:2048],
                self.slw_tm[:, 2048:3072],
                self.delta_tm[:, s * 1024:(s + 1) * 1024],
                self.v_tm[:, s * 1024:(s + 1) * 1024]]
    for h in range(4):
        for b in range(5):
            def mx(e, h=h, b=b):
                src = branches[b][:, h * 256:(h + 1) * 256]
                dst = self.slw_tm[:, h * 256:(h + 1) * 256]
                wcol = self.wsm[:, 5 * h + b:5 * h + b + 1]
                if b == 0:
                    return e.tensor_scalar(out=dst, in0=src, scalar1=wcol,
                                           scalar2=None, op0=ALU.mult)
                return e.scalar_tensor_tensor(out=dst, in0=src, scalar=wcol,
                                              in1=dst, op0=ALU.mult,
                                              op1=ALU.add)
            ctx.op("vector", mx)
    ctx.stage("rms_t%d_s%d" % (t, s))
    for h in range(4):
        ctx.op("scalar", lambda e, h=h:
               e.activation(out=self.scr[:, 0:256],
                            in_=self.slw_tm[:, h * 256:(h + 1) * 256],
                            func=AF.Square,
                            accum_out=self.scr[:, 256 + h:257 + h]))
    ctx.op("scalar", lambda e:
           e.activation(out=self.scr[:, 260:264], in_=self.scr[:, 256:260],
                        func=AF.Sqrt, scale=1.0 / 256,
                        bias=self.C("s_eps5", 1)))
    ctx.stage("rms2_t%d_s%d" % (t, s))
    def rmsf(e):
        return e.reciprocal(self.scr[:, 260:264], self.scr[:, 260:264])
    ctx.op("vector", rmsf)
    ctx.stage("rms3_t%d_s%d" % (t, s))
    for h in range(4):
        ctx.op("vector", lambda e, h=h:
               e.scalar_tensor_tensor(
                   out=self.slw_tm[:, h * 256:(h + 1) * 256],
                   in0=self.slw_tm[:, h * 256:(h + 1) * 256],
                   scalar=self.scr[:, 260 + h:261 + h],
                   in1=self.onorm_sb[:, h * 256:(h + 1) * 256],
                   op0=ALU.mult, op1=ALU.mult))
    ctx.stage("otp_t%d_s%d" % (t, s))
    T = g.TILE
    self.transpose128(
        [(self.slw_tm[:, fc * 128:(fc + 1) * 128],
          self.q_fm[:, fc * T + s * 128:fc * T + s * 128 + 128])
         for fc in range(8)])


def _l0_out(self, t, x1_name="x1"):
    ctx, g, T = self.ctx, self.g, self.g.TILE
    for och in range(2):
        ctx.stage("wo_t%d_o%d" % (t, och))
        ctx.dma("sync", self.dma,
                self.wbuf[:, och * 4096:och * 4096 + 4096]
                .rearrange("p (kc n) -> p kc n", kc=8),
                ctx.dram["wo_img"].ap()[:, och * 512:och * 512 + 512]
                .rearrange("(kc p) n -> p kc n", p=128))
        ctx.wait("sync", self.dma)
        ctx.stage("wo_mm_t%d_o%d" % (t, och))
        for s in range(g.NSUB):
            for kc in range(8):
                ctx.run("tensor",
                        lambda e, s=s, kc=kc, och=och:
                        e.matmul(self.P03[:, (s % 2) * 512:(s % 2) * 512 + 512],
                                 self.q_fm[:, kc * T + s * 128:
                                           kc * T + s * 128 + 128],
                                 self.wbuf[:, och * 4096 + kc * 512:
                                           och * 4096 + (kc + 1) * 512],
                                 start=(kc == 0), stop=(kc == 7)),
                        inc=self.mm)
            ctx.run("vector",
                    lambda e, s=s, och=och:
                    e.tensor_add(self.x1_tm[:, s * 1024 + och * 512:
                                            s * 1024 + och * 512 + 512],
                                 self.P03[:, (s % 2) * 512:(s % 2) * 512 + 512],
                                 self.x_tm[:, s * 1024 + och * 512:
                                           s * 1024 + och * 512 + 512]),
                    waits=[(self.mm, self.mm.total)], inc=self.vv)
    for s in range(g.NSUB):
        ctx.stage("ln_t%d_s%d" % (t, s))
        ysl = self.x1_tm[:, s * 1024:(s + 1) * 1024]
        ctx.op("scalar", lambda e, ysl=ysl:
               e.activation(out=self.kb[:, 0:1024], in_=ysl, func=AF.Copy,
                            accum_out=self.scr[:, 270:271]))
        ctx.stage("ln2_t%d_s%d" % (t, s))
        ctx.op("vector", lambda e:
               e.tensor_scalar(out=self.scr[:, 271:272],
                               in0=self.scr[:, 270:271],
                               scalar1=self.C("s_inv1024", 1), scalar2=None,
                               op0=ALU.mult))
        ctx.stage("ln3_t%d_s%d" % (t, s))
        ctx.op("vector", lambda e, ysl=ysl:
               e.tensor_scalar(out=ysl, in0=ysl,
                               scalar1=self.scr[:, 271:272], scalar2=None,
                               op0=ALU.subtract))
        ctx.stage("ln4_t%d_s%d" % (t, s))
        ctx.op("scalar", lambda e, ysl=ysl:
               e.activation(out=self.kb[:, 0:1024], in_=ysl, func=AF.Square,
                            accum_out=self.scr[:, 272:273]))
        ctx.op("scalar", lambda e:
               e.activation(out=self.scr[:, 273:274], in_=self.scr[:, 272:273],
                            func=AF.Sqrt, scale=1.0 / 1024,
                            bias=self.C("s_eps5", 1)))
        ctx.stage("ln5_t%d_s%d" % (t, s))
        def lnf(e, ysl=ysl):
            ctx.lock(e, "vector",
                     e.reciprocal(self.scr[:, 273:274], self.scr[:, 273:274]))
            return e.tensor_scalar(out=ysl, in0=ysl,
                                   scalar1=self.scr[:, 273:274], scalar2=None,
                                   op0=ALU.mult)
        ctx.op("vector", lnf)
    ctx.stage("x1out_t%d" % t)
    for s in range(g.NSUB):
        ctx.dma("sync", self.dmo,
                ctx.dram[x1_name].ap()[t * T + s * 128:t * T + s * 128 + 128, :],
                self.x1_tm[:, s * 1024:(s + 1) * 1024])
    ctx.wait("sync", self.dmo)


def _l0_post(self, t):
    self._l0_fir(t)
    for s in range(self.g.NSUB):
        self._l0_stats_gate_sub(t, s)
    self._l0_gate(t)
    self._l0_out(t)


L0._l0_init_post = _l0_init_post
L0._l0_fir = _l0_fir
L0._l0_stats_gate_sub = _l0_stats_gate_sub
L0._l0_gate = _l0_gate
L0._l0_mix = _l0_mix
L0._l0_out = _l0_out
L0._l0_post = _l0_post



class FFN0:

    def __init__(self, ctx, coff, consts_w, reuse=None):
        self.ctx = ctx
        self.g = ctx.g
        self.ST = min(1024, self.g.TCORE)
        self.coff = coff
        r = reuse or {}
        def get(name, shape):
            if name in ctx.t:
                h = ctx.t[name]
                assert list(h.shape)[1] >= shape[1], (name, h.shape, shape)
                return h
            return ctx.sbuf(name, shape)
        self.consts = get("consts_sb", [128, consts_w])
        self.x1_fm = get("f_x1fm", [128, 8 * self.ST])
        self.slab = get("wbuf", [128, 8 * 1024])
        self.gh = get("f_gh", [128, 8 * self.ST])
        self.out = get("f_out", [128, (self.ST // 128) * 1024])
        self.gate = get("f_gate", [128, 8 * 8])
        self.cwb = get("f_cwb", [128, (self.ST // 128) * 8])
        self.prb = get("f_prb", [128, (self.ST // 128) * 8])
        self.xt = get("f_xt", [128, 1024])
        self.scr = get("scr", [128, 512])
        self.P03 = ctx.t["P03"]
        self.P47 = ctx.t["P47"]
        self.PB = self.P03.banks
        c = ctx.counter
        self.dma = c("c_dma", dma=True)
        self.dmo = c("c_dmo", dma=True)
        self.mm = c("c_mm")
        self.vv = c("c_vv")
        self.ss = c("c_ss")
        self._tpm = []

    def C(self, name, w=32):
        o = self.coff[name]
        return self.consts[:, o:o + w]

    def transpose_in(self, pairs):
        ctx = self.ctx
        ident = self.C("ident", 128)
        for (src, dst) in pairs:
            idx = len(self._tpm)
            bo = (idx % 4) * 512
            waits = [self._tpm[idx - 4]] if idx >= 4 else []
            mmv = ctx.run("tensor",
                          lambda e, src=src, bo=bo:
                          e.transpose(self.P47[:, bo:bo + 128], src, ident),
                          waits=waits, inc=self.mm)
            mark = ctx.run("vector",
                           lambda e, dst=dst, bo=bo:
                           e.tensor_copy(dst, self.P47[:, bo:bo + 128]),
                           waits=[(self.mm, mmv)], inc=self.vv)
            self._tpm.append((self.vv, mark))

    def build(self, x1_name="x1", x2_name="x2"):
        ctx, g = self.ctx, self.g
        ST = self.ST
        NSU = ST // 128
        NST = g.TCORE // ST
        ctx.stage("f_init")
        ctx.dma("sync", self.dma, self.gate[:], ctx.dram["gate8_img"].ap())
        ctx.wait("sync", self.dma)
        for st in range(NST):
            for s in range(NSU):
                ctx.stage("f_x_%d_%d" % (st, s))
                ctx.dma("sync", self.dma, self.xt[:],
                        ctx.dram[x1_name].ap()[st * ST + s * 128:
                                               st * ST + s * 128 + 128, :])
                ctx.wait("sync", self.dma)
                ctx.stage("f_xtp_%d_%d" % (st, s))
                self.transpose_in(
                    [(self.xt[:, fc * 128:(fc + 1) * 128],
                      self.x1_fm[:, fc * ST + s * 128:fc * ST + s * 128 + 128])
                     for fc in range(8)])
            ctx.stage("f_rt_%d" % st)
            rt_marks = []
            for s in range(NSU):
                for kc in range(8):
                    w = []
                    if kc == 0 and s >= 2:
                        w = [(self.vv, rt_marks[s - 2])]
                    ctx.run("tensor",
                            lambda e, s=s, kc=kc:
                            e.matmul(self.P03[:, (s % 2) * 512:(s % 2) * 512 + 8],
                                     self.x1_fm[:, kc * ST + s * 128:
                                                kc * ST + s * 128 + 128],
                                     self.gate[:, kc * 8:(kc + 1) * 8],
                                     start=(kc == 0), stop=(kc == 7)),
                            waits=w, inc=self.mm)
                rt_marks.append(ctx.run("vector",
                        lambda e, s=s:
                        e.tensor_copy(self.prb[:, s * 8:(s + 1) * 8],
                                      self.P03[:, (s % 2) * 512:
                                               (s % 2) * 512 + 8]),
                        waits=[(self.mm, self.mm.total)], inc=self.vv))
            ctx.stage("f_sm_%d" % st)
            for s in range(NSU):
                p = self.prb[:, s * 8:(s + 1) * 8]
                ctx.run("vector",
                        lambda e, p=p: e.reduce_max(self.scr[:, 0:1], p,
                                                    axis=AX.X, negate=True),
                        inc=self.vv)
                ctx.run("scalar",
                        lambda e, p=p:
                        e.activation(out=p, in_=p, func=AF.Exp,
                                     bias=self.scr[:, 0:1],
                                     accum_out=self.scr[:, 1:2]),
                        waits=[(self.vv, self.vv.total)], inc=self.ss)
                def smf(e, p=p, s=s):
                    ctx.lock(e, "vector",
                             e.reciprocal(self.scr[:, 2:3], self.scr[:, 1:2]))
                    ctx.lock(e, "vector",
                             e.tensor_scalar(out=p, in0=p,
                                             scalar1=self.scr[:, 2:3],
                                             scalar2=None, op0=ALU.mult))
                    cw = self.cwb[:, s * 8:(s + 1) * 8]
                    for ee in range(8):
                        if ee == 0:
                            ins = e.tensor_scalar(out=cw, in0=p,
                                                  scalar1=p[:, 0:1],
                                                  scalar2=None, op0=ALU.is_lt)
                        else:
                            ins = e.scalar_tensor_tensor(out=cw, in0=p,
                                                         scalar=p[:, ee:ee + 1],
                                                         in1=cw, op0=ALU.is_lt,
                                                         op1=ALU.add)
                        ctx.lock(e, "vector", ins)
                    ctx.lock(e, "vector",
                             e.tensor_scalar(out=cw, in0=cw, scalar1=2.0,
                                             scalar2=None, op0=ALU.is_lt))
                    return e.tensor_mul(cw, cw, p)
                ctx.run("vector", smf, waits=[(self.ss, self.ss.total)],
                        inc=self.vv)
            ctx.stage("f_clr_%d" % st)
            ctx.op("vector", lambda e: e.memset(self.out[:], 0.0))

            for ex in range(8):
                for half in range(2):
                    self.expert_half(st, ex, half)

            for s in range(NSU):
                ctx.stage("f_res_%d_%d" % (st, s))
                ctx.dma("sync", self.dma, self.xt[:],
                        ctx.dram[x1_name].ap()[st * ST + s * 128:
                                               st * ST + s * 128 + 128, :])
                ctx.wait("sync", self.dma)
                ctx.stage("f_res2_%d_%d" % (st, s))
                ysl = self.out[:, s * 1024:(s + 1) * 1024]
                ctx.op("vector", lambda e, ysl=ysl:
                       e.tensor_add(ysl, ysl, self.xt[:, 0:1024]))
                ctx.stage("f_ln_%d_%d" % (st, s))
                ctx.op("scalar", lambda e, ysl=ysl:
                       e.activation(out=self.xt[:, 0:1024], in_=ysl,
                                    func=AF.Copy,
                                    accum_out=self.scr[:, 4:5]))
                ctx.stage("f_ln2_%d_%d" % (st, s))
                ctx.op("vector", lambda e, ysl=ysl:
                       e.tensor_scalar(out=self.scr[:, 5:6],
                                       in0=self.scr[:, 4:5],
                                       scalar1=self.C("s_inv1024", 1),
                                       scalar2=None, op0=ALU.mult))
                ctx.stage("f_ln3_%d_%d" % (st, s))
                ctx.op("vector", lambda e, ysl=ysl:
                       e.tensor_scalar(out=ysl, in0=ysl,
                                       scalar1=self.scr[:, 5:6], scalar2=None,
                                       op0=ALU.subtract))
                ctx.stage("f_ln4_%d_%d" % (st, s))
                ctx.op("scalar", lambda e, ysl=ysl:
                       e.activation(out=self.xt[:, 0:1024], in_=ysl,
                                    func=AF.Square,
                                    accum_out=self.scr[:, 6:7]))
                ctx.op("scalar", lambda e:
                       e.activation(out=self.scr[:, 7:8], in_=self.scr[:, 6:7],
                                    func=AF.Sqrt, scale=1.0 / 1024,
                                    bias=self.C("s_eps5", 1)))
                ctx.stage("f_ln5_%d_%d" % (st, s))
                def lnf(e, ysl=ysl):
                    ctx.lock(e, "vector",
                             e.reciprocal(self.scr[:, 7:8], self.scr[:, 7:8]))
                    return e.tensor_scalar(out=ysl, in0=ysl,
                                           scalar1=self.scr[:, 7:8],
                                           scalar2=None, op0=ALU.mult)
                ctx.op("vector", lnf)
            ctx.stage("f_out_%d" % st)
            for s in range(NSU):
                ctx.dma("sync", self.dmo,
                        ctx.dram[x2_name].ap()[st * ST + s * 128:
                                               st * ST + s * 128 + 128, :],
                        self.out[:, s * 1024:(s + 1) * 1024])
            ctx.wait("sync", self.dmo)

    def expert_half(self, st, ex, half):
        ctx, g = self.ctx, self.g
        ST = self.ST
        NSU = ST // 128
        ctx.stage("f_wg_%d_%d_%d" % (st, ex, half))
        ctx.dma("sync", self.dma, self.slab[:],
                ctx.dram["wg_img"].ap()[ex, half, :, :])
        ctx.wait("sync", self.dma)
        ctx.stage("f_g_%d_%d_%d" % (st, ex, half))
        g_marks = {}
        for oc8 in range(8):
            for nh in range(ST // 512):
                for kc in range(8):
                    w = []
                    if kc == 0 and oc8 > 0:
                        w = [(self.ss, g_marks[(oc8 - 1, nh)])]
                    ctx.run("tensor",
                            lambda e, oc8=oc8, nh=nh, kc=kc:
                            e.matmul(self.P03[:, (nh % 2) * 512:
                                              (nh % 2) * 512 + 512],
                                     self.slab[:, kc * 1024 + oc8 * 128:
                                               kc * 1024 + oc8 * 128 + 128],
                                     self.x1_fm[:, kc * ST + nh * 512:
                                                kc * ST + nh * 512 + 512],
                                     start=(kc == 0), stop=(kc == 7)),
                            waits=w, inc=self.mm)
                g_marks[(oc8, nh)] = ctx.run("scalar",
                        lambda e, oc8=oc8, nh=nh:
                        e.activation(out=self.gh[:, oc8 * ST + nh * 512:
                                                 oc8 * ST + nh * 512 + 512],
                                     in_=self.P03[:, (nh % 2) * 512:
                                                  (nh % 2) * 512 + 512],
                                     func=AF.Silu),
                        waits=[(self.mm, self.mm.total)], inc=self.ss)
        ctx.stage("f_wu_%d_%d_%d" % (st, ex, half))
        ctx.dma("sync", self.dma, self.slab[:],
                ctx.dram["wu_img"].ap()[ex, half, :, :])
        ctx.wait("sync", self.dma)
        ctx.stage("f_u_%d_%d_%d" % (st, ex, half))
        u_marks = {}
        for oc8 in range(8):
            for nh in range(ST // 512):
                for kc in range(8):
                    w = []
                    if kc == 0 and oc8 > 0:
                        w = [(self.vv, u_marks[(oc8 - 1, nh)])]
                    ctx.run("tensor",
                            lambda e, oc8=oc8, nh=nh, kc=kc:
                            e.matmul(self.P03[:, (nh % 2) * 512:
                                              (nh % 2) * 512 + 512],
                                     self.slab[:, kc * 1024 + oc8 * 128:
                                               kc * 1024 + oc8 * 128 + 128],
                                     self.x1_fm[:, kc * ST + nh * 512:
                                                kc * ST + nh * 512 + 512],
                                     start=(kc == 0), stop=(kc == 7)),
                            waits=w, inc=self.mm)
                u_marks[(oc8, nh)] = ctx.run("vector",
                        lambda e, oc8=oc8, nh=nh:
                        e.tensor_mul(self.gh[:, oc8 * ST + nh * 512:
                                             oc8 * ST + nh * 512 + 512],
                                     self.gh[:, oc8 * ST + nh * 512:
                                             oc8 * ST + nh * 512 + 512],
                                     self.P03[:, (nh % 2) * 512:
                                              (nh % 2) * 512 + 512]),
                        waits=[(self.mm, self.mm.total)], inc=self.vv)
        ctx.stage("f_wd_%d_%d_%d" % (st, ex, half))
        ctx.dma("sync", self.dma, self.slab[:],
                ctx.dram["wd_img"].ap()[ex, half, :, :])
        ctx.wait("sync", self.dma)
        for s in range(NSU):
            ctx.stage("f_d_%d_%d_%d_%d" % (st, ex, half, s))
            for och in range(2):
                for oc8 in range(8):
                    ctx.run("tensor",
                            lambda e, s=s, och=och, oc8=oc8:
                            e.matmul(self.P47[:, 1024 + och * 512:
                                              1024 + och * 512 + 512],
                                     self.gh[:, oc8 * ST + s * 128:
                                             oc8 * ST + s * 128 + 128],
                                     self.slab[:, oc8 * 1024 + och * 512:
                                               oc8 * 1024 + och * 512 + 512],
                                     start=(oc8 == 0), stop=(oc8 == 7)),
                            inc=self.mm)
                ctx.run("vector",
                        lambda e, s=s, och=och, ex=ex:
                        e.scalar_tensor_tensor(
                            out=self.out[:, s * 1024 + och * 512:
                                         s * 1024 + och * 512 + 512],
                            in0=self.P47[:, 1024 + och * 512:
                                         1024 + och * 512 + 512],
                            scalar=self.cwb[:, s * 8 + ex:s * 8 + ex + 1],
                            in1=self.out[:, s * 1024 + och * 512:
                                         s * 1024 + och * 512 + 512],
                            op0=ALU.mult, op1=ALU.add),
                        waits=[(self.mm, self.mm.total)], inc=self.vv)



class L1A:

    def __init__(self, ctx, coff, consts_w):
        self.ctx = ctx
        self.g = ctx.g
        self.coff = coff
        self.T1 = 512
        T = self.T1
        def get(name, shape):
            return ctx.t[name] if name in ctx.t else ctx.sbuf(name, shape)
        self.consts = get("consts_sb", [128, consts_w])
        self.x_fm = get("a_xfm", [128, 8 * T])
        self.wbuf = get("wbuf", [128, 8 * 1024])
        self.xt = get("a_xt", [128, 1024])
        self.qkv_tm = get("a_qkv", [128, (T // 128) * 3 * 1024])
        self.kp_tm = get("a_kp", [128, 1024])
        self.bias = get("a_bias", [128, 3 * 1024])
        self.R_sb = get("a_R", [128, 64])
        self.sq_tm = get("a_sq", [128, 2 * 16])
        self.zrec = get("a_zrec", [128, 16])
        self.kv_sb = get("a_kvs", [64, 1024])
        self.scr = get("scr", [128, 512])
        self.P03 = ctx.t["P03"]
        self.P47 = ctx.t["P47"]
        self.PB = self.P03.banks
        c = ctx.counter
        self.dma = c("c_dma", dma=True)
        self.dmo = c("c_dmo", dma=True)
        self.mm = c("c_mm")
        self.vv = c("c_vv")
        self.ss = c("c_ss")
        self._tpm = []

    def C(self, name, w=32):
        o = self.coff[name]
        return self.consts[:, o:o + w]

    def transpose_in(self, pairs, banks=(0, 1)):
        ctx = self.ctx
        ident = self.C("ident", 128)
        nb = len(banks)
        for (src, dst) in pairs:
            idx = len(self._tpm)
            bo = banks[idx % nb] * 512
            waits = [self._tpm[idx - nb]] if idx >= nb else []
            mmv = ctx.run("tensor",
                          lambda e, src=src, bo=bo:
                          e.transpose(self.P47[:, bo:bo + 128], src, ident),
                          waits=waits, inc=self.mm)
            mark = ctx.run("vector",
                           lambda e, dst=dst, bo=bo:
                           e.tensor_copy(dst, self.P47[:, bo:bo + 128]),
                           waits=[(self.mm, mmv)], inc=self.vv)
            self._tpm.append((self.vv, mark))

    def build(self, x2_name="x2", q1_name="q1", kvp_name="kv_part"):
        ctx, g = self.ctx, self.g
        T = self.T1
        NSU = T // 128
        NT1 = g.TCORE // T
        ctx.stage("a_init")
        ctx.dma("sync", self.dma, self.bias[:], ctx.dram["bqkv_img"].ap())
        ctx.dma("sync", self.dma, self.R_sb[:], ctx.dram["r_img"].ap())
        ctx.wait("sync", self.dma)
        ctx.stage("a_init2")
        ctx.op("vector", lambda e: e.tensor_scalar(
            out=self.R_sb[:], in0=self.R_sb[:], scalar1=0.125, scalar2=None,
            op0=ALU.mult))
        ctx.op("vector", lambda e: e.memset(self.kv_sb[:], 0.0))

        kv_started = [False]
        for t in range(NT1):
            for s in range(NSU):
                ctx.stage("a_x_%d_%d" % (t, s))
                ctx.dma("sync", self.dma, self.xt[:],
                        ctx.dram[x2_name].ap()[t * T + s * 128:
                                               t * T + s * 128 + 128, :])
                ctx.wait("sync", self.dma)
                ctx.stage("a_xtp_%d_%d" % (t, s))
                self.transpose_in(
                    [(self.xt[:, fc * 128:(fc + 1) * 128],
                      self.x_fm[:, fc * T + s * 128:fc * T + s * 128 + 128])
                     for fc in range(8)])
            self.proj_tile(t)
            for s in range(NSU):
                self.subtile(t, s, kv_started, q1_name,
                             is_last=(t == NT1 - 1 and s == NSU - 1))
        ctx.stage("a_kvout")
        ctx.dma("sync", self.dmo, ctx.dram[kvp_name].ap(),
                self.kv_sb[0:64, 0:1024])
        ctx.wait("sync", self.dmo)

    def proj_tile(self, t):
        ctx, g, T = self.ctx, self.g, self.T1
        NSU = T // 128
        for pi, wname in enumerate(("wq1_img", "wk1_img", "wv1_img")):
            ctx.stage("a_w_%d_%s" % (t, wname))
            for och in range(2):
                ctx.dma("sync", self.dma,
                        self.wbuf[:, och * 4096:och * 4096 + 4096]
                        .rearrange("p (kc n) -> p kc n", kc=8),
                        ctx.dram[wname].ap()[:, och * 512:och * 512 + 512]
                        .rearrange("(kc p) n -> p kc n", p=128))
            ctx.wait("sync", self.dma)
            ctx.stage("a_p_%d_%d" % (t, pi))
            ev_marks = []
            for s in range(NSU):
                for och in range(2):
                    idx = s * 2 + och
                    w = [(self.vv, ev_marks[idx - 2])] if idx >= 2 else []
                    for kc in range(8):
                        ctx.run("tensor",
                                lambda e, och=och, kc=kc, s=s:
                                e.matmul(self.P03[:, och * 512:och * 512 + 512],
                                         self.x_fm[:, kc * T + s * 128:
                                                   kc * T + s * 128 + 128],
                                         self.wbuf[:, och * 4096 + kc * 512:
                                                   och * 4096 + (kc + 1) * 512],
                                         start=(kc == 0), stop=(kc == 7)),
                                waits=(w if kc == 0 else []), inc=self.mm)
                    ev_marks.append(ctx.run(
                        "vector",
                        lambda e, och=och, pi=pi, s=s:
                        e.tensor_add(
                            self.qkv_tm[:, s * 3072 + pi * 1024 + och * 512:
                                        s * 3072 + pi * 1024 + och * 512 + 512],
                            self.P03[:, och * 512:och * 512 + 512],
                            self.bias[:, pi * 1024 + och * 512:
                                      pi * 1024 + och * 512 + 512]),
                        waits=[(self.mm, self.mm.total)], inc=self.vv))

    def subtile(self, t, s, kv_started, q1_name, is_last=False):
        ctx, g, T = self.ctx, self.g, self.T1
        if "dbg_aqkv" in ctx.dram:
            ctx.stage("a_dbg_%d_%d" % (t, s))
            ctx.dma("sync", self.dmo, ctx.dram["dbg_aqkv"].ap()[s, :, :],
                    self.qkv_tm[:, s * 3072:(s + 1) * 3072])
            ctx.wait("sync", self.dmo)
        ctx.stage("a_sq_%d_%d" % (t, s))
        for pi in range(2):
            for h in range(16):
                ctx.op("scalar", lambda e, pi=pi, h=h:
                       e.activation(out=self.scr[:, 0:64],
                                    in_=self.qkv_tm[:, s * 3072 + pi * 1024 +
                                                    h * 64:
                                                    s * 3072 + pi * 1024 +
                                                    h * 64 + 64],
                                    func=AF.Square, scale=0.125,
                                    accum_out=self.sq_tm[:, pi * 16 + h:
                                                         pi * 16 + h + 1]))
        ctx.stage("a_sq2_%d_%d" % (t, s))
        ctx.op("vector", lambda e: e.tensor_scalar(
            out=self.sq_tm[:], in0=self.sq_tm[:], scalar1=-0.5, scalar2=None,
            op0=ALU.mult))
        for pi in range(2):
            dst = self.qkv_tm if pi == 0 else self.kp_tm
            dcol = 0
            for h in range(16):
                ctx.stage("a_fm_%d_%d_%d_%d" % (t, s, pi, h))
                mmv = ctx.run("tensor",
                              lambda e, pi=pi, h=h, s=s:
                              e.transpose(self.P47[0:64, 0:128],
                                          self.qkv_tm[:, s * 3072 + pi * 1024 +
                                                      h * 64:
                                                      s * 3072 + pi * 1024 +
                                                      h * 64 + 64],
                                          self.C("ident", 128)),
                              inc=self.mm)
                vv1 = ctx.run("vector",
                              lambda e:
                              e.tensor_copy(self.scr[0:64, 0:128],
                                            self.P47[0:64, 0:128]),
                              waits=[(self.mm, mmv)], inc=self.vv)
                mmv2 = ctx.run("tensor",
                               lambda e:
                               e.matmul(self.P47[:, 256:320],
                                        self.scr[0:64, 0:128],
                                        self.R_sb[0:64, 0:64],
                                        start=True, stop=True),
                               waits=[(self.vv, vv1),
                                      (self.ss, self.ss.total)],
                               inc=self.mm)
                if pi == 0:
                    ctx.run("scalar",
                            lambda e, h=h, s=s:
                            e.activation(out=self.qkv_tm[:, s * 3072 + h * 64:
                                                         s * 3072 + h * 64 + 64],
                                         in_=self.P47[:, 256:320], func=AF.Exp,
                                         bias=self.sq_tm[:, h:h + 1],
                                         accum_out=self.zrec[:, h:h + 1]),
                            waits=[(self.mm, mmv2)], inc=self.ss)
                else:
                    ctx.run("scalar",
                            lambda e, h=h:
                            e.activation(out=self.kp_tm[:, h * 64:h * 64 + 64],
                                         in_=self.P47[:, 256:320], func=AF.Exp,
                                         bias=self.sq_tm[:, 16 + h:16 + h + 1]),
                            waits=[(self.mm, mmv2)], inc=self.ss)
        ctx.stage("a_z_%d_%d" % (t, s))
        def zf(e):
            ctx.lock(e, "vector",
                     e.tensor_scalar(out=self.zrec[:], in0=self.zrec[:],
                                     scalar1=self.C("s_eps8", 1), scalar2=None,
                                     op0=ALU.add))
            return e.reciprocal(self.zrec[:], self.zrec[:])
        ctx.op("vector", zf)
        ctx.stage("a_qz_%d_%d" % (t, s))
        for h in range(16):
            ctx.op("vector", lambda e, h=h, s=s:
                   e.tensor_scalar(out=self.qkv_tm[:, s * 3072 + h * 64:
                                                   s * 3072 + h * 64 + 64],
                                   in0=self.qkv_tm[:, s * 3072 + h * 64:
                                                   s * 3072 + h * 64 + 64],
                                   scalar1=self.zrec[:, h:h + 1], scalar2=None,
                                   op0=ALU.mult))
        ctx.stage("a_kv_%d_%d" % (t, s))
        kv_marks = {}
        for h in range(16):
            bank = 6 + h % 2
            w = [kv_marks[h - 2]] if h >= 2 else []
            mmv = ctx.run("tensor",
                          lambda e, h=h, bank=bank, s=s:
                          e.matmul(self.PB[bank][0:64, 0:64],
                                   self.kp_tm[:, h * 64:h * 64 + 64],
                                   self.qkv_tm[:, s * 3072 + 2 * 1024 + h * 64:
                                               s * 3072 + 2 * 1024 + h * 64 +
                                               64],
                                   start=True, stop=True),
                          waits=w, inc=self.mm)
            kv_marks[h] = (self.vv, ctx.run(
                "vector",
                lambda e, h=h, bank=bank:
                e.tensor_add(self.kv_sb[0:64, h * 64:h * 64 + 64],
                             self.kv_sb[0:64, h * 64:h * 64 + 64],
                             self.PB[bank][0:64, 0:64]),
                waits=[(self.mm, mmv)], inc=self.vv))
        kv_started[0] = True
        ctx.stage("a_spill_%d_%d" % (t, s))
        T1 = self.T1
        ctx.dma("sync", self.dmo,
                ctx.dram["q1"].ap()[t * T1 + s * 128:t * T1 + s * 128 + 128, :],
                self.qkv_tm[:, s * 3072:s * 3072 + 1024])
        ctx.wait("sync", self.dmo)


class L1B:

    def __init__(self, ctx, coff, consts_w):
        self.ctx = ctx
        self.g = ctx.g
        self.coff = coff
        self.T1 = 512
        T = self.T1
        def get(name, shape):
            return ctx.t[name] if name in ctx.t else ctx.sbuf(name, shape)
        self.consts = get("consts_sb", [128, consts_w])
        self.wbuf = get("wbuf", [128, 8 * 1024])
        self.xt = get("b_xt", [128, 1024])
        self.att = get("b_att", [128, (T // 128) * 1024])
        self.afm = get("b_afm", [128, 8 * T])
        self.kv2 = get("b_kv2", [64, 1024])
        self.bo = get("b_bo", [128, 1024])
        self.scr = get("scr", [128, 512])
        self.P03 = ctx.t["P03"]
        self.P47 = ctx.t["P47"]
        self.PB = self.P03.banks
        c = ctx.counter
        self.dma = c("c_dma", dma=True)
        self.dmo = c("c_dmo", dma=True)
        self.mm = c("c_mm")
        self.vv = c("c_vv")
        self.ss = c("c_ss")

    def C(self, name, w=32):
        o = self.coff[name]
        return self.consts[:, o:o + w]

    def build(self, kvt_name="kv_tot", x2_name="x2", x3_name="x3"):
        ctx, g = self.ctx, self.g
        T = self.T1
        NSU = T // 128
        NT1 = g.TCORE // T
        ctx.stage("b_init")
        ctx.dma("sync", self.dma, self.kv2[:], ctx.dram[kvt_name].ap())
        ctx.dma("sync", self.dma, self.bo[:], ctx.dram["bo_img"].ap())
        ctx.wait("sync", self.dma)
        for t in range(NT1):
            for s in range(NSU):
                ctx.stage("b_q_%d_%d" % (t, s))
                ctx.dma("sync", self.dma, self.xt[:],
                        ctx.dram["q1"].ap()[t * T + s * 128:
                                            t * T + s * 128 + 128, :])
                ctx.wait("sync", self.dma)
                out_marks = {}
                for h in range(16):
                    ctx.stage("b_h_%d_%d_%d" % (t, s, h))
                    mmv = ctx.run("tensor",
                                  lambda e, h=h:
                                  e.transpose(self.P47[0:64, 0:128],
                                              self.xt[:, h * 64:h * 64 + 64],
                                              self.C("ident", 128)),
                                  inc=self.mm)
                    vv1 = ctx.run("vector",
                                  lambda e:
                                  e.tensor_copy(self.scr[0:64, 0:128],
                                                self.P47[0:64, 0:128]),
                                  waits=[(self.mm, mmv)], inc=self.vv)
                    w = [(self.vv, vv1)]
                    if h >= 4:
                        w.append(out_marks[h - 4])
                    mmo = ctx.run("tensor",
                            lambda e, h=h:
                            e.matmul(self.PB[h % 4][:, 0:64],
                                     self.scr[0:64, 0:128],
                                     self.kv2[0:64, h * 64:h * 64 + 64],
                                     start=True, stop=True),
                            waits=w, inc=self.mm)
                    out_marks[h] = (self.vv, ctx.run(
                        "vector",
                        lambda e, h=h, s=s:
                        e.tensor_copy(self.att[:, s * 1024 + h * 64:
                                               s * 1024 + h * 64 + 64],
                                      self.PB[h % 4][:, 0:64]),
                        waits=[(self.mm, mmo)], inc=self.vv))
            ctx.stage("b_tp_%d" % t)
            prs = []
            for s in range(NSU):
                for fc in range(8):
                    prs.append((self.att[:, s * 1024 + fc * 128:
                                         s * 1024 + (fc + 1) * 128],
                                self.afm[:, fc * T + s * 128:
                                         fc * T + s * 128 + 128]))
            marks = []
            for idx, (srcp, dstp) in enumerate(prs):
                bo_ = (idx % 4) * 512
                w = [marks[idx - 4]] if idx >= 4 else []
                mmv = ctx.run("tensor",
                              lambda e, srcp=srcp, bo_=bo_:
                              e.transpose(self.P47[:, bo_:bo_ + 128], srcp,
                                          self.C("ident", 128)),
                              waits=w, inc=self.mm)
                marks.append((self.vv, ctx.run(
                    "vector",
                    lambda e, dstp=dstp, bo_=bo_:
                    e.tensor_copy(dstp, self.P47[:, bo_:bo_ + 128]),
                    waits=[(self.mm, mmv)], inc=self.vv)))
                marks[-1] = (self.vv, marks[-1][1])
            for och in range(2):
                ctx.stage("b_wo_%d_%d" % (t, och))
                ctx.dma("sync", self.dma,
                        self.wbuf[:, och * 4096:och * 4096 + 4096]
                        .rearrange("p (kc n) -> p kc n", kc=8),
                        ctx.dram["wo1_img"].ap()[:, och * 512:och * 512 + 512]
                        .rearrange("(kc p) n -> p kc n", p=128))
                ctx.wait("sync", self.dma)
                ctx.stage("b_womm_%d_%d" % (t, och))
                wo_marks = []
                for s in range(NSU):
                    for kc in range(8):
                        w = []
                        if kc == 0 and s >= 2:
                            w = [(self.vv, wo_marks[s - 2])]
                        ctx.run("tensor",
                                lambda e, s=s, kc=kc, och=och:
                                e.matmul(self.P03[:, (s % 2) * 512:
                                                  (s % 2) * 512 + 512],
                                         self.afm[:, kc * T + s * 128:
                                                  kc * T + s * 128 + 128],
                                         self.wbuf[:, och * 4096 + kc * 512:
                                                   och * 4096 + (kc + 1) * 512],
                                         start=(kc == 0), stop=(kc == 7)),
                                waits=w, inc=self.mm)
                    wo_marks.append(ctx.run("vector",
                            lambda e, s=s, och=och:
                            e.tensor_add(self.att[:, s * 1024 + och * 512:
                                                 s * 1024 + och * 512 + 512],
                                         self.P03[:, (s % 2) * 512:
                                                  (s % 2) * 512 + 512],
                                         self.bo[:, och * 512:
                                                 och * 512 + 512]),
                            waits=[(self.mm, self.mm.total)], inc=self.vv))
            for s in range(NSU):
                ctx.stage("b_res_%d_%d" % (t, s))
                ctx.dma("sync", self.dma, self.xt[:],
                        ctx.dram[x2_name].ap()[t * T + s * 128:
                                               t * T + s * 128 + 128, :])
                ctx.wait("sync", self.dma)
                ctx.stage("b_res2_%d_%d" % (t, s))
                ysl = self.att[:, s * 1024:(s + 1) * 1024]
                ctx.op("vector", lambda e, ysl=ysl:
                       e.tensor_add(ysl, ysl, self.xt[:, 0:1024]))
                emit_ln(self, ysl, "b_ln_%d_%d" % (t, s))
            ctx.stage("b_out_%d" % t)
            for s in range(NSU):
                ctx.dma("sync", self.dmo,
                        ctx.dram[x3_name].ap()[t * T + s * 128:
                                               t * T + s * 128 + 128, :],
                        self.att[:, s * 1024:(s + 1) * 1024])
            ctx.wait("sync", self.dmo)


def emit_ln(ph, ysl, tag):
    ctx = ph.ctx
    ctx.stage(tag + "_a")
    ctx.op("scalar", lambda e: e.activation(out=ph.xt[:, 0:1024], in_=ysl,
                                            func=AF.Copy,
                                            accum_out=ph.scr[:, 4:5]))
    ctx.stage(tag + "_b")
    ctx.op("vector", lambda e: e.tensor_scalar(
        out=ph.scr[:, 5:6], in0=ph.scr[:, 4:5],
        scalar1=ph.C("s_inv1024", 1), scalar2=None, op0=ALU.mult))
    ctx.stage(tag + "_c")
    ctx.op("vector", lambda e: e.tensor_scalar(
        out=ysl, in0=ysl, scalar1=ph.scr[:, 5:6], scalar2=None,
        op0=ALU.subtract))
    ctx.stage(tag + "_d")
    ctx.op("scalar", lambda e: e.activation(out=ph.xt[:, 0:1024], in_=ysl,
                                            func=AF.Square,
                                            accum_out=ph.scr[:, 6:7]))
    ctx.op("scalar", lambda e: e.activation(
        out=ph.scr[:, 7:8], in_=ph.scr[:, 6:7], func=AF.Sqrt,
        scale=1.0 / 1024, bias=ph.C("s_eps5", 1)))
    ctx.stage(tag + "_e")
    def lnf(e):
        ctx.lock(e, "vector",
                 e.reciprocal(ph.scr[:, 7:8], ph.scr[:, 7:8]))
        return e.tensor_scalar(out=ysl, in0=ysl, scalar1=ph.scr[:, 7:8],
                               scalar2=None, op0=ALU.mult)
    ctx.op("vector", lnf)


class FFN1:

    def __init__(self, ctx, coff, consts_w):
        self.ctx = ctx
        self.g = ctx.g
        self.ST = min(1024, self.g.TCORE)
        self.coff = coff
        def get(name, shape):
            return ctx.t[name] if name in ctx.t else ctx.sbuf(name, shape)
        self.consts = get("consts_sb", [128, consts_w])
        self.x_fm = get("n_xfm", [128, 8 * self.ST])
        self.slab = get("wbuf", [128, 8 * 1024])
        self.gh = get("n_gh", [128, 4 * self.ST])
        self.out = get("n_out", [128, (self.ST // 128) * 1024])
        self.g16 = get("n_g16", [16, 1024])
        self.gwnT = get("n_gwnT", [128, 8 * 16])
        self.cwb = get("n_cwb", [128, (self.ST // 128) * 16])
        self.xt = get("n_xt", [128, 1024])
        self.scr = get("scr", [128, 512])
        self.P03 = ctx.t["P03"]
        self.P47 = ctx.t["P47"]
        self.PB = self.P03.banks
        c = ctx.counter
        self.dma = c("c_dma", dma=True)
        self.dmo = c("c_dmo", dma=True)
        self.mm = c("c_mm")
        self.vv = c("c_vv")
        self.ss = c("c_ss")
        self._tpm = []

    def C(self, name, w=32):
        o = self.coff[name]
        return self.consts[:, o:o + w]

    def transpose_in(self, pairs):
        ctx = self.ctx
        ident = self.C("ident", 128)
        for (src, dst) in pairs:
            idx = len(self._tpm)
            bo = (idx % 4) * 512
            waits = [self._tpm[idx - 4]] if idx >= 4 else []
            mmv = ctx.run("tensor",
                          lambda e, src=src, bo=bo:
                          e.transpose(self.P47[:, bo:bo + 128], src, ident),
                          waits=waits, inc=self.mm)
            mark = ctx.run("vector",
                           lambda e, dst=dst, bo=bo:
                           e.tensor_copy(dst, self.P47[:, bo:bo + 128]),
                           waits=[(self.mm, mmv)], inc=self.vv)
            self._tpm.append((self.vv, mark))

    def build(self, x3_name="x3", out_name="out"):
        ctx, g = self.ctx, self.g
        ST = self.ST
        NSU = ST // 128
        NST = g.TCORE // ST
        ctx.stage("n_init")
        ctx.dma("sync", self.dma, self.g16[:], ctx.dram["gate16_img"].ap())
        ctx.wait("sync", self.dma)
        ctx.stage("n_gwn")
        ctx.op("scalar", lambda e: e.activation(
            out=self.xt[0:16, 0:1024], in_=self.g16[:], func=AF.Square,
            accum_out=self.scr[0:16, 0:1]))
        ctx.stage("n_gwn2")
        ctx.op("scalar", lambda e: e.activation(
            out=self.scr[0:16, 1:2], in_=self.scr[0:16, 0:1], func=AF.Sqrt))
        ctx.stage("n_gwn3")
        def gwn3(e):
            ctx.lock(e, "vector",
                     e.tensor_scalar(out=self.scr[0:16, 1:2],
                                     in0=self.scr[0:16, 1:2],
                                     scalar1=self.C("s_eps12", 1)[0:16, :],
                                     scalar2=None, op0=ALU.max))
            ctx.lock(e, "vector",
                     e.reciprocal(self.scr[0:16, 1:2], self.scr[0:16, 1:2]))
            return e.tensor_scalar(out=self.g16[:], in0=self.g16[:],
                                   scalar1=self.scr[0:16, 1:2], scalar2=None,
                                   op0=ALU.mult)
        ctx.op("vector", gwn3)
        ctx.stage("n_gwn4")
        ev_marks = []
        for kc in range(8):
            bo = (kc % 4) * 512
            w = [(self.vv, ev_marks[kc - 4])] if kc >= 4 else []
            mmv = ctx.run("tensor",
                          lambda e, kc=kc, bo=bo:
                          e.transpose(self.P47[0:128, bo:bo + 16],
                                      self.g16[0:16, kc * 128:(kc + 1) * 128],
                                      self.C("ident", 128)[0:16, 0:16]),
                          waits=w, inc=self.mm)
            ev_marks.append(ctx.run(
                "vector",
                lambda e, kc=kc, bo=bo:
                e.tensor_copy(self.gwnT[:, kc * 16:(kc + 1) * 16],
                              self.P47[0:128, bo:bo + 16]),
                waits=[(self.mm, mmv)], inc=self.vv))

        if "dbg_ngwnT" in ctx.dram:
            ctx.stage("n_dbgg")
            ctx.dma("sync", self.dmo, ctx.dram["dbg_ngwnT"].ap(),
                    self.gwnT[:])
            ctx.wait("sync", self.dmo)
        for st in range(NST):
            for s in range(NSU):
                ctx.stage("n_x_%d_%d" % (st, s))
                ctx.dma("sync", self.dma, self.xt[:],
                        ctx.dram[x3_name].ap()[st * ST + s * 128:
                                               st * ST + s * 128 + 128, :])
                ctx.wait("sync", self.dma)
                ctx.stage("n_xtp_%d_%d" % (st, s))
                self.transpose_in(
                    [(self.xt[:, fc * 128:(fc + 1) * 128],
                      self.x_fm[:, fc * ST + s * 128:fc * ST + s * 128 + 128])
                     for fc in range(8)])
                ctx.stage("n_xn_%d_%d" % (st, s))
                ctx.run("scalar",
                        lambda e, s=s:
                        e.activation(out=self.xt[:, 0:1024],
                                     in_=self.xt[:, 0:1024], func=AF.Square,
                                     accum_out=self.scr[:, 8:9]),
                        inc=self.ss)
                ctx.stage("n_xn2_%d_%d" % (st, s))
                ctx.op("scalar", lambda e: e.activation(
                    out=self.scr[:, 9:10], in_=self.scr[:, 8:9], func=AF.Sqrt))
                ctx.stage("n_xn3_%d_%d" % (st, s))
                def xn3(e, s=s):
                    ctx.lock(e, "vector",
                             e.tensor_scalar(out=self.scr[:, 9:10],
                                             in0=self.scr[:, 9:10],
                                             scalar1=self.C("s_eps12", 1),
                                             scalar2=None, op0=ALU.max))
                    return e.reciprocal(self.scr[:, 40 + s:41 + s],
                                        self.scr[:, 9:10])
                ctx.op("vector", xn3)
            if "dbg_nxfm" in ctx.dram:
                ctx.stage("n_dbgx_%d" % st)
                ctx.dma("sync", self.dmo, ctx.dram["dbg_nxfm"].ap()[st, :, :],
                        self.x_fm[:])
                ctx.wait("sync", self.dmo)
            ctx.stage("n_rt_%d" % st)
            rt_marks = []
            for s in range(NSU):
                for kc in range(8):
                    w = []
                    if kc == 0 and s >= 2:
                        w = [(self.vv, rt_marks[s - 2])]
                    ctx.run("tensor",
                            lambda e, s=s, kc=kc:
                            e.matmul(self.P03[:, (s % 2) * 512:
                                              (s % 2) * 512 + 16],
                                     self.x_fm[:, kc * ST + s * 128:
                                                kc * ST + s * 128 + 128],
                                     self.gwnT[:, kc * 16:(kc + 1) * 16],
                                     start=(kc == 0), stop=(kc == 7)),
                            waits=w, inc=self.mm)
                def rtev(e, s=s):
                    if "dbg_nlg" in self.ctx.dram:
                        e.tensor_copy(self.gh[:, s * 16:(s + 1) * 16],
                                      self.P03[:, (s % 2) * 512:
                                               (s % 2) * 512 + 16])
                    e.tensor_scalar(out=self.cwb[:, s * 16:(s + 1) * 16],
                                    in0=self.P03[:, (s % 2) * 512:
                                                 (s % 2) * 512 + 16],
                                    scalar1=self.scr[:, 40 + s:41 + s],
                                    scalar2=None, op0=ALU.mult)
                    return e.engine_nop()
                rt_marks.append(ctx.run("vector", rtev,
                                        waits=[(self.mm, self.mm.total)],
                                        inc=self.vv))
            ctx.stage("n_sig_%d" % st)
            for s in range(NSU):
                ctx.op("scalar", lambda e, s=s: e.activation(
                    out=self.cwb[:, s * 16:(s + 1) * 16],
                    in_=self.cwb[:, s * 16:(s + 1) * 16], func=AF.Sigmoid))
            if "dbg_nlg" in ctx.dram:
                ctx.stage("n_dbgl_%d" % st)
                ctx.dma("sync", self.dmo, ctx.dram["dbg_nlg"].ap()[st, :, :],
                        self.gh[:, 0:128])
                ctx.wait("sync", self.dmo)
            if "dbg_nsc" in ctx.dram:
                ctx.stage("n_dbgs_%d" % st)
                ctx.dma("sync", self.dmo, ctx.dram["dbg_nsc"].ap()[st, :, :],
                        self.cwb[:])
                ctx.wait("sync", self.dmo)
            ctx.stage("n_topk_%d" % st)
            for s in range(NSU):
                def topk(e, s=s):
                    p = self.cwb[:, s * 16:(s + 1) * 16]
                    cnt = self.scr[:, 16:32]
                    for ee in range(16):
                        if ee == 0:
                            ins = e.tensor_scalar(out=cnt, in0=p,
                                                  scalar1=p[:, 0:1],
                                                  scalar2=None, op0=ALU.is_lt)
                        else:
                            ins = e.scalar_tensor_tensor(out=cnt, in0=p,
                                                         scalar=p[:, ee:ee + 1],
                                                         in1=cnt,
                                                         op0=ALU.is_lt,
                                                         op1=ALU.add)
                        ctx.lock(e, "vector", ins)
                    ctx.lock(e, "vector",
                             e.tensor_scalar(out=cnt, in0=cnt, scalar1=8.0,
                                             scalar2=None, op0=ALU.is_lt))
                    ctx.lock(e, "vector", e.tensor_mul(p, p, cnt))
                    ctx.lock(e, "vector",
                             e.reduce_sum(self.scr[:, 32:33], p, axis=AX.X))
                    ctx.lock(e, "vector",
                             e.reciprocal(self.scr[:, 33:34],
                                          self.scr[:, 32:33]))
                    return e.tensor_scalar(out=p, in0=p,
                                           scalar1=self.scr[:, 33:34],
                                           scalar2=None, op0=ALU.mult)
                ctx.op("vector", topk)
            if "dbg_ncwb" in ctx.dram:
                ctx.stage("n_dbgc_%d" % st)
                ctx.dma("sync", self.dmo, ctx.dram["dbg_ncwb"].ap()[st, :, :],
                        self.cwb[:])
                ctx.wait("sync", self.dmo)
            ctx.stage("n_clr_%d" % st)
            ctx.op("vector", lambda e: e.memset(self.out[:], 0.0))
            for ex in range(17):
                self.expert(st, ex)
            for s in range(NSU):
                ctx.stage("n_res_%d_%d" % (st, s))
                ctx.dma("sync", self.dma, self.xt[:],
                        ctx.dram[x3_name].ap()[st * ST + s * 128:
                                               st * ST + s * 128 + 128, :])
                ctx.wait("sync", self.dma)
                ctx.stage("n_res2_%d_%d" % (st, s))
                ysl = self.out[:, s * 1024:(s + 1) * 1024]
                ctx.op("vector", lambda e, ysl=ysl:
                       e.tensor_add(ysl, ysl, self.xt[:, 0:1024]))
                emit_ln(self, ysl, "n_ln_%d_%d" % (st, s))
            ctx.stage("n_out_%d" % st)
            for s in range(NSU):
                ctx.dma("sync", self.dmo,
                        ctx.dram[out_name].ap()[st * ST + s * 128:
                                                st * ST + s * 128 + 128, :],
                        self.out[:, s * 1024:(s + 1) * 1024])
            ctx.wait("sync", self.dmo)

    def expert(self, st, ex):
        ctx, g = self.ctx, self.g
        ST = self.ST
        NSU = ST // 128
        shared = (ex == 16)
        wg_ap = (ctx.dram["swg_img"].ap() if shared
                 else ctx.dram["wg1_img"].ap()[ex, :, :])
        wu_ap = (ctx.dram["swu_img"].ap() if shared
                 else ctx.dram["wu1_img"].ap()[ex, :, :])
        wd_ap = (ctx.dram["swd_img"].ap() if shared
                 else ctx.dram["wd1_img"].ap()[ex, :, :])
        ctx.stage("n_wg_%d_%d" % (st, ex))
        ctx.dma("sync", self.dma, self.slab[:, 0:4096], wg_ap)
        ctx.wait("sync", self.dma)
        ctx.stage("n_g_%d_%d" % (st, ex))
        g_marks = {}
        for oc4 in range(4):
            for nh in range(ST // 512):
                for kc in range(8):
                    w = []
                    if kc == 0 and oc4 > 0:
                        w = [(self.ss, g_marks[(oc4 - 1, nh)])]
                    ctx.run("tensor",
                            lambda e, oc4=oc4, nh=nh, kc=kc:
                            e.matmul(self.P03[:, (nh % 2) * 512:
                                              (nh % 2) * 512 + 512],
                                     self.slab[:, kc * 512 + oc4 * 128:
                                               kc * 512 + oc4 * 128 + 128],
                                     self.x_fm[:, kc * ST + nh * 512:
                                                kc * ST + nh * 512 + 512],
                                     start=(kc == 0), stop=(kc == 7)),
                            waits=w, inc=self.mm)
                g_marks[(oc4, nh)] = ctx.run(
                    "scalar",
                    lambda e, oc4=oc4, nh=nh:
                    e.activation(out=self.gh[:, oc4 * ST + nh * 512:
                                             oc4 * ST + nh * 512 + 512],
                                 in_=self.P03[:, (nh % 2) * 512:
                                              (nh % 2) * 512 + 512],
                                 func=AF.Silu),
                    waits=[(self.mm, self.mm.total)], inc=self.ss)
        ctx.stage("n_wu_%d_%d" % (st, ex))
        ctx.dma("sync", self.dma, self.slab[:, 0:4096], wu_ap)
        ctx.wait("sync", self.dma)
        ctx.stage("n_u_%d_%d" % (st, ex))
        u_marks = {}
        for oc4 in range(4):
            for nh in range(ST // 512):
                for kc in range(8):
                    w = []
                    if kc == 0 and oc4 > 0:
                        w = [(self.vv, u_marks[(oc4 - 1, nh)])]
                    ctx.run("tensor",
                            lambda e, oc4=oc4, nh=nh, kc=kc:
                            e.matmul(self.P03[:, (nh % 2) * 512:
                                              (nh % 2) * 512 + 512],
                                     self.slab[:, kc * 512 + oc4 * 128:
                                               kc * 512 + oc4 * 128 + 128],
                                     self.x_fm[:, kc * ST + nh * 512:
                                                kc * ST + nh * 512 + 512],
                                     start=(kc == 0), stop=(kc == 7)),
                            waits=w, inc=self.mm)
                u_marks[(oc4, nh)] = ctx.run(
                    "vector",
                    lambda e, oc4=oc4, nh=nh:
                    e.tensor_mul(self.gh[:, oc4 * ST + nh * 512:
                                         oc4 * ST + nh * 512 + 512],
                                 self.gh[:, oc4 * ST + nh * 512:
                                         oc4 * ST + nh * 512 + 512],
                                 self.P03[:, (nh % 2) * 512:
                                              (nh % 2) * 512 + 512]),
                    waits=[(self.mm, self.mm.total)], inc=self.vv)
        ctx.stage("n_wd_%d_%d" % (st, ex))
        ctx.dma("sync", self.dma, self.slab[:, 0:4096], wd_ap)
        ctx.wait("sync", self.dma)
        for s in range(NSU):
            ctx.stage("n_d_%d_%d_%d" % (st, ex, s))
            for och in range(2):
                for oc4 in range(4):
                    ctx.run("tensor",
                            lambda e, s=s, och=och, oc4=oc4:
                            e.matmul(self.P47[:, 1024 + och * 512:
                                              1024 + och * 512 + 512],
                                     self.gh[:, oc4 * ST + s * 128:
                                             oc4 * ST + s * 128 + 128],
                                     self.slab[:, oc4 * 1024 + och * 512:
                                               oc4 * 1024 + och * 512 + 512],
                                     start=(oc4 == 0), stop=(oc4 == 3)),
                            inc=self.mm)
                def dev(e, s=s, och=och, ex=ex, shared=shared):
                    o = self.out[:, s * 1024 + och * 512:
                                 s * 1024 + och * 512 + 512]
                    if shared:
                        return e.tensor_add(o, o,
                                            self.P47[:, 1024 + och * 512:
                                                     1024 + och * 512 + 512])
                    return e.scalar_tensor_tensor(
                        out=o, in0=self.P47[:, 1024 + och * 512:
                                            1024 + och * 512 + 512],
                        scalar=self.cwb[:, s * 16 + ex:s * 16 + ex + 1],
                        in1=o, op0=ALU.mult, op1=ALU.add)
                ctx.run("vector", dev, waits=[(self.mm, self.mm.total)],
                        inc=self.vv)



RG_PAIRS = [[0, 1], [2, 3], [4, 5], [6, 7]]



def g1_image(g1_w):
    g1T = np.zeros((1152, 2048), np.float32)
    g1T[:1084] = np.asarray(g1_w).T
    img = np.zeros((16, 128, 1152), np.float32)
    for oc in range(16):
        for kc in range(9):
            img[oc, :, kc * 128:(kc + 1) * 128] = \
                g1T[kc * 128:(kc + 1) * 128, oc * 128:(oc + 1) * 128]
    return img


def g2_image(g2_w):
    g2T = np.asarray(g2_w).T
    img = np.zeros((128, 16 * 20), np.float32)
    for kc in range(16):
        img[:, kc * 20:(kc + 1) * 20] = g2T[kc * 128:(kc + 1) * 128, :]
    return img


def fir_image(fs, fl, fw):
    f3 = np.asarray(fs).reshape(1024, 3)
    f31 = np.asarray(fl).reshape(1024, 31)
    f64 = np.asarray(fw).reshape(1024, 64)
    img = np.zeros((128, 8 * 98), np.float32)
    for fc in range(8):
        rows = slice(fc * 128, (fc + 1) * 128)
        img[:, fc * 98 + 0:fc * 98 + 3] = f3[rows]
        img[:, fc * 98 + 3:fc * 98 + 34] = f31[rows]
        img[:, fc * 98 + 34:fc * 98 + 98] = f64[rows]
    return img


def wgu_image(w):
    w = np.asarray(w)
    img = np.zeros((8, 2, 128, 8192), np.float32)
    for e in range(8):
        for half in range(2):
            for kc in range(8):
                for oc8 in range(8):
                    img[e, half, :, kc * 1024 + oc8 * 128:
                        kc * 1024 + oc8 * 128 + 128] = \
                        w[e][kc * 128:(kc + 1) * 128,
                             (half * 8 + oc8) * 128:(half * 8 + oc8) * 128 + 128]
    return img


def wd_image(w):
    w = np.asarray(w)
    img = np.zeros((8, 2, 128, 8192), np.float32)
    for e in range(8):
        for half in range(2):
            for oc8 in range(8):
                img[e, half, :, oc8 * 1024:(oc8 + 1) * 1024] = \
                    w[e][(half * 8 + oc8) * 128:(half * 8 + oc8) * 128 + 128, :]
    return img


def wgu1_image(w):
    w = np.asarray(w)
    img = np.zeros((16, 128, 4096), np.float32)
    for e in range(16):
        for kc in range(8):
            img[e, :, kc * 512:(kc + 1) * 512] = w[e][kc * 128:(kc + 1) * 128]
    return img


def wd1_image(w):
    w = np.asarray(w)
    img = np.zeros((16, 128, 4096), np.float32)
    for e in range(16):
        for oc4 in range(4):
            img[e, :, oc4 * 1024:(oc4 + 1) * 1024] = \
                w[e][oc4 * 128:(oc4 + 1) * 128]
    return img


def sw_image(w, down=False):
    w = np.asarray(w)
    img = np.zeros((128, 4096), np.float32)
    if down:
        for oc4 in range(4):
            img[:, oc4 * 1024:(oc4 + 1) * 1024] = w[oc4 * 128:(oc4 + 1) * 128]
    else:
        for kc in range(8):
            img[:, kc * 512:(kc + 1) * 512] = w[kc * 128:(kc + 1) * 128]
    return img


def build_images(I):
    cimg, coff = consts_image()
    vb20 = np.zeros(20, np.float32)
    vb20[4::5] = np.asarray(I["L0_attn_value_bias"])
    imgs = {
        "consts_img": cimg,
        "wq_img": lhsT_of(I["L0_attn_q_w"]),
        "wk_img": lhsT_of(I["L0_attn_k_w"]),
        "wv_img": lhsT_of(I["L0_attn_v_w"]),
        "bw_img": conv_w_layout(lhsT_of(I["L0_attn_b_w"])),
        "wconv_img": np.concatenate(
            [conv_w_layout(I["L0_attn_convq_w"]),
             conv_w_layout(I["L0_attn_convk_w"]),
             conv_w_layout(I["L0_attn_convv_w"])], axis=1),
        "fir_img": fir_image(I["L0_attn_fir_s"], I["L0_attn_fir_l"],
                             I["L0_attn_fir_wd"]),
        "g1_img": g1_image(I["L0_attn_g1_w"]),
        "g2_img": g2_image(I["L0_attn_g2_w"]),
        "g1b_img": np.ascontiguousarray(
            np.asarray(I["L0_attn_g1_b"]).reshape(16, 128).T),
        "g2b_img": np.ascontiguousarray(
            np.stack([np.asarray(I["L0_attn_g2_b"]), vb20], 1)),
        "lt_img": np.ascontiguousarray(
            np.repeat(np.asarray(I["L0_attn_log_temp"]), 5).reshape(20, 1)),
        "flr_img": np.ascontiguousarray(
            np.tile(np.asarray(I["L0_attn_floor_logit"]), (128, 1))),
        "onorm_img": np.ascontiguousarray(
            np.tile(np.asarray(I["L0_attn_onorm_w"]), (128, 4))),
        "wo_img": lhsT_of(I["L0_attn_o_w"]),
        "gate8_img": conv_w_layout(lhsT_of(I["L0_ffn_gate_w"])),
        "wg_img": wgu_image(I["L0_ffn_wg"]),
        "wu_img": wgu_image(I["L0_ffn_wu"]),
        "wd_img": wd_image(I["L0_ffn_wd"]),
        "wq1_img": lhsT_of(I["L1_attn_Wq"]),
        "wk1_img": lhsT_of(I["L1_attn_Wk"]),
        "wv1_img": lhsT_of(I["L1_attn_Wv"]),
        "wo1_img": lhsT_of(I["L1_attn_Wo"]),
        "bqkv_img": np.concatenate(
            [np.tile(np.asarray(I["L1_attn_b" + n]), (128, 1))
             for n in ("q", "k", "v")], 1).astype(np.float32),
        "bo_img": np.ascontiguousarray(
            np.tile(np.asarray(I["L1_attn_bo"]), (128, 1))).astype(np.float32),
        "r_img": np.concatenate([np.asarray(I["L1_attn_R"], np.float32),
                                 np.zeros((64, 64), np.float32)], 0),
        "gate16_img": np.asarray(I["L1_ffn_gate_w"], np.float32),
        "wg1_img": wgu1_image(I["L1_ffn_w_gate"]),
        "wu1_img": wgu1_image(I["L1_ffn_w_up"]),
        "wd1_img": wd1_image(I["L1_ffn_w_down"]),
        "swg_img": sw_image(I["L1_ffn_sw_gate"]),
        "swu_img": sw_image(I["L1_ffn_sw_up"]),
        "swd_img": sw_image(I["L1_ffn_sw_down"], down=True),
    }
    return imgs, coff, cimg.shape[1]



def build_program(geo, coff, consts_w, img_shapes, l0_tile=128,
                  rg=None):
    if rg is None:
        rg = RG_PAIRS
    nc = bass.Bass("TRN2", target_bir_lowering=False,
                   detect_race_conditions=False)
    ctx = Ctx(nc, geo)
    D = 1024
    ctx.din("x", [geo.TCORE, D])
    ctx.din("x_halo", [128, D])
    ctx.din("maskeo", [128, 2])
    for name, shp in img_shapes.items():
        ctx.din(name, list(shp))
    for name in ("x1", "x2", "q1", "x3"):
        ctx.dram[name] = nc.dram_tensor(name, [geo.TCORE, D], F32,
                                        kind="Internal")
    ctx.dram["cS_in"] = nc.dram_tensor("cS_in", [128, 2048], F32,
                                       kind="Internal")
    ctx.dram["cS_out"] = nc.dram_tensor("cS_out", [128, 2048], F32,
                                        kind="Internal")
    ctx.dram["kv_part"] = nc.dram_tensor("kv_part", [64, 1024], F32,
                                         kind="Internal")
    ctx.dram["kv_tot"] = nc.dram_tensor("kv_tot", [64, 1024], F32,
                                        kind="Internal")
    ctx.dout("out", [geo.TCORE, D])

    ctx.alloc_shared(consts_w)
    maskeo = ctx.sbuf("maskeo_sb", [128, 2])
    boot_dma = ctx.counter("c_dma", dma=True)
    cc = ctx.counter("c_cc")
    ctx.stage("boot")
    ctx.dma("sync", boot_dma, ctx.t["consts_sb"][:],
            ctx.dram["consts_img"].ap())
    ctx.dma("sync", boot_dma, maskeo[:], ctx.dram["maskeo"].ap())
    ctx.wait("sync", boot_dma)

    geo_l0 = Geo(TCORE=geo.TCORE, TILE=l0_tile)
    ctx.g = geo_l0
    ctx.phase_begin()
    l0 = L0(ctx, coff, consts_w)
    l0.build_pass(False)
    ctx.stage("scoll_1")
    ctx.op("vector", lambda e: e.tensor_scalar(
        out=l0.S[:], in0=l0.S[:], scalar1=maskeo[:, 0:1],
        scalar2=None, op0=ALU.mult))
    ctx.stage("scoll_2")
    ctx.dma("sync", l0.dmo, ctx.dram["cS_in"].ap(), l0.S[:])
    ctx.wait("sync", l0.dmo)
    ctx.stage("scoll_3")
    def coll_s(e):
        e.collective_compute(
            "AllReduce", ALU.add,
            ins=[ctx.dram["cS_in"].ap()], outs=[ctx.dram["cS_out"].ap()],
            replica_groups=rg).then_inc(cc.sem, 1)
        cc.total += 1
        e.wait_ge(cc.sem, cc.total)
    ctx.op("gpsimd", coll_s)
    l0.build_pass(True, s_init_name="cS_out", s_init_mask=maskeo[:, 1:2])
    ctx.phase_end()

    ctx.g = geo
    ctx.phase_begin()
    f0 = FFN0(ctx, coff, consts_w)
    f0.build("x1", "x2")
    ctx.phase_end()

    ctx.phase_begin()
    a = L1A(ctx, coff, consts_w)
    a.build("x2", "q1", "kv_part")
    ctx.stage("kvcoll")
    def coll_kv(e):
        e.collective_compute(
            "AllReduce", ALU.add,
            ins=[ctx.dram["kv_part"].ap()], outs=[ctx.dram["kv_tot"].ap()],
            replica_groups=rg).then_inc(cc.sem, 1)
        cc.total += 1
        e.wait_ge(cc.sem, cc.total)
    ctx.op("gpsimd", coll_kv)
    ctx.phase_end()
    ctx.phase_begin()
    b = L1B(ctx, coff, consts_w)
    b.build("kv_tot", "x2", "x3")
    ctx.phase_end()

    ctx.phase_begin()
    f1 = FFN1(ctx, coff, consts_w)
    f1.build("x3", "out")
    ctx.phase_end()

    ctx.emit()
    return nc, ctx


def make_in_maps(inputs, n_cores=8, tcore=4096):
    X = np.ascontiguousarray(np.asarray(inputs["L0_attn_hidden_states"],
                                        np.float32))
    imgs, coff, consts_w = build_images(inputs)
    in_maps = []
    for c in range(n_cores):
        b, h = c // 2, c % 2
        x = np.ascontiguousarray(X[b, h * tcore:(h + 1) * tcore])
        halo = (np.zeros((128, 1024), np.float32) if h == 0 else
                np.ascontiguousarray(X[b, tcore - 128:tcore]))
        mask = np.zeros((128, 2), np.float32)
        mask[:, h] = 1.0
        m = {"x": x, "x_halo": halo, "maskeo": mask}
        m.update(imgs)
        in_maps.append(m)
    return in_maps, imgs, coff, consts_w


def kernel(**inputs):
    tcore = 4096
    in_maps, imgs, coff, consts_w = make_in_maps(inputs, 8, tcore)
    geo = Geo(TCORE=tcore, TILE=512)
    img_shapes = {k: v.shape for k, v in imgs.items()}
    nc, ctx = build_program(geo, coff, consts_w, img_shapes, l0_tile=128)
    res = run_bass_kernel_spmd(nc, in_maps, core_ids=list(range(8)))
    out = np.zeros((4, 8192, 1024), np.float32)
    for c in range(8):
        b, h = c // 2, c % 2
        out[b, h * tcore:(h + 1) * tcore] = res.results[c]["out"]
    return out


# version 07
# speedup vs baseline: 1.0846x; 1.0846x over previous
import numpy as np
import concourse.bass as bass
from concourse.bass_utils import run_bass_kernel_spmd
import concourse.mybir as mybir

F32 = mybir.dt.float32
AF = mybir.ActivationFunctionType
ALU = mybir.AluOpType
AX = mybir.AxisListType

ENGS = ("sync", "tensor", "vector", "scalar", "gpsimd")

SCALARS = {
    "eps6": 1e-6, "eps5": 1e-5, "eps8": 1e-8, "eps12": 1e-12,
    "inv256": 1.0 / 256, "inv1024": 1.0 / 1024,
}
CNT_ENG = {"mm": "tensor", "vv": "vector", "ss": "scalar", "gg": "gpsimd"}


class Geo:
    def __init__(self, TCORE=4096, TILE=512):
        self.D = 1024
        self.H = 4
        self.DK = 256
        self.TCORE = TCORE
        self.TILE = TILE
        self.NSUB = TILE // 128
        self.NT = TCORE // TILE
        self.NCH = TILE // 32
        assert TILE % 128 == 0 and TCORE % TILE == 0


class Counter:
    def __init__(self, nc, name, dma=False):
        self.sem = nc.alloc_semaphore(name)
        self.total = 0
        self.step = 16 if dma else 1

    def bump(self):
        self.total += self.step
        return self.total


class PsumPool:

    def __init__(self, banks, base):
        self.banks = banks
        self.base = base

    def __getitem__(self, key):
        if not isinstance(key, tuple):
            key = (key, slice(None))
        ps, cs = key
        c0 = 0 if cs.start is None else cs.start
        c1 = 512 * 4 if cs.stop is None else cs.stop
        b = c0 // 512
        assert c1 - c0 <= 512 and (c1 - 1) // 512 == b, ("bank cross", c0, c1)
        return self.banks[self.base + b][ps, c0 - b * 512: c1 - b * 512]


class Ctx:
    def __init__(self, nc, geo):
        self.nc = nc
        self.g = geo
        self.deferred = []
        self.stages = []
        self.cur = None
        self.t = {}
        self.dram = {}
        self._counters = {}

    def sbuf(self, name, shape, dtype=F32):
        h = self.nc.alloc_sbuf_tensor(name, shape, dtype)
        self.t[name] = h
        return h

    def psum(self, name, shape, dtype=F32):
        h = self.nc.alloc_psum_tensor(name, shape, dtype)
        self.t[name] = h
        return h

    def counter(self, name, dma=False):
        if name not in self._counters:
            self._counters[name] = Counter(self.nc, name, dma)
        return self._counters[name]

    def din(self, name, shape, dtype=F32):
        h = self.nc.dram_tensor(name, list(shape), dtype, kind="ExternalInput")
        self.dram[name] = h
        return h

    def dout(self, name, shape, dtype=F32):
        h = self.nc.dram_tensor(name, list(shape), dtype, kind="ExternalOutput")
        self.dram[name] = h
        return h

    def phase_begin(self):
        self._ph_base = self.nc.sbuf_base
        self._ph_names = set(self.t.keys())

    def phase_end(self):
        self.nc.sbuf_base = self._ph_base
        for n in list(self.t):
            if n not in self._ph_names:
                del self.t[n]

    def alloc_shared(self, consts_w):
        self.sbuf("consts_sb", [128, consts_w])
        self.sbuf("wbuf", [128, 8 * 1024])
        self.sbuf("wbuf2", [128, 8 * 1024])
        self.sbuf("scr", [128, 512])
        banks = [self.psum("PB%d" % i, [128, 512]) for i in range(8)]
        self.t["P03"] = PsumPool(banks, 0)
        self.t["P47"] = PsumPool(banks, 4)

    def stage(self, name):
        self.cur = {e: [] for e in ENGS}
        self.cur["_name"] = name
        self.stages.append(self.cur)

    def op(self, eng, fn):
        self.cur[eng].append(fn)

    def run(self, eng, build, waits=(), inc=None):
        val = None
        if inc is not None:
            val = inc.bump()
        def fn(e, build=build, waits=tuple(waits), inc=inc):
            for c, v in waits:
                if v > 0:
                    e.wait_ge(c.sem, v)
            ins = build(e)
            if inc is not None:
                ins.then_inc(inc.sem, inc.step)
        self.op(eng, fn)
        return val

    def lock(self, e, eng, ins):
        c = self.counter("c_lk_" + eng)
        ins.then_inc(c.sem, 1)
        c.total += 1
        e.wait_ge(c.sem, c.total)
        return ins

    def dma(self, eng, cnt, out_ap, in_ap, waits=()):
        return self.run(eng,
                        lambda e: e.dma_start(out=out_ap, in_=in_ap),
                        waits=waits, inc=cnt)

    def wait(self, eng, cnt, val=None):
        v = cnt.total if val is None else val
        self.op(eng, lambda e, v=v: e.wait_ge(cnt.sem, v))

    def emit(self):
        done = {e: self.counter("done_" + e) for e in ENGS}
        stages = self.stages
        cum = {e: [0] for e in ENGS}
        for st in stages:
            for e in ENGS:
                cum[e].append(cum[e][-1] + (1 if st[e] else 0))

        def engine_body(ename):
            def body(eng):
                for i, st in enumerate(stages):
                    if not st[ename]:
                        continue
                    for o in ENGS:
                        if o != ename and cum[o][i] > 0:
                            eng.wait_ge(done[o].sem, cum[o][i])
                    for fn in st[ename]:
                        fn(eng)
                    if ename == "tensor":
                        eng.nop().then_inc(done[ename].sem, 1)
                    else:
                        eng.drain().then_inc(done[ename].sem, 1)
                for o in ENGS:
                    if o != ename and cum[o][-1] > 0:
                        eng.wait_ge(done[o].sem, cum[o][-1])
            return body

        with self.nc.Block() as block:
            block.sync(engine_body("sync"))
            block.tensor(engine_body("tensor"))
            block.vector(engine_body("vector"))
            block.scalar(engine_body("scalar"))
            block.gpsimd(engine_body("gpsimd"))



def lhsT_of(w):
    return np.ascontiguousarray(np.asarray(w).T.astype(np.float32))


def conv_w_layout(wc):
    wc = np.asarray(wc, dtype=np.float32)
    K = wc.shape[1]
    img = np.zeros((128, 8 * K), np.float32)
    for fc in range(8):
        img[:, fc * K:(fc + 1) * K] = wc[fc * 128:(fc + 1) * 128, :]
    return img


def consts_image():
    off = {}
    cols = []

    def add(name, arr):
        off[name] = sum(c.shape[1] for c in cols)
        cols.append(arr.astype(np.float32))

    add("ident", np.eye(128, dtype=np.float32))
    add("trilS", np.tile(np.tril(np.ones((32, 32), np.float32), -1), (4, 1)))
    add("triuI", np.tile(np.triu(np.ones((32, 32), np.float32), 0), (4, 1)))
    add("eye2", np.tile(2.0 * np.eye(32, dtype=np.float32), (4, 1)))
    add("eye1", np.tile(np.eye(32, dtype=np.float32), (4, 1)))
    add("ones", np.ones((128, 32), np.float32))
    for i, (nm, v) in enumerate(SCALARS.items()):
        add("s_" + nm, np.full((128, 1), v, np.float32))
    img = np.concatenate(cols, axis=1)
    return img, off



class L0:
    def __init__(self, ctx, coff, consts_w, dbg=()):
        self.ctx = ctx
        self.g = ctx.g
        self.coff = coff
        self.dbg = set(dbg)
        g = self.g
        T = g.TILE
        def get(name, shape):
            return ctx.t[name] if name in ctx.t else ctx.sbuf(name, shape)
        self.consts = get("consts_sb", [128, consts_w])
        self.x_tm = ctx.sbuf("x_tm", [128, g.NSUB * 1024])
        self.x_fm = ctx.sbuf("x_fm", [128, 8 * T])
        self.wbuf = get("wbuf", [128, 8 * 1024])
        self.bw_sb = ctx.sbuf("bw_sb", [128, 8 * 4])
        self.wconv = ctx.sbuf("wconv", [128, 3 * 8 * 4])
        self.raw = ctx.sbuf("raw", [128, 8 * (3 + T)])
        self.tail = ctx.sbuf("tail", [128, 3 * 8 * 3])
        self.k_fm = ctx.sbuf("k_fm", [128, 8 * T])
        self.VS = 63 + T
        self.v_fm = ctx.sbuf("v_fm", [128, 8 * self.VS])
        self.k_tm = ctx.sbuf("k_tm", [128, g.NSUB * 1024])
        self.v_tm = ctx.sbuf("v_tm", [128, g.NSUB * 1024])
        self.kb = ctx.sbuf("kb", [128, 1024])
        self.vb = ctx.sbuf("vb", [128, 1024])
        self.beta = ctx.sbuf("beta", [128, g.NSUB * 4])
        self.nrm = ctx.sbuf("nrm", [128, 3 * g.NSUB * 4])
        self.invT = ctx.sbuf("invT", [128, 16 * 32])
        self.nwt = ctx.sbuf("nwt", [128, 5 * 32])
        self.u_sb = ctx.sbuf("u_sb", [128, g.NSUB * 1024])
        self.w_sb = ctx.sbuf("w_sb", [128, g.NSUB * 4 * 256])
        self.uadj = ctx.sbuf("uadj", [128, 1024])
        self.S = ctx.sbuf("S", [128, 2048])
        self.scr = get("scr", [128, 512])
        self.pass2 = False
        self.P03 = ctx.t["P03"]
        self.P47 = ctx.t["P47"]
        self.PB = self.P03.banks
        c = ctx.counter
        self.dma = c("c_dma", dma=True)
        self.dmo = c("c_dmo", dma=True)
        self.mm = c("c_mm")
        self.vv = c("c_vv")
        self.ss = c("c_ss")
        self.gg = c("c_gg")
        self._tp_marks = []

    def C(self, name, w=32):
        o = self.coff[name]
        return self.consts[:, o:o + w]

    def transpose128(self, pairs, evac_engs=("vector", "scalar")):
        ctx = self.ctx
        ident = self.C("ident", 128)
        for idx0, (src, dst) in enumerate(pairs):
            idx = len(self._tp_marks)
            bo = (idx % 4) * 512
            waits = []
            if idx >= 4:
                cnt, mark = self._tp_marks[idx - 4]
                waits.append((cnt, mark))
            mmv = ctx.run("tensor",
                          lambda e, src=src, bo=bo:
                          e.transpose(self.P47[:, bo:bo + 128], src, ident),
                          waits=waits, inc=self.mm)
            eng = evac_engs[idx % len(evac_engs)]
            cnt = self.vv if eng == "vector" else self.ss
            if eng == "vector":
                mark = ctx.run(eng,
                               lambda e, dst=dst, bo=bo:
                               e.tensor_copy(dst, self.P47[:, bo:bo + 128]),
                               waits=[(self.mm, mmv)], inc=cnt)
            else:
                mark = ctx.run(eng,
                               lambda e, dst=dst, bo=bo:
                               e.copy(dst, self.P47[:, bo:bo + 128]),
                               waits=[(self.mm, mmv)], inc=cnt)
            self._tp_marks.append((cnt, mark))

    def load_w(self, wname, n_out=1024):
        ctx = self.ctx
        for kc in range(8):
            ctx.dma("sync", self.dma,
                    self.wbuf[:, kc * n_out:(kc + 1) * n_out],
                    ctx.dram[wname].ap()[kc * 128:(kc + 1) * 128, :])
        ctx.wait("sync", self.dma)

    def proj_fm(self, evac, ntok=None, n_out=1024):
        ctx, T = self.ctx, self.g.TILE
        nt = T if ntok is None else ntok
        noc = n_out // 128
        marks = []
        for oc in range(noc):
            reg = 512 * (oc % 2)
            waits = [(self.vv, marks[oc - 2])] if oc >= 2 else []
            for kc in range(8):
                mmv = ctx.run("tensor",
                              lambda e, kc=kc, oc=oc, reg=reg, nt=nt,
                              n_out=n_out:
                              e.matmul(self.P03[:, reg:reg + nt],
                                       self.wbuf[:, kc * n_out + oc * 128:
                                                 kc * n_out + oc * 128 + 128],
                                       self.x_fm[:, kc * T:kc * T + nt],
                                       start=(kc == 0), stop=(kc == 7)),
                              waits=(waits if kc == 0 else []), inc=self.mm)
            b = evac(oc)
            marks.append(ctx.run("vector", lambda e, b=b, reg=reg: b(e, reg),
                                 waits=[(self.mm, self.mm.total)], inc=self.vv))

    def conv4(self, out_fm, wcol_base, eng_name, ntok=None, out_stride=None,
              out_off=0):
        ctx, T = self.ctx, self.g.TILE
        nt = T if ntok is None else ntok
        ostr = T if out_stride is None else out_stride
        for fc in range(8):
            base = fc * (3 + T)
            for j in range(4):
                def c_fn(e, fc=fc, j=j, base=base, nt=nt, out_fm=out_fm,
                         wcol_base=wcol_base, ostr=ostr, out_off=out_off):
                    src = self.raw[:, base + j: base + j + nt]
                    dst = out_fm[:, fc * ostr + out_off:
                                 fc * ostr + out_off + nt]
                    wj = self.wconv[:, wcol_base + fc * 4 + j:
                                    wcol_base + fc * 4 + j + 1]
                    if j == 0:
                        e.tensor_scalar(out=dst, in0=src, scalar1=wj,
                                        scalar2=None, op0=ALU.mult)
                    else:
                        e.scalar_tensor_tensor(out=dst, in0=src, scalar=wj,
                                               in1=dst, op0=ALU.mult,
                                               op1=ALU.add)
                ctx.op(eng_name, c_fn)

    def margins_in(self, tail_base, eng="vector"):
        ctx, T = self.ctx, self.g.TILE
        for fc in range(8):
            ctx.op(eng, lambda e, fc=fc, tb=tail_base:
                   e.tensor_copy(self.raw[:, fc * (3 + T): fc * (3 + T) + 3],
                                 self.tail[:, tb + fc * 3: tb + fc * 3 + 3]))

    def tails_out(self, tail_base, ntok=None, eng="vector"):
        ctx, T = self.ctx, self.g.TILE
        nt = T if ntok is None else ntok
        for fc in range(8):
            ctx.op(eng, lambda e, fc=fc, tb=tail_base, nt=nt:
                   e.tensor_copy(self.tail[:, tb + fc * 3: tb + fc * 3 + 3],
                                 self.raw[:, fc * (3 + T) + nt:
                                          fc * (3 + T) + nt + 3]))

    def silu_fm(self, buf, ntok=None, stride=None, off=0):
        ctx, T = self.ctx, self.g.TILE
        nt = T if ntok is None else ntok
        st = T if stride is None else stride
        for fc in range(8):
            ctx.op("scalar", lambda e, fc=fc, nt=nt, st=st, off=off:
                   e.activation(out=buf[:, fc * st + off:fc * st + off + nt],
                                in_=buf[:, fc * st + off:fc * st + off + nt],
                                func=AF.Silu))

    def alloc_pass2(self):
        ctx, g = self.ctx, self.g
        T = g.TILE
        self.pass2 = True
        self.q_fm = ctx.sbuf("q_fm", [128, 8 * T])
        self.qnrm = ctx.sbuf("qnrm", [128, 2 * g.NSUB * 4])
        self.at_sb = ctx.sbuf("at_sb", [128, g.NSUB * 4 * 32])
        self.delta_tm = ctx.sbuf("delta_tm", [128, g.NSUB * 1024])
        self.fir_w = ctx.sbuf("fir_w", [128, 8 * 98])
        self.short_fm = ctx.sbuf("short_fm", [128, 8 * T])
        self.long_fm = ctx.sbuf("long_fm", [128, 8 * T])
        self.wide_fm = ctx.sbuf("wide_fm", [128, 8 * T])
        self.slw_tm = ctx.sbuf("slw_tm", [128, 3 * 1024])
        self.stat_tm = ctx.sbuf("stat_tm", [128, 64])
        self.gstat_fm = ctx.sbuf("gstat_fm", [128, T])
        self.hdn_fm = ctx.sbuf("hdn_fm", [128, g.NSUB * 2048])
        self.wsl = ctx.sbuf("wsl", [128, 2 * 1152])
        self.g2_sb = ctx.sbuf("g2_sb", [128, 16 * 20])
        self.g1b_sb = ctx.sbuf("g1b_sb", [128, 16])
        self.g2b_sb = ctx.sbuf("g2b_sb", [32, 4])
        self.ltexp_sb = ctx.sbuf("ltexp_sb", [32, 2])
        self.fsc_sb = ctx.sbuf("fsc_sb", [128, 12])
        self.onorm_sb = ctx.sbuf("onorm_sb", [128, 1024])
        self.wsm = ctx.sbuf("wsm", [128, 20])
        self.x1_tm = ctx.sbuf("x1_tm", [128, g.NSUB * 1024])

    def build_pass(self, with_outputs, s_init_name=None, s_init_mask=None):
        ctx, g, T = self.ctx, self.g, self.g.TILE
        if with_outputs and not self.pass2:
            self.alloc_pass2()
        ctx.stage("init" + ("_p2" if with_outputs else ""))
        ctx.dma("sync", self.dma, self.bw_sb[:], ctx.dram["bw_img"].ap())
        ctx.dma("sync", self.dma, self.wconv[:], ctx.dram["wconv_img"].ap())
        ctx.dma("sync", self.dma, self.consts[:], ctx.dram["consts_img"].ap())
        if s_init_name is not None:
            sap = ctx.dram[s_init_name]
            if list(sap.shape) == [128, 2048]:
                ctx.dma("sync", self.dma, self.S[:], sap.ap())
            else:
                ctx.dma("sync", self.dma,
                        self.S[:].rearrange("p (h c v) -> p h c v", h=4, c=2),
                        sap.ap().rearrange("h (c p) v -> p h c v", p=128))
        ctx.wait("sync", self.dma)
        if with_outputs:
            self._l0_init_post()
        if s_init_name is None:
            ctx.op("vector", lambda e: e.memset(self.S[:], 0.0))
        elif s_init_mask is not None:
            ctx.stage("sinit_mask" + ("_p2" if with_outputs else ""))
            ctx.op("vector", lambda e: e.tensor_scalar(
                out=self.S[:], in0=self.S[:], scalar1=s_init_mask,
                scalar2=None, op0=ALU.mult))
        ctx.op("vector", lambda e: e.memset(self.tail[:], 0.0))
        ctx.op("vector", lambda e: e.memset(self.raw[:], 0.0))
        ctx.op("vector", lambda e: e.memset(self.x_fm[:], 0.0))
        ctx.op("vector", lambda e: e.memset(self.uadj[:], 0.0))

        self.halo_pretile(with_outputs)
        for t in range(g.NT):
            self.tile_front(t, with_outputs)
            if with_outputs:
                self._l0_fir(t, defer=True)
            for s in range(g.NSUB):
                self.prep_subtile(t, s, with_outputs)
            for cj in range(g.NCH):
                self.scan_step(t, cj, with_outputs)
            if ctx.deferred:
                ctx.stage("fir_rem_t%d" % t)
                for eng_, fn_ in ctx.deferred:
                    ctx.op(eng_, fn_)
                ctx.deferred = []
            if with_outputs:
                self._l0_post(t)
            self.dump_dbg(t)

        ctx.stage("s_out")
        if "S_local" in ctx.dram:
            ctx.dma("sync", self.dmo,
                    ctx.dram["S_local"].ap().rearrange(
                        "h (c p) v -> p h c v", p=128),
                    self.S[:].rearrange("p (h c v) -> p h c v", h=4, c=2))
        ctx.wait("sync", self.dmo)

    def halo_pretile(self, with_outputs):
        ctx, g, T = self.ctx, self.g, self.g.TILE
        ctx.stage("halo_load")
        ctx.dma("sync", self.dma, self.x_tm[:, 0:1024],
                ctx.dram["x_halo"].ap())
        ctx.wait("sync", self.dma)
        ctx.stage("halo_tp")
        self.transpose128(
            [(self.x_tm[:, fc * 128:(fc + 1) * 128],
              self.x_fm[:, fc * T:fc * T + 128]) for fc in range(8)])
        plist = [("wk_img", 24), ("wv_img", 48)]
        if with_outputs:
            plist = [("wq_img", 0)] + plist
        for wname, tb in plist:
            ctx.stage("halo_w_" + wname)
            self.load_w(wname)
            ctx.stage("halo_" + wname)
            self.proj_fm(lambda oc: (lambda e, reg, oc=oc:
                                     e.tensor_copy(
                                         self.raw[:, oc * (3 + T) + 3:
                                                  oc * (3 + T) + 3 + 128],
                                         self.P03[:, reg:reg + 128])),
                         ntok=128)
            ctx.stage("halo_tail_" + wname)
            self.tails_out(tb, ntok=128)
            if wname == "wv_img" and with_outputs:
                ctx.stage("halo_vhist")
                self.conv4(self.v_fm, 64, "vector", ntok=128,
                           out_stride=self.VS, out_off=63)
                ctx.stage("halo_vhist_silu")
                self.silu_fm(self.v_fm, ntok=128, stride=self.VS, off=63)
                ctx.stage("halo_vhist2")
                for fc in range(8):
                    ctx.op("vector", lambda e, fc=fc:
                           e.tensor_copy(
                               self.v_fm[:, fc * self.VS:fc * self.VS + 63],
                               self.v_fm[:, fc * self.VS + 63 + 128 - 63:
                                         fc * self.VS + 63 + 128]))

    def tile_front(self, t, with_outputs):
        ctx, g, T = self.ctx, self.g, self.g.TILE
        x = ctx.dram["x"]
        for s in range(g.NSUB):
            ctx.stage("x_t%d_s%d" % (t, s))
            ctx.dma("sync", self.dma, self.x_tm[:, s * 1024:(s + 1) * 1024],
                    x.ap()[t * T + s * 128: t * T + s * 128 + 128, :])
            ctx.wait("sync", self.dma)
            ctx.stage("x_tp_t%d_s%d" % (t, s))
            self.transpose128(
                [(self.x_tm[:, s * 1024 + fc * 128:s * 1024 + (fc + 1) * 128],
                  self.x_fm[:, fc * T + s * 128:fc * T + s * 128 + 128])
                 for fc in range(8)])

        ctx.stage("beta_t%d" % t)
        for s in range(g.NSUB):
            reg = 1024 + 512 * (s % 2)
            for kc in range(8):
                ctx.run("tensor",
                        lambda e, s=s, kc=kc, reg=reg:
                        e.matmul(self.P03[:, reg:reg + 4],
                                 self.x_fm[:, kc * T + s * 128:
                                           kc * T + s * 128 + 128],
                                 self.bw_sb[:, kc * 4:(kc + 1) * 4],
                                 start=(kc == 0), stop=(kc == 7)),
                        inc=self.mm)
            ctx.run("scalar",
                    lambda e, s=s, reg=reg:
                    e.activation(out=self.beta[:, s * 4:(s + 1) * 4],
                                 in_=self.P03[:, reg:reg + 4],
                                 func=AF.Sigmoid),
                    waits=[(self.mm, self.mm.total)], inc=self.ss)

        plist = [("wk_img", self.k_fm, 24, 32, "vector", T, 0),
                 ("wv_img", self.v_fm, 48, 64, "vector", self.VS, 63)]
        if with_outputs:
            plist = [("wq_img", self.q_fm, 0, 0, "vector", T, 0)] + plist
        for wname, out_buf, tb, wcol, conv_eng, ostr, ooff in plist:
            ctx.stage("ldw_%s_t%d" % (wname, t))
            self.load_w(wname)
            ctx.stage("proj_%s_t%d" % (wname, t))
            self.proj_fm(lambda oc: (lambda e, reg, oc=oc:
                                     e.tensor_copy(
                                         self.raw[:, oc * (3 + T) + 3:
                                                  oc * (3 + T) + 3 + T],
                                         self.P03[:, reg:reg + T])))
            ctx.stage("conv_m_%s_t%d" % (wname, t))
            self.margins_in(tb)
            ctx.stage("conv_t_%s_t%d" % (wname, t))
            self.tails_out(tb)
            if wname == "wv_img" and t > 0:
                ctx.stage("vmargin_t%d" % t)
                for fc in range(8):
                    ctx.op("vector", lambda e, fc=fc:
                           e.tensor_copy(
                               self.v_fm[:, fc * self.VS:fc * self.VS + 63],
                               self.v_fm[:, fc * self.VS + T:
                                         fc * self.VS + T + 63]))
            ctx.stage("conv_%s_t%d" % (wname, t))
            self.conv4(out_buf, wcol, conv_eng, out_stride=ostr, out_off=ooff)
            ctx.stage("silu_%s_t%d" % (wname, t))
            self.silu_fm(out_buf, stride=ostr, off=ooff)

        ctx.stage("kv_tp_t%d" % t)
        pairs = []
        for buf_fm, buf_tm, st_, of_ in (
                (self.k_fm, self.k_tm, T, 0),
                (self.v_fm, self.v_tm, self.VS, 63)):
            for s in range(self.g.NSUB):
                for fc in range(8):
                    pairs.append((buf_fm[:, fc * st_ + of_ + s * 128:
                                         fc * st_ + of_ + s * 128 + 128],
                                  buf_tm[:, s * 1024 + fc * 128:
                                         s * 1024 + fc * 128 + 128]))
        self.transpose128(pairs)

        ctx.stage("norm_t%d" % t)
        NS4 = self.g.NSUB * 4
        for s in range(self.g.NSUB):
            for h in range(4):
                ctx.op("scalar", lambda e, s=s, h=h:
                       e.activation(out=self.scr[:, 0:256],
                                    in_=self.k_tm[:, s * 1024 + h * 256:
                                                  s * 1024 + h * 256 + 256],
                                    func=AF.Square,
                                    accum_out=self.nrm[:, s * 4 + h:
                                                       s * 4 + h + 1]))
        ctx.op("scalar", lambda e:
               e.activation(out=self.nrm[:, NS4:2 * NS4],
                            in_=self.nrm[:, 0:NS4], func=AF.Sqrt,
                            bias=self.C("s_eps6", 1)))
        if with_outputs:
            for s in range(self.g.NSUB):
                for h in range(4):
                    ctx.stage("qn_t%d_s%d_h%d" % (t, s, h))
                    for dkc in range(2):
                        fc = 2 * h + dkc
                        self.transpose128(
                            [(self.q_fm[:, fc * T + s * 128:
                                        fc * T + s * 128 + 128],
                              self.scr[:, dkc * 128:dkc * 128 + 128])],
                            evac_engs=("vector",))
                    vmark = self._tp_marks[-1][1]
                    ctx.run("scalar",
                            lambda e, s=s, h=h:
                            e.activation(out=self.scr[:, 256:512],
                                         in_=self.scr[:, 0:256],
                                         func=AF.Square,
                                         accum_out=self.qnrm[:, s * 4 + h:
                                                             s * 4 + h + 1]),
                            waits=[(self.vv, vmark)], inc=self.ss)
            ctx.stage("qn_fin_t%d" % t)
            ctx.op("scalar", lambda e:
                   e.activation(out=self.qnrm[:, NS4:2 * NS4],
                                in_=self.qnrm[:, 0:NS4], func=AF.Sqrt,
                                bias=self.C("s_eps6", 1)))
            ctx.stage("qn_fin2_t%d" % t)
            ctx.op("vector", lambda e:
                   e.reciprocal(self.qnrm[:, NS4:2 * NS4],
                                self.qnrm[:, NS4:2 * NS4]))
        ctx.stage("norm2_t%d" % t)
        def fin2(e):
            ctx.lock(e, "vector",
                     e.reciprocal(self.nrm[:, NS4:2 * NS4],
                                  self.nrm[:, NS4:2 * NS4]))
            e.tensor_mul(self.nrm[:, 2 * NS4:3 * NS4],
                         self.nrm[:, NS4:2 * NS4], self.beta[:, 0:NS4])
        ctx.op("vector", fin2)

    def prep_subtile(self, t, s, with_outputs):
        ctx, g, T = self.ctx, self.g, self.g.TILE
        NS4 = g.NSUB * 4
        ctx.stage("kscale_t%d_s%d" % (t, s))
        for h in range(4):
            def sc_fn(e, s=s, h=h, NS4=NS4):
                ksl = self.k_tm[:, s * 1024 + h * 256: s * 1024 + h * 256 + 256]
                e.tensor_scalar(out=self.kb[:, h * 256:(h + 1) * 256], in0=ksl,
                                scalar1=self.nrm[:, 2 * NS4 + s * 4 + h:
                                                 2 * NS4 + s * 4 + h + 1],
                                scalar2=None, op0=ALU.mult)
                e.tensor_scalar(out=ksl, in0=ksl,
                                scalar1=self.nrm[:, NS4 + s * 4 + h:
                                                 NS4 + s * 4 + h + 1],
                                scalar2=None, op0=ALU.mult)
                vsl = self.v_tm[:, s * 1024 + h * 256: s * 1024 + h * 256 + 256]
                e.tensor_scalar(out=self.vb[:, h * 256:(h + 1) * 256], in0=vsl,
                                scalar1=self.beta[:, s * 4 + h: s * 4 + h + 1],
                                scalar2=None, op0=ALU.mult)
            ctx.op("vector", sc_fn)

        ctx.stage("invTclr_t%d_s%d" % (t, s))
        ctx.op("vector", lambda e: e.memset(self.invT[:], 0.0))
        for h in range(4):
            ctx.stage("prep_t%d_s%d_h%d" % (t, s, h))
            self.prep_sh(t, s, h, with_outputs)

        ctx.stage("u_t%d_s%d" % (t, s))
        um = {}
        for h in range(4):
            um[h] = ctx.run("tensor",
                            lambda e, h=h:
                            e.matmul(self.PB[4 + h][:, 0:256],
                                     self.invT[:, h * 128:h * 128 + 128],
                                     self.vb[:, h * 256:(h + 1) * 256],
                                     start=True, stop=True),
                            inc=self.mm)
        for h in range(4):
            ctx.run("vector",
                    lambda e, h=h, s=s:
                    e.tensor_copy(self.u_sb[:, s * 1024 + h * 256:
                                            s * 1024 + h * 256 + 256],
                                  self.PB[4 + h][:, 0:256]),
                    waits=[(self.mm, um[h])], inc=self.vv)

    def prep_sh(self, t, s, h, with_outputs):
        ctx, g, T = self.ctx, self.g, self.g.TILE
        NS4 = g.NSUB * 4
        C = self.C
        nwt = self.nwt
        A = nwt[:, 0:32]
        AT = nwt[:, 32:64]
        X = nwt[:, 64:96]
        Z = nwt[:, 96:128]
        SC = nwt[:, 128:160]

        for j in range(4):
            for dkc in range(2):
                fc = 2 * h + dkc
                col = fc * T + s * 128 + j * 32
                ctx.run("tensor",
                        lambda e, j=j, col=col, dkc=dkc:
                        e.matmul(self.PB[4 + j][32 * j:32 * j + 32, 0:32],
                                 self.k_fm[:, col:col + 32],
                                 self.k_fm[:, col:col + 32],
                                 start=(dkc == 0), stop=(dkc == 1),
                                 tile_position=(0, 32 * j)),
                        inc=self.mm)
        gmm = self.mm.total

        def a_build(e, s=s, h=h, NS4=NS4):
            for j in range(4):
                e.tensor_scalar(out=A[32 * j:32 * j + 32, :],
                                in0=self.PB[4 + j][32 * j:32 * j + 32, 0:32],
                                scalar1=self.nrm[32 * j:32 * j + 32,
                                                 2 * NS4 + s * 4 + h:
                                                 2 * NS4 + s * 4 + h + 1],
                                scalar2=None, op0=ALU.mult)
            ctx.lock(e, "vector",
                     e.tensor_scalar(out=SC, in0=C("ones"),
                                     scalar1=self.nrm[:, NS4 + s * 4 + h:
                                                      NS4 + s * 4 + h + 1],
                                     scalar2=None, op0=ALU.mult))
            ctx.lock(e, "vector", e.transpose(Z, SC))
            ctx.lock(e, "vector", e.tensor_mul(A, A, Z))
            ctx.lock(e, "vector", e.tensor_mul(A, A, C("trilS")))
            e.transpose(AT, A)
            ctx.lock(e, "vector", e.tensor_sub(X, C("eye1"), A))
            return e.engine_nop()
        ctx.run("vector", a_build, waits=[(self.mm, gmm)], inc=self.vv)

        for it in range(4):
            yv = ctx.run("vector", lambda e: e.transpose(SC, X), inc=self.vv)
            for j in range(4):
                ctx.run("tensor",
                        lambda e, j=j:
                        e.matmul(self.PB[4 + j][32 * j:32 * j + 32, 0:32],
                                 AT[32 * j:32 * j + 32, :],
                                 X[32 * j:32 * j + 32, :],
                                 start=True, stop=True,
                                 tile_position=(32 * j, 32 * j)),
                        waits=([(self.vv, yv)] if j == 0 else []),
                        inc=self.mm)
            pv = self.mm.total

            def z_build(e):
                ctx.lock(e, "vector", e.tensor_sub(Z, C("eye2"), X))
                for j in range(4):
                    e.tensor_sub(Z[32 * j:32 * j + 32, :],
                                 Z[32 * j:32 * j + 32, :],
                                 self.PB[4 + j][32 * j:32 * j + 32, 0:32])
                return e.engine_nop()
            zv = ctx.run("vector", z_build, waits=[(self.mm, pv)], inc=self.vv)

            for j in range(4):
                ctx.run("tensor",
                        lambda e, j=j:
                        e.matmul(self.PB[4 + j][32 * j:32 * j + 32, 0:32],
                                 SC[32 * j:32 * j + 32, :],
                                 Z[32 * j:32 * j + 32, :],
                                 start=True, stop=True,
                                 tile_position=(32 * j, 32 * j)),
                        waits=([(self.vv, zv)] if j == 0 else []),
                        inc=self.mm)
            xv = self.mm.total
            last = (it == 3)

            def x_build(e, last=last, h=h):
                for j in range(4):
                    ins = e.tensor_copy(X[32 * j:32 * j + 32, :],
                                        self.PB[4 + j][32 * j:32 * j + 32,
                                                       0:32])
                ctx.lock(e, "vector", ins)
                if last:
                    ctx.lock(e, "vector", e.transpose(SC, X))
                    for j in range(4):
                        ins = e.tensor_copy(
                            self.invT[32 * j:32 * j + 32,
                                      (h * 4 + j) * 32:(h * 4 + j) * 32 + 32],
                            SC[32 * j:32 * j + 32, :])
                    ctx.lock(e, "vector", ins)
                return e.engine_nop()
            ctx.run("vector", x_build, waits=[(self.mm, xv)], inc=self.vv)

        ctx.stage("prep_w_t%d_s%d_h%d" % (t, s, h))
        wm = {}
        for dkc in range(2):
            wm[dkc] = ctx.run("tensor",
                    lambda e, dkc=dkc, h=h:
                    e.matmul(self.PB[4 + dkc][:, 0:128],
                             self.kb[:, h * 256 + dkc * 128:
                                     h * 256 + dkc * 128 + 128],
                             self.invT[:, h * 128:h * 128 + 128],
                             start=True, stop=True),
                    inc=self.mm)
        for dkc in range(2):
            ctx.run("vector",
                    lambda e, s=s, h=h, dkc=dkc:
                    e.tensor_copy(
                        self.w_sb[:, (s * 4 + h) * 256 + dkc * 128:
                                  (s * 4 + h) * 256 + dkc * 128 + 128],
                        self.PB[4 + dkc][:, 0:128]),
                    waits=[(self.mm, wm[dkc])], inc=self.vv)

        if with_outputs:
            self.prep_attnT(t, s, h)

    def prep_attnT(self, t, s, h):
        ctx, g, T = self.ctx, self.g, self.g.TILE
        NS4 = g.NSUB * 4
        ctx.stage("prep_at_t%d_s%d_h%d" % (t, s, h))
        for j in range(4):
            for dkc in range(2):
                fc = 2 * h + dkc
                col = fc * T + s * 128 + j * 32
                ctx.run("tensor",
                        lambda e, j=j, col=col, dkc=dkc:
                        e.matmul(self.PB[4 + j][32 * j:32 * j + 32, 0:32],
                                 self.k_fm[:, col:col + 32],
                                 self.q_fm[:, col:col + 32],
                                 start=(dkc == 0), stop=(dkc == 1),
                                 tile_position=(0, 32 * j)),
                        inc=self.mm)

        def at_build(e, s=s, h=h, NS4=NS4):
            for j in range(4):
                ins = e.tensor_scalar(
                    out=self.at_sb[32 * j:32 * j + 32,
                                   (s * 4 + h) * 32:(s * 4 + h) * 32 + 32],
                    in0=self.PB[4 + j][32 * j:32 * j + 32, 0:32],
                    scalar1=self.nrm[32 * j:32 * j + 32,
                                     NS4 + s * 4 + h:NS4 + s * 4 + h + 1],
                    scalar2=None, op0=ALU.mult)
            ctx.lock(e, "vector", ins)
            at = self.at_sb[:, (s * 4 + h) * 32:(s * 4 + h) * 32 + 32]
            return e.tensor_mul(at, at, self.C("triuI"))
        ctx.run("vector", at_build, waits=[(self.mm, self.mm.total)],
                inc=self.vv)

    def scan_step(self, t, cj, with_outputs):
        ctx, g, T = self.ctx, self.g, self.g.TILE
        s, jr = cj // 4, cj % 4
        p0 = 32 * jr
        ctx.stage("scan_t%d_c%d" % (t, cj))

        for h in range(4):
            for dkc in range(2):
                ctx.run("tensor",
                        lambda e, h=h, dkc=dkc, s=s, jr=jr, p0=p0:
                        e.matmul(self.PB[4 + h][p0:p0 + 32, 0:256],
                                 self.w_sb[:, (s * 4 + h) * 256 + dkc * 128 +
                                           jr * 32:
                                           (s * 4 + h) * 256 + dkc * 128 +
                                           jr * 32 + 32],
                                 self.S[:, (h * 2 + dkc) * 256:
                                        (h * 2 + dkc) * 256 + 256],
                                 start=(dkc == 0), stop=(dkc == 1),
                                 tile_position=(0, p0)),
                        inc=self.mm)
        pwsv = self.mm.total

        def ua(e, s=s, p0=p0):
            for h in range(4):
                e.tensor_sub(self.uadj[p0:p0 + 32, h * 256:(h + 1) * 256],
                             self.u_sb[p0:p0 + 32,
                                       s * 1024 + h * 256:
                                       s * 1024 + h * 256 + 256],
                             self.PB[4 + h][p0:p0 + 32, 0:256])
            return e.engine_nop()
        uav = ctx.run("vector", ua, waits=[(self.mm, pwsv)], inc=self.vv)

        if with_outputs:
            self.scan_outputs(t, cj, uav)
            supd_w = [(self.vv, self.vv.total)]
        else:
            supd_w = [(self.vv, uav)]

        first = True
        for h in range(4):
            for dkc in range(2):
                ctx.run("tensor",
                        lambda e, h=h, dkc=dkc, s=s:
                        e.matmul(self.PB[h * 2 + dkc][:, 0:256],
                                 self.k_tm[:, s * 1024 + h * 256 + dkc * 128:
                                           s * 1024 + h * 256 + dkc * 128 +
                                           128],
                                 self.uadj[:, h * 256:(h + 1) * 256],
                                 start=True, stop=True),
                        waits=(supd_w if first else []),
                        inc=self.mm)
                first = False
        summ = self.mm.total

        def s_add(e):
            for b in range(8):
                ins = e.tensor_add(self.S[:, b * 256:(b + 1) * 256],
                                   self.S[:, b * 256:(b + 1) * 256],
                                   self.PB[b][:, 0:256])
            return ins
        ctx.run("vector", s_add, waits=[(self.mm, summ)], inc=self.vv)
        ctx.run("gpsimd", lambda e, p0=p0:
                e.memset(self.uadj[p0:p0 + 32, :], 0.0),
                waits=[(self.mm, summ)], inc=self.gg)
        if ctx.deferred:
            left = g.NCH - cj
            chunk = (len(ctx.deferred) + left - 1) // left
            for eng_, fn_ in ctx.deferred[:chunk]:
                ctx.op(eng_, fn_)
            ctx.deferred = ctx.deferred[chunk:]

    def scan_outputs(self, t, cj, uav):
        ctx, g, T = self.ctx, self.g, self.g.TILE
        s, jr = cj // 4, cj % 4
        p0 = 32 * jr
        NS4 = g.NSUB * 4
        q_fm = self.q_fm
        marks = {}
        for h in range(4):
            for dkc in range(2):
                fc = 2 * h + dkc
                col = fc * T + s * 128 + jr * 32
                ctx.run("tensor",
                        lambda e, h=h, dkc=dkc, col=col, p0=p0:
                        e.matmul(self.PB[4 + h][p0:p0 + 32, 0:256],
                                 q_fm[:, col:col + 32],
                                 self.S[:, (h * 2 + dkc) * 256:
                                        (h * 2 + dkc) * 256 + 256],
                                 start=(dkc == 0), stop=False,
                                 tile_position=(0, p0)),
                        waits=([(self.vv, uav)] if h == 0 and dkc == 0
                               else []),
                        inc=self.mm)
            marks[h] = ctx.run("tensor",
                    lambda e, h=h, s=s, p0=p0:
                    e.matmul(self.PB[4 + h][p0:p0 + 32, 0:256],
                             self.at_sb[:, (s * 4 + h) * 32:
                                        (s * 4 + h) * 32 + 32],
                             self.uadj[:, h * 256:(h + 1) * 256],
                             start=False, stop=True,
                             tile_position=(0, p0)),
                    inc=self.mm)
        for h in range(4):
            ctx.run("vector",
                    lambda e, h=h, s=s, p0=p0, NS4=NS4:
                    e.tensor_scalar(
                        out=self.delta_tm[p0:p0 + 32,
                                          s * 1024 + h * 256:
                                          s * 1024 + h * 256 + 256],
                        in0=self.PB[4 + h][p0:p0 + 32, 0:256],
                        scalar1=self.qnrm[p0:p0 + 32,
                                          NS4 + s * 4 + h:
                                          NS4 + s * 4 + h + 1],
                        scalar2=None, op0=ALU.mult),
                    waits=[(self.mm, marks[h])], inc=self.vv)

    def dump_dbg(self, t):
        ctx, g = self.ctx, self.g
        if not self.dbg:
            return
        ctx.stage("dbg_t%d" % t)
        for name in sorted(self.dbg):
            buf = self.ctx.t.get(name)
            d = ctx.dram.get("dbg_" + name)
            if buf is None or d is None:
                continue
            w = list(d.shape)[2]
            ctx.dma("sync", self.dmo, d.ap()[t, :, :], buf[:, 0:w])
        ctx.wait("sync", self.dmo)



def _l0_init_post(self):
    ctx = self.ctx
    ctx.dma("sync", self.dma, self.fir_w[:], ctx.dram["fir_img"].ap())
    ctx.dma("sync", self.dma, self.g2_sb[:], ctx.dram["g2_img"].ap())
    ctx.dma("sync", self.dma, self.g1b_sb[:], ctx.dram["g1b_img"].ap())
    ctx.dma("sync", self.dma, self.g2b_sb[0:20, 0:2], ctx.dram["g2b_img"].ap())
    ctx.dma("sync", self.dma, self.ltexp_sb[0:20, 0:1], ctx.dram["lt_img"].ap())
    ctx.dma("sync", self.dma, self.fsc_sb[:, 0:4], ctx.dram["flr_img"].ap())
    ctx.dma("sync", self.dma, self.onorm_sb[:], ctx.dram["onorm_img"].ap())
    ctx.wait("sync", self.dma)
    ctx.stage("init_post")
    ctx.op("scalar", lambda e: e.activation(
        out=self.ltexp_sb[0:20, 1:2], in_=self.ltexp_sb[0:20, 0:1],
        func=AF.Exp, scale=-1.0))
    ctx.op("scalar", lambda e: e.activation(
        out=self.fsc_sb[:, 4:8], in_=self.fsc_sb[:, 0:4], func=AF.Sigmoid))
    ctx.stage("init_post2")
    def fsc2(e):
        e.tensor_scalar(out=self.fsc_sb[:, 8:12], in0=self.fsc_sb[:, 4:8],
                        scalar1=-4.0, scalar2=1.0, op0=ALU.mult, op1=ALU.add)
        ctx.lock(e, "vector",
                 e.tensor_add(self.g2b_sb[0:20, 2:3], self.g2b_sb[0:20, 0:1],
                              self.g2b_sb[0:20, 1:2]))
        return e.tensor_scalar(out=self.g2b_sb[0:20, 2:3],
                               in0=self.g2b_sb[0:20, 2:3],
                               scalar1=self.ltexp_sb[0:20, 1:2],
                               scalar2=None, op0=ALU.mult)
    ctx.op("vector", fsc2)
    ctx.op("vector", lambda e: e.memset(self.gstat_fm[:], 0.0))
    ctx.op("vector", lambda e: e.memset(self.stat_tm[:, 60:64], 0.0))


def _l0_fir(self, t, defer=False):
    ctx, T, VS = self.ctx, self.g.TILE, self.VS
    if not defer:
        ctx.stage("fir_t%d" % t)
    for K, woff, out_fm, eng in ((3, 0, self.short_fm, "vector"),
                                 (31, 3, self.long_fm, "vector"),
                                 (64, 34, self.wide_fm, "vector")):
        for fc in range(8):
            for j in range(K):
                def c_fn(e, fc=fc, j=j, K=K, woff=woff, out_fm=out_fm):
                    src = self.v_fm[:, fc * VS + 64 - K + j:
                                    fc * VS + 64 - K + j + T]
                    dst = out_fm[:, fc * T: fc * T + T]
                    wj = self.fir_w[:, fc * 98 + woff + j:
                                    fc * 98 + woff + j + 1]
                    if j == 0:
                        e.tensor_scalar(out=dst, in0=src, scalar1=wj,
                                        scalar2=None, op0=ALU.mult)
                    else:
                        e.scalar_tensor_tensor(out=dst, in0=src, scalar=wj,
                                               in1=dst, op0=ALU.mult,
                                               op1=ALU.add)
                if defer:
                    ctx.deferred.append((eng, c_fn))
                else:
                    ctx.op(eng, c_fn)


def _l0_stats_gate_sub(self, t, s):
    ctx, g, T = self.ctx, self.g, self.g.TILE
    ctx.stage("slwtp_t%d_s%d" % (t, s))
    pairs = []
    for b, buf in enumerate((self.short_fm, self.long_fm, self.wide_fm)):
        for fc in range(8):
            pairs.append((buf[:, fc * T + s * 128:fc * T + s * 128 + 128],
                          self.slw_tm[:, b * 1024 + fc * 128:
                                      b * 1024 + (fc + 1) * 128]))
    self.transpose128(pairs)

    ctx.stage("stats_t%d_s%d" % (t, s))
    branches = [self.slw_tm[:, 0:1024], self.slw_tm[:, 1024:2048],
                self.slw_tm[:, 2048:3072],
                self.delta_tm[:, s * 1024:(s + 1) * 1024],
                self.v_tm[:, s * 1024:(s + 1) * 1024]]
    for b, buf in enumerate(branches):
        for h in range(4):
            sl = buf[:, h * 256:(h + 1) * 256]
            ctx.op("scalar", lambda e, sl=sl, b=b, h=h:
                   e.activation(out=self.scr[:, 0:256], in_=sl, func=AF.Copy,
                                accum_out=self.stat_tm[:, b * 12 + h:
                                                       b * 12 + h + 1]))
            ctx.op("scalar", lambda e, sl=sl, b=b, h=h:
                   e.activation(out=self.scr[:, 256:512], in_=sl,
                                func=AF.Square,
                                accum_out=self.stat_tm[:, b * 12 + 4 + h:
                                                       b * 12 + 4 + h + 1]))
            ctx.op("vector", lambda e, sl=sl, b=b, h=h:
                   e.reduce_max(self.stat_tm[:, b * 12 + 8 + h:
                                             b * 12 + 8 + h + 1],
                                sl, axis=AX.X, apply_absolute_value=True))
    ctx.stage("statsf_t%d_s%d" % (t, s))
    sv = self.stat_tm[:, 0:60].rearrange("p (b k) -> p b k", b=5)

    def fin(e):
        e.tensor_scalar(out=sv[:, :, 0:4], in0=sv[:, :, 0:4],
                        scalar1=self.C("s_inv256", 1), scalar2=None,
                        op0=ALU.mult)
        ctx.lock(e, "vector",
                 e.tensor_scalar(out=sv[:, :, 4:8], in0=sv[:, :, 4:8],
                                 scalar1=self.C("s_inv256", 1), scalar2=None,
                                 op0=ALU.mult))
        return e.tensor_scalar(out=sv[:, :, 4:8], in0=sv[:, :, 4:8],
                               scalar1=self.C("s_eps8", 1), scalar2=None,
                               op0=ALU.max)
    ctx.op("vector", fin)
    ctx.stage("statsf2_t%d_s%d" % (t, s))
    ctx.op("scalar", lambda e:
           e.activation(out=sv[:, :, 4:8], in_=sv[:, :, 4:8], func=AF.Sqrt))
    ctx.stage("gstat_t%d_s%d" % (t, s))
    mmv = ctx.run("tensor",
                  lambda e, s=s:
                  e.transpose(self.P47[0:64, 0:128], self.stat_tm[:, 0:64],
                              self.C("ident", 128)),
                  inc=self.mm)
    ctx.run("vector",
            lambda e, s=s:
            e.tensor_copy(self.gstat_fm[0:64, s * 128:s * 128 + 128],
                          self.P47[0:64, 0:128]),
            waits=[(self.mm, mmv)], inc=self.vv)


def _l0_gate(self, t):
    ctx, g, T = self.ctx, self.g, self.g.TILE
    NSUB = g.NSUB
    ctx.stage("g1w_t%d" % t)
    ctx.dma("sync", self.dma, self.wsl[:, 0:1152],
            ctx.dram["g1_img"].ap()[0, :, :])
    ctx.wait("sync", self.dma)
    for oc in range(16):
        slab = self.wsl[:, (oc % 2) * 1152:(oc % 2) * 1152 + 1152]
        ctx.stage("g1_t%d_o%d" % (t, oc))
        if oc < 15:
            ctx.dma("sync", self.dma,
                    self.wsl[:, ((oc + 1) % 2) * 1152:
                             ((oc + 1) % 2) * 1152 + 1152],
                    ctx.dram["g1_img"].ap()[oc + 1, :, :])
            ctx.wait("sync", self.dma)
        for s in range(NSUB):
            for kc in range(9):
                rhs = (self.x_fm[:, kc * T + s * 128:kc * T + s * 128 + 128]
                       if kc < 8 else
                       self.gstat_fm[:, s * 128:s * 128 + 128])
                ctx.run("tensor",
                        lambda e, kc=kc, s=s, slab=slab, rhs=rhs:
                        e.matmul(self.P03[:, (s % 2) * 512:(s % 2) * 512 + 128],
                                 slab[:, kc * 128:(kc + 1) * 128],
                                 rhs, start=(kc == 0), stop=(kc == 8)),
                        inc=self.mm)
            ctx.run("scalar",
                    lambda e, oc=oc, s=s:
                    e.activation(out=self.hdn_fm[:, s * 2048 + oc * 128:
                                                 s * 2048 + oc * 128 + 128],
                                 in_=self.P03[:, (s % 2) * 512:
                                              (s % 2) * 512 + 128],
                                 func=AF.Gelu,
                                 bias=self.g1b_sb[:, oc:oc + 1]),
                    waits=[(self.mm, self.mm.total)], inc=self.ss)
    for s in range(NSUB):
        ctx.stage("g2_t%d_s%d" % (t, s))
        for kc in range(16):
            ctx.run("tensor",
                    lambda e, kc=kc, s=s:
                    e.matmul(self.P47[0:20, 256:384],
                             self.g2_sb[:, kc * 20:(kc + 1) * 20],
                             self.hdn_fm[:, s * 2048 + kc * 128:
                                         s * 2048 + (kc + 1) * 128],
                             start=(kc == 0), stop=(kc == 15)),
                    inc=self.mm)
        ctx.run("scalar",
                lambda e:
                e.activation(out=self.scr[0:20, 0:128],
                             in_=self.P47[0:20, 256:384], func=AF.Identity,
                             scale=self.ltexp_sb[0:20, 1:2],
                             bias=self.g2b_sb[0:20, 2:3]),
                waits=[(self.mm, self.mm.total)], inc=self.ss)
        ctx.stage("g2tp_t%d_s%d" % (t, s))
        mmv = ctx.run("tensor",
                      lambda e:
                      e.transpose(self.P47[0:128, 512:532],
                                  self.scr[0:20, 0:128],
                                  self.C("ident", 128)[0:20, 0:20]),
                      inc=self.mm)
        ctx.run("vector",
                lambda e:
                e.tensor_copy(self.wsm[:, 0:20], self.P47[0:128, 512:532]),
                waits=[(self.mm, mmv)], inc=self.vv)
        ctx.stage("sm_t%d_s%d" % (t, s))
        for h in range(4):
            def sm1(e, h=h):
                return e.reduce_max(self.scr[:, 0:1],
                                    self.wsm[:, 5 * h:5 * h + 5],
                                    axis=AX.X, negate=True)
            ctx.run("vector", sm1, inc=self.vv)
            ctx.run("scalar",
                    lambda e, h=h:
                    e.activation(out=self.wsm[:, 5 * h:5 * h + 5],
                                 in_=self.wsm[:, 5 * h:5 * h + 5],
                                 func=AF.Exp, bias=self.scr[:, 0:1],
                                 accum_out=self.scr[:, 1:2]),
                    waits=[(self.vv, self.vv.total)], inc=self.ss)
            def sm3(e, h=h):
                ctx.lock(e, "vector",
                         e.reciprocal(self.scr[:, 2:3], self.scr[:, 1:2]))
                ctx.lock(e, "vector",
                         e.tensor_scalar(out=self.wsm[:, 5 * h:5 * h + 5],
                                         in0=self.wsm[:, 5 * h:5 * h + 5],
                                         scalar1=self.scr[:, 2:3],
                                         scalar2=None, op0=ALU.mult))
                ctx.lock(e, "vector",
                         e.tensor_scalar(out=self.wsm[:, 5 * h:5 * h + 5],
                                         in0=self.wsm[:, 5 * h:5 * h + 5],
                                         scalar1=self.fsc_sb[:, 8 + h:9 + h],
                                         scalar2=None, op0=ALU.mult))
                return e.tensor_scalar(out=self.wsm[:, 5 * h:5 * h + 4],
                                       in0=self.wsm[:, 5 * h:5 * h + 4],
                                       scalar1=self.fsc_sb[:, 4 + h:5 + h],
                                       scalar2=None, op0=ALU.add)
            ctx.run("vector", sm3, waits=[(self.ss, self.ss.total)],
                    inc=self.vv)
        self._l0_mix(t, s)


def _l0_mix(self, t, s):
    ctx, g = self.ctx, self.g
    ctx.stage("mix_t%d_s%d" % (t, s))
    branches = [self.slw_tm[:, 0:1024], self.slw_tm[:, 1024:2048],
                self.slw_tm[:, 2048:3072],
                self.delta_tm[:, s * 1024:(s + 1) * 1024],
                self.v_tm[:, s * 1024:(s + 1) * 1024]]
    for h in range(4):
        for b in range(5):
            def mx(e, h=h, b=b):
                src = branches[b][:, h * 256:(h + 1) * 256]
                dst = self.slw_tm[:, h * 256:(h + 1) * 256]
                wcol = self.wsm[:, 5 * h + b:5 * h + b + 1]
                if b == 0:
                    return e.tensor_scalar(out=dst, in0=src, scalar1=wcol,
                                           scalar2=None, op0=ALU.mult)
                return e.scalar_tensor_tensor(out=dst, in0=src, scalar=wcol,
                                              in1=dst, op0=ALU.mult,
                                              op1=ALU.add)
            ctx.op("vector", mx)
    ctx.stage("rms_t%d_s%d" % (t, s))
    for h in range(4):
        ctx.op("scalar", lambda e, h=h:
               e.activation(out=self.scr[:, 0:256],
                            in_=self.slw_tm[:, h * 256:(h + 1) * 256],
                            func=AF.Square,
                            accum_out=self.scr[:, 256 + h:257 + h]))
    ctx.op("scalar", lambda e:
           e.activation(out=self.scr[:, 260:264], in_=self.scr[:, 256:260],
                        func=AF.Sqrt, scale=1.0 / 256,
                        bias=self.C("s_eps5", 1)))
    ctx.stage("rms2_t%d_s%d" % (t, s))
    def rmsf(e):
        return e.reciprocal(self.scr[:, 260:264], self.scr[:, 260:264])
    ctx.op("vector", rmsf)
    ctx.stage("rms3_t%d_s%d" % (t, s))
    for h in range(4):
        ctx.op("vector", lambda e, h=h:
               e.scalar_tensor_tensor(
                   out=self.slw_tm[:, h * 256:(h + 1) * 256],
                   in0=self.slw_tm[:, h * 256:(h + 1) * 256],
                   scalar=self.scr[:, 260 + h:261 + h],
                   in1=self.onorm_sb[:, h * 256:(h + 1) * 256],
                   op0=ALU.mult, op1=ALU.mult))
    ctx.stage("otp_t%d_s%d" % (t, s))
    T = g.TILE
    self.transpose128(
        [(self.slw_tm[:, fc * 128:(fc + 1) * 128],
          self.q_fm[:, fc * T + s * 128:fc * T + s * 128 + 128])
         for fc in range(8)])


def _l0_out(self, t, x1_name="x1"):
    ctx, g, T = self.ctx, self.g, self.g.TILE
    for och in range(2):
        ctx.stage("wo_t%d_o%d" % (t, och))
        ctx.dma("sync", self.dma,
                self.wbuf[:, och * 4096:och * 4096 + 4096]
                .rearrange("p (kc n) -> p kc n", kc=8),
                ctx.dram["wo_img"].ap()[:, och * 512:och * 512 + 512]
                .rearrange("(kc p) n -> p kc n", p=128))
        ctx.wait("sync", self.dma)
        ctx.stage("wo_mm_t%d_o%d" % (t, och))
        for s in range(g.NSUB):
            for kc in range(8):
                ctx.run("tensor",
                        lambda e, s=s, kc=kc, och=och:
                        e.matmul(self.P03[:, (s % 2) * 512:(s % 2) * 512 + 512],
                                 self.q_fm[:, kc * T + s * 128:
                                           kc * T + s * 128 + 128],
                                 self.wbuf[:, och * 4096 + kc * 512:
                                           och * 4096 + (kc + 1) * 512],
                                 start=(kc == 0), stop=(kc == 7)),
                        inc=self.mm)
            ctx.run("vector",
                    lambda e, s=s, och=och:
                    e.tensor_add(self.x1_tm[:, s * 1024 + och * 512:
                                            s * 1024 + och * 512 + 512],
                                 self.P03[:, (s % 2) * 512:(s % 2) * 512 + 512],
                                 self.x_tm[:, s * 1024 + och * 512:
                                           s * 1024 + och * 512 + 512]),
                    waits=[(self.mm, self.mm.total)], inc=self.vv)
    for s in range(g.NSUB):
        ctx.stage("ln_t%d_s%d" % (t, s))
        ysl = self.x1_tm[:, s * 1024:(s + 1) * 1024]
        ctx.op("scalar", lambda e, ysl=ysl:
               e.activation(out=self.kb[:, 0:1024], in_=ysl, func=AF.Copy,
                            accum_out=self.scr[:, 270:271]))
        ctx.stage("ln2_t%d_s%d" % (t, s))
        ctx.op("vector", lambda e:
               e.tensor_scalar(out=self.scr[:, 271:272],
                               in0=self.scr[:, 270:271],
                               scalar1=self.C("s_inv1024", 1), scalar2=None,
                               op0=ALU.mult))
        ctx.stage("ln3_t%d_s%d" % (t, s))
        ctx.op("vector", lambda e, ysl=ysl:
               e.tensor_scalar(out=ysl, in0=ysl,
                               scalar1=self.scr[:, 271:272], scalar2=None,
                               op0=ALU.subtract))
        ctx.stage("ln4_t%d_s%d" % (t, s))
        ctx.op("scalar", lambda e, ysl=ysl:
               e.activation(out=self.kb[:, 0:1024], in_=ysl, func=AF.Square,
                            accum_out=self.scr[:, 272:273]))
        ctx.op("scalar", lambda e:
               e.activation(out=self.scr[:, 273:274], in_=self.scr[:, 272:273],
                            func=AF.Sqrt, scale=1.0 / 1024,
                            bias=self.C("s_eps5", 1)))
        ctx.stage("ln5_t%d_s%d" % (t, s))
        def lnf(e, ysl=ysl):
            ctx.lock(e, "vector",
                     e.reciprocal(self.scr[:, 273:274], self.scr[:, 273:274]))
            return e.tensor_scalar(out=ysl, in0=ysl,
                                   scalar1=self.scr[:, 273:274], scalar2=None,
                                   op0=ALU.mult)
        ctx.op("vector", lnf)
    ctx.stage("x1out_t%d" % t)
    for s in range(g.NSUB):
        ctx.dma("sync", self.dmo,
                ctx.dram[x1_name].ap()[t * T + s * 128:t * T + s * 128 + 128, :],
                self.x1_tm[:, s * 1024:(s + 1) * 1024])
    ctx.wait("sync", self.dmo)


def _l0_post(self, t):
    for s in range(self.g.NSUB):
        self._l0_stats_gate_sub(t, s)
    self._l0_gate(t)
    self._l0_out(t)


L0._l0_init_post = _l0_init_post
L0._l0_fir = _l0_fir
L0._l0_stats_gate_sub = _l0_stats_gate_sub
L0._l0_gate = _l0_gate
L0._l0_mix = _l0_mix
L0._l0_out = _l0_out
L0._l0_post = _l0_post



class FFN0:

    def __init__(self, ctx, coff, consts_w, reuse=None):
        self.ctx = ctx
        self.g = ctx.g
        self.ST = min(1024, self.g.TCORE)
        self.coff = coff
        r = reuse or {}
        def get(name, shape):
            if name in ctx.t:
                h = ctx.t[name]
                assert list(h.shape)[1] >= shape[1], (name, h.shape, shape)
                return h
            return ctx.sbuf(name, shape)
        self.consts = get("consts_sb", [128, consts_w])
        self.x1_fm = get("f_x1fm", [128, 8 * self.ST])
        self.slabs = [get("wbuf", [128, 8 * 1024]),
                      get("wbuf2", [128, 8 * 1024])]
        self.gh = get("f_gh", [128, 8 * self.ST])
        self.out = get("f_out", [128, (self.ST // 128) * 1024])
        self.gate = get("f_gate", [128, 8 * 8])
        self.cwb = get("f_cwb", [128, (self.ST // 128) * 8])
        self.prb = get("f_prb", [128, (self.ST // 128) * 8])
        self.xt = get("f_xt", [128, 1024])
        self.scr = get("scr", [128, 512])
        self.P03 = ctx.t["P03"]
        self.P47 = ctx.t["P47"]
        self.PB = self.P03.banks
        c = ctx.counter
        self.dma = c("c_dma", dma=True)
        self.dmo = c("c_dmo", dma=True)
        self.mm = c("c_mm")
        self.vv = c("c_vv")
        self.ss = c("c_ss")
        self._tpm = []

    def C(self, name, w=32):
        o = self.coff[name]
        return self.consts[:, o:o + w]

    def transpose_in(self, pairs):
        ctx = self.ctx
        ident = self.C("ident", 128)
        for (src, dst) in pairs:
            idx = len(self._tpm)
            bo = (idx % 4) * 512
            waits = [self._tpm[idx - 4]] if idx >= 4 else []
            mmv = ctx.run("tensor",
                          lambda e, src=src, bo=bo:
                          e.transpose(self.P47[:, bo:bo + 128], src, ident),
                          waits=waits, inc=self.mm)
            mark = ctx.run("vector",
                           lambda e, dst=dst, bo=bo:
                           e.tensor_copy(dst, self.P47[:, bo:bo + 128]),
                           waits=[(self.mm, mmv)], inc=self.vv)
            self._tpm.append((self.vv, mark))

    def build(self, x1_name="x1", x2_name="x2"):
        ctx, g = self.ctx, self.g
        ST = self.ST
        NSU = ST // 128
        NST = g.TCORE // ST
        ctx.stage("f_init")
        ctx.dma("sync", self.dma, self.gate[:], ctx.dram["gate8_img"].ap())
        ctx.wait("sync", self.dma)
        for st in range(NST):
            for s in range(NSU):
                ctx.stage("f_x_%d_%d" % (st, s))
                ctx.dma("sync", self.dma, self.xt[:],
                        ctx.dram[x1_name].ap()[st * ST + s * 128:
                                               st * ST + s * 128 + 128, :])
                ctx.wait("sync", self.dma)
                ctx.stage("f_xtp_%d_%d" % (st, s))
                self.transpose_in(
                    [(self.xt[:, fc * 128:(fc + 1) * 128],
                      self.x1_fm[:, fc * ST + s * 128:fc * ST + s * 128 + 128])
                     for fc in range(8)])
            ctx.stage("f_rt_%d" % st)
            rt_marks = []
            for s in range(NSU):
                for kc in range(8):
                    w = []
                    if kc == 0 and s >= 2:
                        w = [(self.vv, rt_marks[s - 2])]
                    ctx.run("tensor",
                            lambda e, s=s, kc=kc:
                            e.matmul(self.P03[:, (s % 2) * 512:(s % 2) * 512 + 8],
                                     self.x1_fm[:, kc * ST + s * 128:
                                                kc * ST + s * 128 + 128],
                                     self.gate[:, kc * 8:(kc + 1) * 8],
                                     start=(kc == 0), stop=(kc == 7)),
                            waits=w, inc=self.mm)
                rt_marks.append(ctx.run("vector",
                        lambda e, s=s:
                        e.tensor_copy(self.prb[:, s * 8:(s + 1) * 8],
                                      self.P03[:, (s % 2) * 512:
                                               (s % 2) * 512 + 8]),
                        waits=[(self.mm, self.mm.total)], inc=self.vv))
            ctx.stage("f_sm_%d" % st)
            for s in range(NSU):
                p = self.prb[:, s * 8:(s + 1) * 8]
                ctx.run("vector",
                        lambda e, p=p: e.reduce_max(self.scr[:, 0:1], p,
                                                    axis=AX.X, negate=True),
                        inc=self.vv)
                ctx.run("scalar",
                        lambda e, p=p:
                        e.activation(out=p, in_=p, func=AF.Exp,
                                     bias=self.scr[:, 0:1],
                                     accum_out=self.scr[:, 1:2]),
                        waits=[(self.vv, self.vv.total)], inc=self.ss)
                def smf(e, p=p, s=s):
                    ctx.lock(e, "vector",
                             e.reciprocal(self.scr[:, 2:3], self.scr[:, 1:2]))
                    ctx.lock(e, "vector",
                             e.tensor_scalar(out=p, in0=p,
                                             scalar1=self.scr[:, 2:3],
                                             scalar2=None, op0=ALU.mult))
                    cw = self.cwb[:, s * 8:(s + 1) * 8]
                    for ee in range(8):
                        if ee == 0:
                            ins = e.tensor_scalar(out=cw, in0=p,
                                                  scalar1=p[:, 0:1],
                                                  scalar2=None, op0=ALU.is_lt)
                        else:
                            ins = e.scalar_tensor_tensor(out=cw, in0=p,
                                                         scalar=p[:, ee:ee + 1],
                                                         in1=cw, op0=ALU.is_lt,
                                                         op1=ALU.add)
                        ctx.lock(e, "vector", ins)
                    ctx.lock(e, "vector",
                             e.tensor_scalar(out=cw, in0=cw, scalar1=2.0,
                                             scalar2=None, op0=ALU.is_lt))
                    return e.tensor_mul(cw, cw, p)
                ctx.run("vector", smf, waits=[(self.ss, self.ss.total)],
                        inc=self.vv)
            ctx.stage("f_clr_%d" % st)
            ctx.op("vector", lambda e: e.memset(self.out[:], 0.0))

            self._seq = []
            for ex in range(8):
                for half in range(2):
                    for mat in ("wg_img", "wu_img", "wd_img"):
                        self._seq.append(ctx.dram[mat].ap()[ex, half, :, :])
            self._li = 0
            ctx.stage("f_w0_%d" % st)
            self._issue_load()
            ctx.wait("sync", self.dma)
            for ex in range(8):
                for half in range(2):
                    self.expert_half(st, ex, half)

            for s in range(NSU):
                ctx.stage("f_res_%d_%d" % (st, s))
                ctx.dma("sync", self.dma, self.xt[:],
                        ctx.dram[x1_name].ap()[st * ST + s * 128:
                                               st * ST + s * 128 + 128, :])
                ctx.wait("sync", self.dma)
                ctx.stage("f_res2_%d_%d" % (st, s))
                ysl = self.out[:, s * 1024:(s + 1) * 1024]
                ctx.op("vector", lambda e, ysl=ysl:
                       e.tensor_add(ysl, ysl, self.xt[:, 0:1024]))
                ctx.stage("f_ln_%d_%d" % (st, s))
                ctx.op("scalar", lambda e, ysl=ysl:
                       e.activation(out=self.xt[:, 0:1024], in_=ysl,
                                    func=AF.Copy,
                                    accum_out=self.scr[:, 4:5]))
                ctx.stage("f_ln2_%d_%d" % (st, s))
                ctx.op("vector", lambda e, ysl=ysl:
                       e.tensor_scalar(out=self.scr[:, 5:6],
                                       in0=self.scr[:, 4:5],
                                       scalar1=self.C("s_inv1024", 1),
                                       scalar2=None, op0=ALU.mult))
                ctx.stage("f_ln3_%d_%d" % (st, s))
                ctx.op("vector", lambda e, ysl=ysl:
                       e.tensor_scalar(out=ysl, in0=ysl,
                                       scalar1=self.scr[:, 5:6], scalar2=None,
                                       op0=ALU.subtract))
                ctx.stage("f_ln4_%d_%d" % (st, s))
                ctx.op("scalar", lambda e, ysl=ysl:
                       e.activation(out=self.xt[:, 0:1024], in_=ysl,
                                    func=AF.Square,
                                    accum_out=self.scr[:, 6:7]))
                ctx.op("scalar", lambda e:
                       e.activation(out=self.scr[:, 7:8], in_=self.scr[:, 6:7],
                                    func=AF.Sqrt, scale=1.0 / 1024,
                                    bias=self.C("s_eps5", 1)))
                ctx.stage("f_ln5_%d_%d" % (st, s))
                def lnf(e, ysl=ysl):
                    ctx.lock(e, "vector",
                             e.reciprocal(self.scr[:, 7:8], self.scr[:, 7:8]))
                    return e.tensor_scalar(out=ysl, in0=ysl,
                                           scalar1=self.scr[:, 7:8],
                                           scalar2=None, op0=ALU.mult)
                ctx.op("vector", lnf)
            ctx.stage("f_out_%d" % st)
            for s in range(NSU):
                ctx.dma("sync", self.dmo,
                        ctx.dram[x2_name].ap()[st * ST + s * 128:
                                               st * ST + s * 128 + 128, :],
                        self.out[:, s * 1024:(s + 1) * 1024])
            ctx.wait("sync", self.dmo)

    def _issue_load(self):
        if self._li >= len(self._seq):
            return
        buf = self.slabs[self._li % 2]
        ap = self._seq[self._li]
        w = list(ap.shape)[-1]
        self.ctx.dma("sync", self.dma, buf[:, 0:w], ap)
        self._li += 1

    def expert_half(self, st, ex, half):
        ctx, g = self.ctx, self.g
        ST = self.ST
        NSU = ST // 128
        base = (ex * 2 + half) * 3
        sg = self.slabs[base % 2]
        su = self.slabs[(base + 1) % 2]
        sd = self.slabs[(base + 2) % 2]
        ctx.stage("f_g_%d_%d_%d" % (st, ex, half))
        self._issue_load()
        ctx.wait("sync", self.dma)
        g_marks = {}
        for oc8 in range(8):
            for nh in range(ST // 512):
                for kc in range(8):
                    w = []
                    if kc == 0 and oc8 > 0:
                        w = [(self.ss, g_marks[(oc8 - 1, nh)])]
                    ctx.run("tensor",
                            lambda e, oc8=oc8, nh=nh, kc=kc:
                            e.matmul(self.P03[:, (nh % 2) * 512:
                                              (nh % 2) * 512 + 512],
                                     sg[:, kc * 1024 + oc8 * 128:
                                               kc * 1024 + oc8 * 128 + 128],
                                     self.x1_fm[:, kc * ST + nh * 512:
                                                kc * ST + nh * 512 + 512],
                                     start=(kc == 0), stop=(kc == 7)),
                            waits=w, inc=self.mm)
                g_marks[(oc8, nh)] = ctx.run("scalar",
                        lambda e, oc8=oc8, nh=nh:
                        e.activation(out=self.gh[:, oc8 * ST + nh * 512:
                                                 oc8 * ST + nh * 512 + 512],
                                     in_=self.P03[:, (nh % 2) * 512:
                                                  (nh % 2) * 512 + 512],
                                     func=AF.Silu),
                        waits=[(self.mm, self.mm.total)], inc=self.ss)
        ctx.stage("f_u_%d_%d_%d" % (st, ex, half))
        self._issue_load()
        ctx.wait("sync", self.dma)
        u_marks = {}
        for oc8 in range(8):
            for nh in range(ST // 512):
                for kc in range(8):
                    w = []
                    if kc == 0 and oc8 > 0:
                        w = [(self.vv, u_marks[(oc8 - 1, nh)])]
                    ctx.run("tensor",
                            lambda e, oc8=oc8, nh=nh, kc=kc:
                            e.matmul(self.P03[:, (nh % 2) * 512:
                                              (nh % 2) * 512 + 512],
                                     su[:, kc * 1024 + oc8 * 128:
                                               kc * 1024 + oc8 * 128 + 128],
                                     self.x1_fm[:, kc * ST + nh * 512:
                                                kc * ST + nh * 512 + 512],
                                     start=(kc == 0), stop=(kc == 7)),
                            waits=w, inc=self.mm)
                u_marks[(oc8, nh)] = ctx.run("vector",
                        lambda e, oc8=oc8, nh=nh:
                        e.tensor_mul(self.gh[:, oc8 * ST + nh * 512:
                                             oc8 * ST + nh * 512 + 512],
                                     self.gh[:, oc8 * ST + nh * 512:
                                             oc8 * ST + nh * 512 + 512],
                                     self.P03[:, (nh % 2) * 512:
                                              (nh % 2) * 512 + 512]),
                        waits=[(self.mm, self.mm.total)], inc=self.vv)
        for s in range(NSU):
            ctx.stage("f_d_%d_%d_%d_%d" % (st, ex, half, s))
            if s == 0:
                self._issue_load()
                ctx.wait("sync", self.dma)
            for och in range(2):
                for oc8 in range(8):
                    ctx.run("tensor",
                            lambda e, s=s, och=och, oc8=oc8:
                            e.matmul(self.P47[:, 1024 + och * 512:
                                              1024 + och * 512 + 512],
                                     self.gh[:, oc8 * ST + s * 128:
                                             oc8 * ST + s * 128 + 128],
                                     sd[:, oc8 * 1024 + och * 512:
                                               oc8 * 1024 + och * 512 + 512],
                                     start=(oc8 == 0), stop=(oc8 == 7)),
                            inc=self.mm)
                ctx.run("vector",
                        lambda e, s=s, och=och, ex=ex:
                        e.scalar_tensor_tensor(
                            out=self.out[:, s * 1024 + och * 512:
                                         s * 1024 + och * 512 + 512],
                            in0=self.P47[:, 1024 + och * 512:
                                         1024 + och * 512 + 512],
                            scalar=self.cwb[:, s * 8 + ex:s * 8 + ex + 1],
                            in1=self.out[:, s * 1024 + och * 512:
                                         s * 1024 + och * 512 + 512],
                            op0=ALU.mult, op1=ALU.add),
                        waits=[(self.mm, self.mm.total)], inc=self.vv)



class L1A:

    def __init__(self, ctx, coff, consts_w):
        self.ctx = ctx
        self.g = ctx.g
        self.coff = coff
        self.T1 = 512
        T = self.T1
        def get(name, shape):
            return ctx.t[name] if name in ctx.t else ctx.sbuf(name, shape)
        self.consts = get("consts_sb", [128, consts_w])
        self.x_fm = get("a_xfm", [128, 8 * T])
        self.wbuf = get("wbuf", [128, 8 * 1024])
        self.xt = get("a_xt", [128, 1024])
        self.qkv_tm = get("a_qkv", [128, (T // 128) * 3 * 1024])
        self.kp_tm = get("a_kp", [128, 1024])
        self.bias = get("a_bias", [128, 3 * 1024])
        self.R_sb = get("a_R", [128, 64])
        self.sq_tm = get("a_sq", [128, 2 * 16])
        self.zrec = get("a_zrec", [128, 16])
        self.kv_sb = get("a_kvs", [64, 1024])
        self.scr = get("scr", [128, 512])
        self.P03 = ctx.t["P03"]
        self.P47 = ctx.t["P47"]
        self.PB = self.P03.banks
        c = ctx.counter
        self.dma = c("c_dma", dma=True)
        self.dmo = c("c_dmo", dma=True)
        self.mm = c("c_mm")
        self.vv = c("c_vv")
        self.ss = c("c_ss")
        self._tpm = []

    def C(self, name, w=32):
        o = self.coff[name]
        return self.consts[:, o:o + w]

    def transpose_in(self, pairs, banks=(0, 1)):
        ctx = self.ctx
        ident = self.C("ident", 128)
        nb = len(banks)
        for (src, dst) in pairs:
            idx = len(self._tpm)
            bo = banks[idx % nb] * 512
            waits = [self._tpm[idx - nb]] if idx >= nb else []
            mmv = ctx.run("tensor",
                          lambda e, src=src, bo=bo:
                          e.transpose(self.P47[:, bo:bo + 128], src, ident),
                          waits=waits, inc=self.mm)
            mark = ctx.run("vector",
                           lambda e, dst=dst, bo=bo:
                           e.tensor_copy(dst, self.P47[:, bo:bo + 128]),
                           waits=[(self.mm, mmv)], inc=self.vv)
            self._tpm.append((self.vv, mark))

    def build(self, x2_name="x2", q1_name="q1", kvp_name="kv_part"):
        ctx, g = self.ctx, self.g
        T = self.T1
        NSU = T // 128
        NT1 = g.TCORE // T
        ctx.stage("a_init")
        ctx.dma("sync", self.dma, self.bias[:], ctx.dram["bqkv_img"].ap())
        ctx.dma("sync", self.dma, self.R_sb[:], ctx.dram["r_img"].ap())
        ctx.wait("sync", self.dma)
        ctx.stage("a_init2")
        ctx.op("vector", lambda e: e.tensor_scalar(
            out=self.R_sb[:], in0=self.R_sb[:], scalar1=0.125, scalar2=None,
            op0=ALU.mult))
        ctx.op("vector", lambda e: e.memset(self.kv_sb[:], 0.0))

        kv_started = [False]
        for t in range(NT1):
            for s in range(NSU):
                ctx.stage("a_x_%d_%d" % (t, s))
                ctx.dma("sync", self.dma, self.xt[:],
                        ctx.dram[x2_name].ap()[t * T + s * 128:
                                               t * T + s * 128 + 128, :])
                ctx.wait("sync", self.dma)
                ctx.stage("a_xtp_%d_%d" % (t, s))
                self.transpose_in(
                    [(self.xt[:, fc * 128:(fc + 1) * 128],
                      self.x_fm[:, fc * T + s * 128:fc * T + s * 128 + 128])
                     for fc in range(8)])
            self.proj_tile(t)
            for s in range(NSU):
                self.subtile(t, s, kv_started, q1_name,
                             is_last=(t == NT1 - 1 and s == NSU - 1))
        ctx.stage("a_kvout")
        ctx.dma("sync", self.dmo, ctx.dram[kvp_name].ap(),
                self.kv_sb[0:64, 0:1024])
        ctx.wait("sync", self.dmo)

    def proj_tile(self, t):
        ctx, g, T = self.ctx, self.g, self.T1
        NSU = T // 128
        for pi, wname in enumerate(("wq1_img", "wk1_img", "wv1_img")):
            ctx.stage("a_w_%d_%s" % (t, wname))
            for och in range(2):
                ctx.dma("sync", self.dma,
                        self.wbuf[:, och * 4096:och * 4096 + 4096]
                        .rearrange("p (kc n) -> p kc n", kc=8),
                        ctx.dram[wname].ap()[:, och * 512:och * 512 + 512]
                        .rearrange("(kc p) n -> p kc n", p=128))
            ctx.wait("sync", self.dma)
            ctx.stage("a_p_%d_%d" % (t, pi))
            ev_marks = []
            for s in range(NSU):
                for och in range(2):
                    idx = s * 2 + och
                    w = [(self.vv, ev_marks[idx - 2])] if idx >= 2 else []
                    for kc in range(8):
                        ctx.run("tensor",
                                lambda e, och=och, kc=kc, s=s:
                                e.matmul(self.P03[:, och * 512:och * 512 + 512],
                                         self.x_fm[:, kc * T + s * 128:
                                                   kc * T + s * 128 + 128],
                                         self.wbuf[:, och * 4096 + kc * 512:
                                                   och * 4096 + (kc + 1) * 512],
                                         start=(kc == 0), stop=(kc == 7)),
                                waits=(w if kc == 0 else []), inc=self.mm)
                    ev_marks.append(ctx.run(
                        "vector",
                        lambda e, och=och, pi=pi, s=s:
                        e.tensor_add(
                            self.qkv_tm[:, s * 3072 + pi * 1024 + och * 512:
                                        s * 3072 + pi * 1024 + och * 512 + 512],
                            self.P03[:, och * 512:och * 512 + 512],
                            self.bias[:, pi * 1024 + och * 512:
                                      pi * 1024 + och * 512 + 512]),
                        waits=[(self.mm, self.mm.total)], inc=self.vv))

    def subtile(self, t, s, kv_started, q1_name, is_last=False):
        ctx, g, T = self.ctx, self.g, self.T1
        if "dbg_aqkv" in ctx.dram:
            ctx.stage("a_dbg_%d_%d" % (t, s))
            ctx.dma("sync", self.dmo, ctx.dram["dbg_aqkv"].ap()[s, :, :],
                    self.qkv_tm[:, s * 3072:(s + 1) * 3072])
            ctx.wait("sync", self.dmo)
        ctx.stage("a_sq_%d_%d" % (t, s))
        for pi in range(2):
            for h in range(16):
                ctx.op("scalar", lambda e, pi=pi, h=h:
                       e.activation(out=self.scr[:, 0:64],
                                    in_=self.qkv_tm[:, s * 3072 + pi * 1024 +
                                                    h * 64:
                                                    s * 3072 + pi * 1024 +
                                                    h * 64 + 64],
                                    func=AF.Square, scale=0.125,
                                    accum_out=self.sq_tm[:, pi * 16 + h:
                                                         pi * 16 + h + 1]))
        ctx.stage("a_sq2_%d_%d" % (t, s))
        ctx.op("vector", lambda e: e.tensor_scalar(
            out=self.sq_tm[:], in0=self.sq_tm[:], scalar1=-0.5, scalar2=None,
            op0=ALU.mult))
        for pi in range(2):
            dst = self.qkv_tm if pi == 0 else self.kp_tm
            dcol = 0
            for h in range(16):
                ctx.stage("a_fm_%d_%d_%d_%d" % (t, s, pi, h))
                mmv = ctx.run("tensor",
                              lambda e, pi=pi, h=h, s=s:
                              e.transpose(self.P47[0:64, 0:128],
                                          self.qkv_tm[:, s * 3072 + pi * 1024 +
                                                      h * 64:
                                                      s * 3072 + pi * 1024 +
                                                      h * 64 + 64],
                                          self.C("ident", 128)),
                              inc=self.mm)
                vv1 = ctx.run("vector",
                              lambda e:
                              e.tensor_copy(self.scr[0:64, 0:128],
                                            self.P47[0:64, 0:128]),
                              waits=[(self.mm, mmv)], inc=self.vv)
                mmv2 = ctx.run("tensor",
                               lambda e:
                               e.matmul(self.P47[:, 256:320],
                                        self.scr[0:64, 0:128],
                                        self.R_sb[0:64, 0:64],
                                        start=True, stop=True),
                               waits=[(self.vv, vv1),
                                      (self.ss, self.ss.total)],
                               inc=self.mm)
                if pi == 0:
                    ctx.run("scalar",
                            lambda e, h=h, s=s:
                            e.activation(out=self.qkv_tm[:, s * 3072 + h * 64:
                                                         s * 3072 + h * 64 + 64],
                                         in_=self.P47[:, 256:320], func=AF.Exp,
                                         bias=self.sq_tm[:, h:h + 1],
                                         accum_out=self.zrec[:, h:h + 1]),
                            waits=[(self.mm, mmv2)], inc=self.ss)
                else:
                    ctx.run("scalar",
                            lambda e, h=h:
                            e.activation(out=self.kp_tm[:, h * 64:h * 64 + 64],
                                         in_=self.P47[:, 256:320], func=AF.Exp,
                                         bias=self.sq_tm[:, 16 + h:16 + h + 1]),
                            waits=[(self.mm, mmv2)], inc=self.ss)
        ctx.stage("a_z_%d_%d" % (t, s))
        def zf(e):
            ctx.lock(e, "vector",
                     e.tensor_scalar(out=self.zrec[:], in0=self.zrec[:],
                                     scalar1=self.C("s_eps8", 1), scalar2=None,
                                     op0=ALU.add))
            return e.reciprocal(self.zrec[:], self.zrec[:])
        ctx.op("vector", zf)
        ctx.stage("a_qz_%d_%d" % (t, s))
        for h in range(16):
            ctx.op("vector", lambda e, h=h, s=s:
                   e.tensor_scalar(out=self.qkv_tm[:, s * 3072 + h * 64:
                                                   s * 3072 + h * 64 + 64],
                                   in0=self.qkv_tm[:, s * 3072 + h * 64:
                                                   s * 3072 + h * 64 + 64],
                                   scalar1=self.zrec[:, h:h + 1], scalar2=None,
                                   op0=ALU.mult))
        ctx.stage("a_kv_%d_%d" % (t, s))
        kv_marks = {}
        for h in range(16):
            bank = 6 + h % 2
            w = [kv_marks[h - 2]] if h >= 2 else []
            mmv = ctx.run("tensor",
                          lambda e, h=h, bank=bank, s=s:
                          e.matmul(self.PB[bank][0:64, 0:64],
                                   self.kp_tm[:, h * 64:h * 64 + 64],
                                   self.qkv_tm[:, s * 3072 + 2 * 1024 + h * 64:
                                               s * 3072 + 2 * 1024 + h * 64 +
                                               64],
                                   start=True, stop=True),
                          waits=w, inc=self.mm)
            kv_marks[h] = (self.vv, ctx.run(
                "vector",
                lambda e, h=h, bank=bank:
                e.tensor_add(self.kv_sb[0:64, h * 64:h * 64 + 64],
                             self.kv_sb[0:64, h * 64:h * 64 + 64],
                             self.PB[bank][0:64, 0:64]),
                waits=[(self.mm, mmv)], inc=self.vv))
        kv_started[0] = True
        ctx.stage("a_spill_%d_%d" % (t, s))
        T1 = self.T1
        ctx.dma("sync", self.dmo,
                ctx.dram["q1"].ap()[t * T1 + s * 128:t * T1 + s * 128 + 128, :],
                self.qkv_tm[:, s * 3072:s * 3072 + 1024])
        ctx.wait("sync", self.dmo)


class L1B:

    def __init__(self, ctx, coff, consts_w):
        self.ctx = ctx
        self.g = ctx.g
        self.coff = coff
        self.T1 = 512
        T = self.T1
        def get(name, shape):
            return ctx.t[name] if name in ctx.t else ctx.sbuf(name, shape)
        self.consts = get("consts_sb", [128, consts_w])
        self.wbuf = get("wbuf", [128, 8 * 1024])
        self.xt = get("b_xt", [128, 1024])
        self.att = get("b_att", [128, (T // 128) * 1024])
        self.afm = get("b_afm", [128, 8 * T])
        self.kv2 = get("b_kv2", [64, 1024])
        self.bo = get("b_bo", [128, 1024])
        self.scr = get("scr", [128, 512])
        self.P03 = ctx.t["P03"]
        self.P47 = ctx.t["P47"]
        self.PB = self.P03.banks
        c = ctx.counter
        self.dma = c("c_dma", dma=True)
        self.dmo = c("c_dmo", dma=True)
        self.mm = c("c_mm")
        self.vv = c("c_vv")
        self.ss = c("c_ss")

    def C(self, name, w=32):
        o = self.coff[name]
        return self.consts[:, o:o + w]

    def build(self, kvt_name="kv_tot", x2_name="x2", x3_name="x3"):
        ctx, g = self.ctx, self.g
        T = self.T1
        NSU = T // 128
        NT1 = g.TCORE // T
        ctx.stage("b_init")
        ctx.dma("sync", self.dma, self.kv2[:], ctx.dram[kvt_name].ap())
        ctx.dma("sync", self.dma, self.bo[:], ctx.dram["bo_img"].ap())
        ctx.wait("sync", self.dma)
        for t in range(NT1):
            for s in range(NSU):
                ctx.stage("b_q_%d_%d" % (t, s))
                ctx.dma("sync", self.dma, self.xt[:],
                        ctx.dram["q1"].ap()[t * T + s * 128:
                                            t * T + s * 128 + 128, :])
                ctx.wait("sync", self.dma)
                out_marks = {}
                for h in range(16):
                    ctx.stage("b_h_%d_%d_%d" % (t, s, h))
                    mmv = ctx.run("tensor",
                                  lambda e, h=h:
                                  e.transpose(self.P47[0:64, 0:128],
                                              self.xt[:, h * 64:h * 64 + 64],
                                              self.C("ident", 128)),
                                  inc=self.mm)
                    vv1 = ctx.run("vector",
                                  lambda e:
                                  e.tensor_copy(self.scr[0:64, 0:128],
                                                self.P47[0:64, 0:128]),
                                  waits=[(self.mm, mmv)], inc=self.vv)
                    w = [(self.vv, vv1)]
                    if h >= 4:
                        w.append(out_marks[h - 4])
                    mmo = ctx.run("tensor",
                            lambda e, h=h:
                            e.matmul(self.PB[h % 4][:, 0:64],
                                     self.scr[0:64, 0:128],
                                     self.kv2[0:64, h * 64:h * 64 + 64],
                                     start=True, stop=True),
                            waits=w, inc=self.mm)
                    out_marks[h] = (self.vv, ctx.run(
                        "vector",
                        lambda e, h=h, s=s:
                        e.tensor_copy(self.att[:, s * 1024 + h * 64:
                                               s * 1024 + h * 64 + 64],
                                      self.PB[h % 4][:, 0:64]),
                        waits=[(self.mm, mmo)], inc=self.vv))
            ctx.stage("b_tp_%d" % t)
            prs = []
            for s in range(NSU):
                for fc in range(8):
                    prs.append((self.att[:, s * 1024 + fc * 128:
                                         s * 1024 + (fc + 1) * 128],
                                self.afm[:, fc * T + s * 128:
                                         fc * T + s * 128 + 128]))
            marks = []
            for idx, (srcp, dstp) in enumerate(prs):
                bo_ = (idx % 4) * 512
                w = [marks[idx - 4]] if idx >= 4 else []
                mmv = ctx.run("tensor",
                              lambda e, srcp=srcp, bo_=bo_:
                              e.transpose(self.P47[:, bo_:bo_ + 128], srcp,
                                          self.C("ident", 128)),
                              waits=w, inc=self.mm)
                marks.append((self.vv, ctx.run(
                    "vector",
                    lambda e, dstp=dstp, bo_=bo_:
                    e.tensor_copy(dstp, self.P47[:, bo_:bo_ + 128]),
                    waits=[(self.mm, mmv)], inc=self.vv)))
                marks[-1] = (self.vv, marks[-1][1])
            for och in range(2):
                ctx.stage("b_wo_%d_%d" % (t, och))
                ctx.dma("sync", self.dma,
                        self.wbuf[:, och * 4096:och * 4096 + 4096]
                        .rearrange("p (kc n) -> p kc n", kc=8),
                        ctx.dram["wo1_img"].ap()[:, och * 512:och * 512 + 512]
                        .rearrange("(kc p) n -> p kc n", p=128))
                ctx.wait("sync", self.dma)
                ctx.stage("b_womm_%d_%d" % (t, och))
                wo_marks = []
                for s in range(NSU):
                    for kc in range(8):
                        w = []
                        if kc == 0 and s >= 2:
                            w = [(self.vv, wo_marks[s - 2])]
                        ctx.run("tensor",
                                lambda e, s=s, kc=kc, och=och:
                                e.matmul(self.P03[:, (s % 2) * 512:
                                                  (s % 2) * 512 + 512],
                                         self.afm[:, kc * T + s * 128:
                                                  kc * T + s * 128 + 128],
                                         self.wbuf[:, och * 4096 + kc * 512:
                                                   och * 4096 + (kc + 1) * 512],
                                         start=(kc == 0), stop=(kc == 7)),
                                waits=w, inc=self.mm)
                    wo_marks.append(ctx.run("vector",
                            lambda e, s=s, och=och:
                            e.tensor_add(self.att[:, s * 1024 + och * 512:
                                                 s * 1024 + och * 512 + 512],
                                         self.P03[:, (s % 2) * 512:
                                                  (s % 2) * 512 + 512],
                                         self.bo[:, och * 512:
                                                 och * 512 + 512]),
                            waits=[(self.mm, self.mm.total)], inc=self.vv))
            for s in range(NSU):
                ctx.stage("b_res_%d_%d" % (t, s))
                ctx.dma("sync", self.dma, self.xt[:],
                        ctx.dram[x2_name].ap()[t * T + s * 128:
                                               t * T + s * 128 + 128, :])
                ctx.wait("sync", self.dma)
                ctx.stage("b_res2_%d_%d" % (t, s))
                ysl = self.att[:, s * 1024:(s + 1) * 1024]
                ctx.op("vector", lambda e, ysl=ysl:
                       e.tensor_add(ysl, ysl, self.xt[:, 0:1024]))
                emit_ln(self, ysl, "b_ln_%d_%d" % (t, s))
            ctx.stage("b_out_%d" % t)
            for s in range(NSU):
                ctx.dma("sync", self.dmo,
                        ctx.dram[x3_name].ap()[t * T + s * 128:
                                               t * T + s * 128 + 128, :],
                        self.att[:, s * 1024:(s + 1) * 1024])
            ctx.wait("sync", self.dmo)


def emit_ln(ph, ysl, tag):
    ctx = ph.ctx
    ctx.stage(tag + "_a")
    ctx.op("scalar", lambda e: e.activation(out=ph.xt[:, 0:1024], in_=ysl,
                                            func=AF.Copy,
                                            accum_out=ph.scr[:, 4:5]))
    ctx.stage(tag + "_b")
    ctx.op("vector", lambda e: e.tensor_scalar(
        out=ph.scr[:, 5:6], in0=ph.scr[:, 4:5],
        scalar1=ph.C("s_inv1024", 1), scalar2=None, op0=ALU.mult))
    ctx.stage(tag + "_c")
    ctx.op("vector", lambda e: e.tensor_scalar(
        out=ysl, in0=ysl, scalar1=ph.scr[:, 5:6], scalar2=None,
        op0=ALU.subtract))
    ctx.stage(tag + "_d")
    ctx.op("scalar", lambda e: e.activation(out=ph.xt[:, 0:1024], in_=ysl,
                                            func=AF.Square,
                                            accum_out=ph.scr[:, 6:7]))
    ctx.op("scalar", lambda e: e.activation(
        out=ph.scr[:, 7:8], in_=ph.scr[:, 6:7], func=AF.Sqrt,
        scale=1.0 / 1024, bias=ph.C("s_eps5", 1)))
    ctx.stage(tag + "_e")
    def lnf(e):
        ctx.lock(e, "vector",
                 e.reciprocal(ph.scr[:, 7:8], ph.scr[:, 7:8]))
        return e.tensor_scalar(out=ysl, in0=ysl, scalar1=ph.scr[:, 7:8],
                               scalar2=None, op0=ALU.mult)
    ctx.op("vector", lnf)


class FFN1:

    def __init__(self, ctx, coff, consts_w):
        self.ctx = ctx
        self.g = ctx.g
        self.ST = min(1024, self.g.TCORE)
        self.coff = coff
        def get(name, shape):
            return ctx.t[name] if name in ctx.t else ctx.sbuf(name, shape)
        self.consts = get("consts_sb", [128, consts_w])
        self.x_fm = get("n_xfm", [128, 8 * self.ST])
        self.slabs = [get("wbuf", [128, 8 * 1024]),
                      get("wbuf2", [128, 8 * 1024])]
        self.gh = get("n_gh", [128, 4 * self.ST])
        self.out = get("n_out", [128, (self.ST // 128) * 1024])
        self.g16 = get("n_g16", [16, 1024])
        self.gwnT = get("n_gwnT", [128, 8 * 16])
        self.cwb = get("n_cwb", [128, (self.ST // 128) * 16])
        self.xt = get("n_xt", [128, 1024])
        self.scr = get("scr", [128, 512])
        self.P03 = ctx.t["P03"]
        self.P47 = ctx.t["P47"]
        self.PB = self.P03.banks
        c = ctx.counter
        self.dma = c("c_dma", dma=True)
        self.dmo = c("c_dmo", dma=True)
        self.mm = c("c_mm")
        self.vv = c("c_vv")
        self.ss = c("c_ss")
        self._tpm = []

    def C(self, name, w=32):
        o = self.coff[name]
        return self.consts[:, o:o + w]

    def transpose_in(self, pairs):
        ctx = self.ctx
        ident = self.C("ident", 128)
        for (src, dst) in pairs:
            idx = len(self._tpm)
            bo = (idx % 4) * 512
            waits = [self._tpm[idx - 4]] if idx >= 4 else []
            mmv = ctx.run("tensor",
                          lambda e, src=src, bo=bo:
                          e.transpose(self.P47[:, bo:bo + 128], src, ident),
                          waits=waits, inc=self.mm)
            mark = ctx.run("vector",
                           lambda e, dst=dst, bo=bo:
                           e.tensor_copy(dst, self.P47[:, bo:bo + 128]),
                           waits=[(self.mm, mmv)], inc=self.vv)
            self._tpm.append((self.vv, mark))

    def build(self, x3_name="x3", out_name="out"):
        ctx, g = self.ctx, self.g
        ST = self.ST
        NSU = ST // 128
        NST = g.TCORE // ST
        ctx.stage("n_init")
        ctx.dma("sync", self.dma, self.g16[:], ctx.dram["gate16_img"].ap())
        ctx.wait("sync", self.dma)
        ctx.stage("n_gwn")
        ctx.op("scalar", lambda e: e.activation(
            out=self.xt[0:16, 0:1024], in_=self.g16[:], func=AF.Square,
            accum_out=self.scr[0:16, 0:1]))
        ctx.stage("n_gwn2")
        ctx.op("scalar", lambda e: e.activation(
            out=self.scr[0:16, 1:2], in_=self.scr[0:16, 0:1], func=AF.Sqrt))
        ctx.stage("n_gwn3")
        def gwn3(e):
            ctx.lock(e, "vector",
                     e.tensor_scalar(out=self.scr[0:16, 1:2],
                                     in0=self.scr[0:16, 1:2],
                                     scalar1=self.C("s_eps12", 1)[0:16, :],
                                     scalar2=None, op0=ALU.max))
            ctx.lock(e, "vector",
                     e.reciprocal(self.scr[0:16, 1:2], self.scr[0:16, 1:2]))
            return e.tensor_scalar(out=self.g16[:], in0=self.g16[:],
                                   scalar1=self.scr[0:16, 1:2], scalar2=None,
                                   op0=ALU.mult)
        ctx.op("vector", gwn3)
        ctx.stage("n_gwn4")
        ev_marks = []
        for kc in range(8):
            bo = (kc % 4) * 512
            w = [(self.vv, ev_marks[kc - 4])] if kc >= 4 else []
            mmv = ctx.run("tensor",
                          lambda e, kc=kc, bo=bo:
                          e.transpose(self.P47[0:128, bo:bo + 16],
                                      self.g16[0:16, kc * 128:(kc + 1) * 128],
                                      self.C("ident", 128)[0:16, 0:16]),
                          waits=w, inc=self.mm)
            ev_marks.append(ctx.run(
                "vector",
                lambda e, kc=kc, bo=bo:
                e.tensor_copy(self.gwnT[:, kc * 16:(kc + 1) * 16],
                              self.P47[0:128, bo:bo + 16]),
                waits=[(self.mm, mmv)], inc=self.vv))

        if "dbg_ngwnT" in ctx.dram:
            ctx.stage("n_dbgg")
            ctx.dma("sync", self.dmo, ctx.dram["dbg_ngwnT"].ap(),
                    self.gwnT[:])
            ctx.wait("sync", self.dmo)
        for st in range(NST):
            for s in range(NSU):
                ctx.stage("n_x_%d_%d" % (st, s))
                ctx.dma("sync", self.dma, self.xt[:],
                        ctx.dram[x3_name].ap()[st * ST + s * 128:
                                               st * ST + s * 128 + 128, :])
                ctx.wait("sync", self.dma)
                ctx.stage("n_xtp_%d_%d" % (st, s))
                self.transpose_in(
                    [(self.xt[:, fc * 128:(fc + 1) * 128],
                      self.x_fm[:, fc * ST + s * 128:fc * ST + s * 128 + 128])
                     for fc in range(8)])
                ctx.stage("n_xn_%d_%d" % (st, s))
                ctx.run("scalar",
                        lambda e, s=s:
                        e.activation(out=self.xt[:, 0:1024],
                                     in_=self.xt[:, 0:1024], func=AF.Square,
                                     accum_out=self.scr[:, 8:9]),
                        inc=self.ss)
                ctx.stage("n_xn2_%d_%d" % (st, s))
                ctx.op("scalar", lambda e: e.activation(
                    out=self.scr[:, 9:10], in_=self.scr[:, 8:9], func=AF.Sqrt))
                ctx.stage("n_xn3_%d_%d" % (st, s))
                def xn3(e, s=s):
                    ctx.lock(e, "vector",
                             e.tensor_scalar(out=self.scr[:, 9:10],
                                             in0=self.scr[:, 9:10],
                                             scalar1=self.C("s_eps12", 1),
                                             scalar2=None, op0=ALU.max))
                    return e.reciprocal(self.scr[:, 40 + s:41 + s],
                                        self.scr[:, 9:10])
                ctx.op("vector", xn3)
            if "dbg_nxfm" in ctx.dram:
                ctx.stage("n_dbgx_%d" % st)
                ctx.dma("sync", self.dmo, ctx.dram["dbg_nxfm"].ap()[st, :, :],
                        self.x_fm[:])
                ctx.wait("sync", self.dmo)
            ctx.stage("n_rt_%d" % st)
            rt_marks = []
            for s in range(NSU):
                for kc in range(8):
                    w = []
                    if kc == 0 and s >= 2:
                        w = [(self.vv, rt_marks[s - 2])]
                    ctx.run("tensor",
                            lambda e, s=s, kc=kc:
                            e.matmul(self.P03[:, (s % 2) * 512:
                                              (s % 2) * 512 + 16],
                                     self.x_fm[:, kc * ST + s * 128:
                                                kc * ST + s * 128 + 128],
                                     self.gwnT[:, kc * 16:(kc + 1) * 16],
                                     start=(kc == 0), stop=(kc == 7)),
                            waits=w, inc=self.mm)
                def rtev(e, s=s):
                    if "dbg_nlg" in self.ctx.dram:
                        e.tensor_copy(self.gh[:, s * 16:(s + 1) * 16],
                                      self.P03[:, (s % 2) * 512:
                                               (s % 2) * 512 + 16])
                    e.tensor_scalar(out=self.cwb[:, s * 16:(s + 1) * 16],
                                    in0=self.P03[:, (s % 2) * 512:
                                                 (s % 2) * 512 + 16],
                                    scalar1=self.scr[:, 40 + s:41 + s],
                                    scalar2=None, op0=ALU.mult)
                    return e.engine_nop()
                rt_marks.append(ctx.run("vector", rtev,
                                        waits=[(self.mm, self.mm.total)],
                                        inc=self.vv))
            ctx.stage("n_sig_%d" % st)
            for s in range(NSU):
                ctx.op("scalar", lambda e, s=s: e.activation(
                    out=self.cwb[:, s * 16:(s + 1) * 16],
                    in_=self.cwb[:, s * 16:(s + 1) * 16], func=AF.Sigmoid))
            if "dbg_nlg" in ctx.dram:
                ctx.stage("n_dbgl_%d" % st)
                ctx.dma("sync", self.dmo, ctx.dram["dbg_nlg"].ap()[st, :, :],
                        self.gh[:, 0:128])
                ctx.wait("sync", self.dmo)
            if "dbg_nsc" in ctx.dram:
                ctx.stage("n_dbgs_%d" % st)
                ctx.dma("sync", self.dmo, ctx.dram["dbg_nsc"].ap()[st, :, :],
                        self.cwb[:])
                ctx.wait("sync", self.dmo)
            ctx.stage("n_topk_%d" % st)
            for s in range(NSU):
                def topk(e, s=s):
                    p = self.cwb[:, s * 16:(s + 1) * 16]
                    cnt = self.scr[:, 16:32]
                    for ee in range(16):
                        if ee == 0:
                            ins = e.tensor_scalar(out=cnt, in0=p,
                                                  scalar1=p[:, 0:1],
                                                  scalar2=None, op0=ALU.is_lt)
                        else:
                            ins = e.scalar_tensor_tensor(out=cnt, in0=p,
                                                         scalar=p[:, ee:ee + 1],
                                                         in1=cnt,
                                                         op0=ALU.is_lt,
                                                         op1=ALU.add)
                        ctx.lock(e, "vector", ins)
                    ctx.lock(e, "vector",
                             e.tensor_scalar(out=cnt, in0=cnt, scalar1=8.0,
                                             scalar2=None, op0=ALU.is_lt))
                    ctx.lock(e, "vector", e.tensor_mul(p, p, cnt))
                    ctx.lock(e, "vector",
                             e.reduce_sum(self.scr[:, 32:33], p, axis=AX.X))
                    ctx.lock(e, "vector",
                             e.reciprocal(self.scr[:, 33:34],
                                          self.scr[:, 32:33]))
                    return e.tensor_scalar(out=p, in0=p,
                                           scalar1=self.scr[:, 33:34],
                                           scalar2=None, op0=ALU.mult)
                ctx.op("vector", topk)
            if "dbg_ncwb" in ctx.dram:
                ctx.stage("n_dbgc_%d" % st)
                ctx.dma("sync", self.dmo, ctx.dram["dbg_ncwb"].ap()[st, :, :],
                        self.cwb[:])
                ctx.wait("sync", self.dmo)
            ctx.stage("n_clr_%d" % st)
            ctx.op("vector", lambda e: e.memset(self.out[:], 0.0))
            self._seq = []
            for ex in range(17):
                if ex == 16:
                    self._seq += [ctx.dram["swg_img"].ap(),
                                  ctx.dram["swu_img"].ap(),
                                  ctx.dram["swd_img"].ap()]
                else:
                    self._seq += [ctx.dram["wg1_img"].ap()[ex, :, :],
                                  ctx.dram["wu1_img"].ap()[ex, :, :],
                                  ctx.dram["wd1_img"].ap()[ex, :, :]]
            self._li = 0
            ctx.stage("n_w0_%d" % st)
            self._issue_load()
            ctx.wait("sync", self.dma)
            for ex in range(17):
                self.expert(st, ex)
            for s in range(NSU):
                ctx.stage("n_res_%d_%d" % (st, s))
                ctx.dma("sync", self.dma, self.xt[:],
                        ctx.dram[x3_name].ap()[st * ST + s * 128:
                                               st * ST + s * 128 + 128, :])
                ctx.wait("sync", self.dma)
                ctx.stage("n_res2_%d_%d" % (st, s))
                ysl = self.out[:, s * 1024:(s + 1) * 1024]
                ctx.op("vector", lambda e, ysl=ysl:
                       e.tensor_add(ysl, ysl, self.xt[:, 0:1024]))
                emit_ln(self, ysl, "n_ln_%d_%d" % (st, s))
            ctx.stage("n_out_%d" % st)
            for s in range(NSU):
                ctx.dma("sync", self.dmo,
                        ctx.dram[out_name].ap()[st * ST + s * 128:
                                                st * ST + s * 128 + 128, :],
                        self.out[:, s * 1024:(s + 1) * 1024])
            ctx.wait("sync", self.dmo)

    def expert(self, st, ex):
        ctx, g = self.ctx, self.g
        ST = self.ST
        NSU = ST // 128
        shared = (ex == 16)
        base = ex * 3
        sg = self.slabs[base % 2]
        su = self.slabs[(base + 1) % 2]
        sd = self.slabs[(base + 2) % 2]
        ctx.stage("n_g_%d_%d" % (st, ex))
        self._issue_load()
        ctx.wait("sync", self.dma)
        g_marks = {}
        for oc4 in range(4):
            for nh in range(ST // 512):
                for kc in range(8):
                    w = []
                    if kc == 0 and oc4 > 0:
                        w = [(self.ss, g_marks[(oc4 - 1, nh)])]
                    ctx.run("tensor",
                            lambda e, oc4=oc4, nh=nh, kc=kc:
                            e.matmul(self.P03[:, (nh % 2) * 512:
                                              (nh % 2) * 512 + 512],
                                     sg[:, kc * 512 + oc4 * 128:
                                               kc * 512 + oc4 * 128 + 128],
                                     self.x_fm[:, kc * ST + nh * 512:
                                                kc * ST + nh * 512 + 512],
                                     start=(kc == 0), stop=(kc == 7)),
                            waits=w, inc=self.mm)
                g_marks[(oc4, nh)] = ctx.run(
                    "scalar",
                    lambda e, oc4=oc4, nh=nh:
                    e.activation(out=self.gh[:, oc4 * ST + nh * 512:
                                             oc4 * ST + nh * 512 + 512],
                                 in_=self.P03[:, (nh % 2) * 512:
                                              (nh % 2) * 512 + 512],
                                 func=AF.Silu),
                    waits=[(self.mm, self.mm.total)], inc=self.ss)
        ctx.stage("n_u_%d_%d" % (st, ex))
        self._issue_load()
        ctx.wait("sync", self.dma)
        u_marks = {}
        for oc4 in range(4):
            for nh in range(ST // 512):
                for kc in range(8):
                    w = []
                    if kc == 0 and oc4 > 0:
                        w = [(self.vv, u_marks[(oc4 - 1, nh)])]
                    ctx.run("tensor",
                            lambda e, oc4=oc4, nh=nh, kc=kc:
                            e.matmul(self.P03[:, (nh % 2) * 512:
                                              (nh % 2) * 512 + 512],
                                     su[:, kc * 512 + oc4 * 128:
                                               kc * 512 + oc4 * 128 + 128],
                                     self.x_fm[:, kc * ST + nh * 512:
                                                kc * ST + nh * 512 + 512],
                                     start=(kc == 0), stop=(kc == 7)),
                            waits=w, inc=self.mm)
                u_marks[(oc4, nh)] = ctx.run(
                    "vector",
                    lambda e, oc4=oc4, nh=nh:
                    e.tensor_mul(self.gh[:, oc4 * ST + nh * 512:
                                         oc4 * ST + nh * 512 + 512],
                                 self.gh[:, oc4 * ST + nh * 512:
                                         oc4 * ST + nh * 512 + 512],
                                 self.P03[:, (nh % 2) * 512:
                                              (nh % 2) * 512 + 512]),
                    waits=[(self.mm, self.mm.total)], inc=self.vv)
        for s in range(NSU):
            ctx.stage("n_d_%d_%d_%d" % (st, ex, s))
            if s == 0:
                self._issue_load()
                ctx.wait("sync", self.dma)
            for och in range(2):
                for oc4 in range(4):
                    ctx.run("tensor",
                            lambda e, s=s, och=och, oc4=oc4:
                            e.matmul(self.P47[:, 1024 + och * 512:
                                              1024 + och * 512 + 512],
                                     self.gh[:, oc4 * ST + s * 128:
                                             oc4 * ST + s * 128 + 128],
                                     sd[:, oc4 * 1024 + och * 512:
                                               oc4 * 1024 + och * 512 + 512],
                                     start=(oc4 == 0), stop=(oc4 == 3)),
                            inc=self.mm)
                def dev(e, s=s, och=och, ex=ex, shared=shared):
                    o = self.out[:, s * 1024 + och * 512:
                                 s * 1024 + och * 512 + 512]
                    if shared:
                        return e.tensor_add(o, o,
                                            self.P47[:, 1024 + och * 512:
                                                     1024 + och * 512 + 512])
                    return e.scalar_tensor_tensor(
                        out=o, in0=self.P47[:, 1024 + och * 512:
                                            1024 + och * 512 + 512],
                        scalar=self.cwb[:, s * 16 + ex:s * 16 + ex + 1],
                        in1=o, op0=ALU.mult, op1=ALU.add)
                ctx.run("vector", dev, waits=[(self.mm, self.mm.total)],
                        inc=self.vv)


FFN1._issue_load = FFN0._issue_load



RG_PAIRS = [[0, 1], [2, 3], [4, 5], [6, 7]]



def g1_image(g1_w):
    g1T = np.zeros((1152, 2048), np.float32)
    g1T[:1084] = np.asarray(g1_w).T
    img = np.zeros((16, 128, 1152), np.float32)
    for oc in range(16):
        for kc in range(9):
            img[oc, :, kc * 128:(kc + 1) * 128] = \
                g1T[kc * 128:(kc + 1) * 128, oc * 128:(oc + 1) * 128]
    return img


def g2_image(g2_w):
    g2T = np.asarray(g2_w).T
    img = np.zeros((128, 16 * 20), np.float32)
    for kc in range(16):
        img[:, kc * 20:(kc + 1) * 20] = g2T[kc * 128:(kc + 1) * 128, :]
    return img


def fir_image(fs, fl, fw):
    f3 = np.asarray(fs).reshape(1024, 3)
    f31 = np.asarray(fl).reshape(1024, 31)
    f64 = np.asarray(fw).reshape(1024, 64)
    img = np.zeros((128, 8 * 98), np.float32)
    for fc in range(8):
        rows = slice(fc * 128, (fc + 1) * 128)
        img[:, fc * 98 + 0:fc * 98 + 3] = f3[rows]
        img[:, fc * 98 + 3:fc * 98 + 34] = f31[rows]
        img[:, fc * 98 + 34:fc * 98 + 98] = f64[rows]
    return img


def wgu_image(w):
    w = np.asarray(w)
    img = np.zeros((8, 2, 128, 8192), np.float32)
    for e in range(8):
        for half in range(2):
            for kc in range(8):
                for oc8 in range(8):
                    img[e, half, :, kc * 1024 + oc8 * 128:
                        kc * 1024 + oc8 * 128 + 128] = \
                        w[e][kc * 128:(kc + 1) * 128,
                             (half * 8 + oc8) * 128:(half * 8 + oc8) * 128 + 128]
    return img


def wd_image(w):
    w = np.asarray(w)
    img = np.zeros((8, 2, 128, 8192), np.float32)
    for e in range(8):
        for half in range(2):
            for oc8 in range(8):
                img[e, half, :, oc8 * 1024:(oc8 + 1) * 1024] = \
                    w[e][(half * 8 + oc8) * 128:(half * 8 + oc8) * 128 + 128, :]
    return img


def wgu1_image(w):
    w = np.asarray(w)
    img = np.zeros((16, 128, 4096), np.float32)
    for e in range(16):
        for kc in range(8):
            img[e, :, kc * 512:(kc + 1) * 512] = w[e][kc * 128:(kc + 1) * 128]
    return img


def wd1_image(w):
    w = np.asarray(w)
    img = np.zeros((16, 128, 4096), np.float32)
    for e in range(16):
        for oc4 in range(4):
            img[e, :, oc4 * 1024:(oc4 + 1) * 1024] = \
                w[e][oc4 * 128:(oc4 + 1) * 128]
    return img


def sw_image(w, down=False):
    w = np.asarray(w)
    img = np.zeros((128, 4096), np.float32)
    if down:
        for oc4 in range(4):
            img[:, oc4 * 1024:(oc4 + 1) * 1024] = w[oc4 * 128:(oc4 + 1) * 128]
    else:
        for kc in range(8):
            img[:, kc * 512:(kc + 1) * 512] = w[kc * 128:(kc + 1) * 128]
    return img


def build_images(I):
    cimg, coff = consts_image()
    vb20 = np.zeros(20, np.float32)
    vb20[4::5] = np.asarray(I["L0_attn_value_bias"])
    imgs = {
        "consts_img": cimg,
        "wq_img": lhsT_of(I["L0_attn_q_w"]),
        "wk_img": lhsT_of(I["L0_attn_k_w"]),
        "wv_img": lhsT_of(I["L0_attn_v_w"]),
        "bw_img": conv_w_layout(lhsT_of(I["L0_attn_b_w"])),
        "wconv_img": np.concatenate(
            [conv_w_layout(I["L0_attn_convq_w"]),
             conv_w_layout(I["L0_attn_convk_w"]),
             conv_w_layout(I["L0_attn_convv_w"])], axis=1),
        "fir_img": fir_image(I["L0_attn_fir_s"], I["L0_attn_fir_l"],
                             I["L0_attn_fir_wd"]),
        "g1_img": g1_image(I["L0_attn_g1_w"]),
        "g2_img": g2_image(I["L0_attn_g2_w"]),
        "g1b_img": np.ascontiguousarray(
            np.asarray(I["L0_attn_g1_b"]).reshape(16, 128).T),
        "g2b_img": np.ascontiguousarray(
            np.stack([np.asarray(I["L0_attn_g2_b"]), vb20], 1)),
        "lt_img": np.ascontiguousarray(
            np.repeat(np.asarray(I["L0_attn_log_temp"]), 5).reshape(20, 1)),
        "flr_img": np.ascontiguousarray(
            np.tile(np.asarray(I["L0_attn_floor_logit"]), (128, 1))),
        "onorm_img": np.ascontiguousarray(
            np.tile(np.asarray(I["L0_attn_onorm_w"]), (128, 4))),
        "wo_img": lhsT_of(I["L0_attn_o_w"]),
        "gate8_img": conv_w_layout(lhsT_of(I["L0_ffn_gate_w"])),
        "wg_img": wgu_image(I["L0_ffn_wg"]),
        "wu_img": wgu_image(I["L0_ffn_wu"]),
        "wd_img": wd_image(I["L0_ffn_wd"]),
        "wq1_img": lhsT_of(I["L1_attn_Wq"]),
        "wk1_img": lhsT_of(I["L1_attn_Wk"]),
        "wv1_img": lhsT_of(I["L1_attn_Wv"]),
        "wo1_img": lhsT_of(I["L1_attn_Wo"]),
        "bqkv_img": np.concatenate(
            [np.tile(np.asarray(I["L1_attn_b" + n]), (128, 1))
             for n in ("q", "k", "v")], 1).astype(np.float32),
        "bo_img": np.ascontiguousarray(
            np.tile(np.asarray(I["L1_attn_bo"]), (128, 1))).astype(np.float32),
        "r_img": np.concatenate([np.asarray(I["L1_attn_R"], np.float32),
                                 np.zeros((64, 64), np.float32)], 0),
        "gate16_img": np.asarray(I["L1_ffn_gate_w"], np.float32),
        "wg1_img": wgu1_image(I["L1_ffn_w_gate"]),
        "wu1_img": wgu1_image(I["L1_ffn_w_up"]),
        "wd1_img": wd1_image(I["L1_ffn_w_down"]),
        "swg_img": sw_image(I["L1_ffn_sw_gate"]),
        "swu_img": sw_image(I["L1_ffn_sw_up"]),
        "swd_img": sw_image(I["L1_ffn_sw_down"], down=True),
    }
    return imgs, coff, cimg.shape[1]



def build_program(geo, coff, consts_w, img_shapes, l0_tile=128,
                  rg=None):
    if rg is None:
        rg = RG_PAIRS
    nc = bass.Bass("TRN2", target_bir_lowering=False,
                   detect_race_conditions=False)
    ctx = Ctx(nc, geo)
    D = 1024
    ctx.din("x", [geo.TCORE, D])
    ctx.din("x_halo", [128, D])
    ctx.din("maskeo", [128, 2])
    for name, shp in img_shapes.items():
        ctx.din(name, list(shp))
    for name in ("x1", "x2", "q1", "x3"):
        ctx.dram[name] = nc.dram_tensor(name, [geo.TCORE, D], F32,
                                        kind="Internal")
    ctx.dram["cS_in"] = nc.dram_tensor("cS_in", [128, 2048], F32,
                                       kind="Internal")
    ctx.dram["cS_out"] = nc.dram_tensor("cS_out", [128, 2048], F32,
                                        kind="Internal")
    ctx.dram["kv_part"] = nc.dram_tensor("kv_part", [64, 1024], F32,
                                         kind="Internal")
    ctx.dram["kv_tot"] = nc.dram_tensor("kv_tot", [64, 1024], F32,
                                        kind="Internal")
    ctx.dout("out", [geo.TCORE, D])

    ctx.alloc_shared(consts_w)
    maskeo = ctx.sbuf("maskeo_sb", [128, 2])
    boot_dma = ctx.counter("c_dma", dma=True)
    cc = ctx.counter("c_cc")
    ctx.stage("boot")
    ctx.dma("sync", boot_dma, ctx.t["consts_sb"][:],
            ctx.dram["consts_img"].ap())
    ctx.dma("sync", boot_dma, maskeo[:], ctx.dram["maskeo"].ap())
    ctx.wait("sync", boot_dma)

    geo_l0 = Geo(TCORE=geo.TCORE, TILE=l0_tile)
    ctx.g = geo_l0
    ctx.phase_begin()
    l0 = L0(ctx, coff, consts_w)
    l0.build_pass(False)
    ctx.stage("scoll_1")
    ctx.op("vector", lambda e: e.tensor_scalar(
        out=l0.S[:], in0=l0.S[:], scalar1=maskeo[:, 0:1],
        scalar2=None, op0=ALU.mult))
    ctx.stage("scoll_2")
    ctx.dma("sync", l0.dmo, ctx.dram["cS_in"].ap(), l0.S[:])
    ctx.wait("sync", l0.dmo)
    ctx.stage("scoll_3")
    def coll_s(e):
        e.collective_compute(
            "AllReduce", ALU.add,
            ins=[ctx.dram["cS_in"].ap()], outs=[ctx.dram["cS_out"].ap()],
            replica_groups=rg).then_inc(cc.sem, 1)
        cc.total += 1
        e.wait_ge(cc.sem, cc.total)
    ctx.op("gpsimd", coll_s)
    l0.build_pass(True, s_init_name="cS_out", s_init_mask=maskeo[:, 1:2])
    ctx.phase_end()

    ctx.g = geo
    ctx.phase_begin()
    f0 = FFN0(ctx, coff, consts_w)
    f0.build("x1", "x2")
    ctx.phase_end()

    ctx.phase_begin()
    a = L1A(ctx, coff, consts_w)
    a.build("x2", "q1", "kv_part")
    ctx.stage("kvcoll")
    def coll_kv(e):
        e.collective_compute(
            "AllReduce", ALU.add,
            ins=[ctx.dram["kv_part"].ap()], outs=[ctx.dram["kv_tot"].ap()],
            replica_groups=rg).then_inc(cc.sem, 1)
        cc.total += 1
        e.wait_ge(cc.sem, cc.total)
    ctx.op("gpsimd", coll_kv)
    ctx.phase_end()
    ctx.phase_begin()
    b = L1B(ctx, coff, consts_w)
    b.build("kv_tot", "x2", "x3")
    ctx.phase_end()

    ctx.phase_begin()
    f1 = FFN1(ctx, coff, consts_w)
    f1.build("x3", "out")
    ctx.phase_end()

    ctx.emit()
    return nc, ctx


def make_in_maps(inputs, n_cores=8, tcore=4096):
    X = np.ascontiguousarray(np.asarray(inputs["L0_attn_hidden_states"],
                                        np.float32))
    imgs, coff, consts_w = build_images(inputs)
    in_maps = []
    for c in range(n_cores):
        b, h = c // 2, c % 2
        x = np.ascontiguousarray(X[b, h * tcore:(h + 1) * tcore])
        halo = (np.zeros((128, 1024), np.float32) if h == 0 else
                np.ascontiguousarray(X[b, tcore - 128:tcore]))
        mask = np.zeros((128, 2), np.float32)
        mask[:, h] = 1.0
        m = {"x": x, "x_halo": halo, "maskeo": mask}
        m.update(imgs)
        in_maps.append(m)
    return in_maps, imgs, coff, consts_w


def kernel(**inputs):
    tcore = 4096
    in_maps, imgs, coff, consts_w = make_in_maps(inputs, 8, tcore)
    geo = Geo(TCORE=tcore, TILE=512)
    img_shapes = {k: v.shape for k, v in imgs.items()}
    nc, ctx = build_program(geo, coff, consts_w, img_shapes, l0_tile=128)
    res = run_bass_kernel_spmd(nc, in_maps, core_ids=list(range(8)))
    out = np.zeros((4, 8192, 1024), np.float32)
    for c in range(8):
        b, h = c // 2, c % 2
        out[b, h * tcore:(h + 1) * tcore] = res.results[c]["out"]
    return out


# version 08
# speedup vs baseline: 1.1195x; 1.0321x over previous
import numpy as np
import concourse.bass as bass
from concourse.bass_utils import run_bass_kernel_spmd
import concourse.mybir as mybir

F32 = mybir.dt.float32
AF = mybir.ActivationFunctionType
ALU = mybir.AluOpType
AX = mybir.AxisListType

ENGS = ("sync", "tensor", "vector", "scalar", "gpsimd")

SCALARS = {
    "eps6": 1e-6, "eps5": 1e-5, "eps8": 1e-8, "eps12": 1e-12,
    "inv256": 1.0 / 256, "inv1024": 1.0 / 1024,
}
CNT_ENG = {"mm": "tensor", "vv": "vector", "ss": "scalar", "gg": "gpsimd"}


class Geo:
    def __init__(self, TCORE=4096, TILE=512):
        self.D = 1024
        self.H = 4
        self.DK = 256
        self.TCORE = TCORE
        self.TILE = TILE
        self.NSUB = TILE // 128
        self.NT = TCORE // TILE
        self.NCH = TILE // 32
        assert TILE % 128 == 0 and TCORE % TILE == 0


class Counter:
    def __init__(self, nc, name, dma=False):
        self.sem = nc.alloc_semaphore(name)
        self.total = 0
        self.step = 16 if dma else 1

    def bump(self):
        self.total += self.step
        return self.total


class PsumPool:

    def __init__(self, banks, base):
        self.banks = banks
        self.base = base

    def __getitem__(self, key):
        if not isinstance(key, tuple):
            key = (key, slice(None))
        ps, cs = key
        c0 = 0 if cs.start is None else cs.start
        c1 = 512 * 4 if cs.stop is None else cs.stop
        b = c0 // 512
        assert c1 - c0 <= 512 and (c1 - 1) // 512 == b, ("bank cross", c0, c1)
        return self.banks[self.base + b][ps, c0 - b * 512: c1 - b * 512]


class Ctx:
    def __init__(self, nc, geo):
        self.nc = nc
        self.g = geo
        self.deferred = []
        self.stages = []
        self.cur = None
        self.t = {}
        self.dram = {}
        self._counters = {}

    def sbuf(self, name, shape, dtype=F32):
        h = self.nc.alloc_sbuf_tensor(name, shape, dtype)
        self.t[name] = h
        return h

    def psum(self, name, shape, dtype=F32):
        h = self.nc.alloc_psum_tensor(name, shape, dtype)
        self.t[name] = h
        return h

    def counter(self, name, dma=False):
        if name not in self._counters:
            self._counters[name] = Counter(self.nc, name, dma)
        return self._counters[name]

    def din(self, name, shape, dtype=F32):
        h = self.nc.dram_tensor(name, list(shape), dtype, kind="ExternalInput")
        self.dram[name] = h
        return h

    def dout(self, name, shape, dtype=F32):
        h = self.nc.dram_tensor(name, list(shape), dtype, kind="ExternalOutput")
        self.dram[name] = h
        return h

    def phase_begin(self):
        self._ph_base = self.nc.sbuf_base
        self._ph_names = set(self.t.keys())

    def phase_end(self):
        self.nc.sbuf_base = self._ph_base
        for n in list(self.t):
            if n not in self._ph_names:
                del self.t[n]

    def alloc_shared(self, consts_w):
        self.sbuf("consts_sb", [128, consts_w])
        self.sbuf("wbuf", [128, 8 * 1024])
        self.sbuf("wbuf2", [128, 8 * 1024])
        self.sbuf("scr", [128, 512])
        banks = [self.psum("PB%d" % i, [128, 512]) for i in range(8)]
        self.t["P03"] = PsumPool(banks, 0)
        self.t["P47"] = PsumPool(banks, 4)

    def stage(self, name):
        self.cur = {e: [] for e in ENGS}
        self.cur["_name"] = name
        self.stages.append(self.cur)

    def op(self, eng, fn):
        self.cur[eng].append(fn)

    def run(self, eng, build, waits=(), inc=None):
        val = None
        if inc is not None:
            val = inc.bump()
        def fn(e, build=build, waits=tuple(waits), inc=inc):
            for c, v in waits:
                if v > 0:
                    e.wait_ge(c.sem, v)
            ins = build(e)
            if inc is not None:
                ins.then_inc(inc.sem, inc.step)
        self.op(eng, fn)
        return val

    def lock(self, e, eng, ins):
        c = self.counter("c_lk_" + eng)
        ins.then_inc(c.sem, 1)
        c.total += 1
        e.wait_ge(c.sem, c.total)
        return ins

    def dma(self, eng, cnt, out_ap, in_ap, waits=()):
        return self.run(eng,
                        lambda e: e.dma_start(out=out_ap, in_=in_ap),
                        waits=waits, inc=cnt)

    def wait(self, eng, cnt, val=None):
        v = cnt.total if val is None else val
        self.op(eng, lambda e, v=v: e.wait_ge(cnt.sem, v))

    def emit(self):
        done = {e: self.counter("done_" + e) for e in ENGS}
        stages = self.stages
        cum = {e: [0] for e in ENGS}
        for st in stages:
            for e in ENGS:
                cum[e].append(cum[e][-1] + (1 if st[e] else 0))

        def engine_body(ename):
            def body(eng):
                for i, st in enumerate(stages):
                    if not st[ename]:
                        continue
                    for o in ENGS:
                        if o != ename and cum[o][i] > 0:
                            eng.wait_ge(done[o].sem, cum[o][i])
                    for fn in st[ename]:
                        fn(eng)
                    if ename == "tensor":
                        eng.nop().then_inc(done[ename].sem, 1)
                    else:
                        eng.drain().then_inc(done[ename].sem, 1)
                for o in ENGS:
                    if o != ename and cum[o][-1] > 0:
                        eng.wait_ge(done[o].sem, cum[o][-1])
            return body

        with self.nc.Block() as block:
            block.sync(engine_body("sync"))
            block.tensor(engine_body("tensor"))
            block.vector(engine_body("vector"))
            block.scalar(engine_body("scalar"))
            block.gpsimd(engine_body("gpsimd"))



def lhsT_of(w):
    return np.ascontiguousarray(np.asarray(w).T.astype(np.float32))


def conv_w_layout(wc):
    wc = np.asarray(wc, dtype=np.float32)
    K = wc.shape[1]
    img = np.zeros((128, 8 * K), np.float32)
    for fc in range(8):
        img[:, fc * K:(fc + 1) * K] = wc[fc * 128:(fc + 1) * 128, :]
    return img


def consts_image():
    off = {}
    cols = []

    def add(name, arr):
        off[name] = sum(c.shape[1] for c in cols)
        cols.append(arr.astype(np.float32))

    add("ident", np.eye(128, dtype=np.float32))
    add("trilS", np.tile(np.tril(np.ones((32, 32), np.float32), -1), (4, 1)))
    add("triuI", np.tile(np.triu(np.ones((32, 32), np.float32), 0), (4, 1)))
    add("eye2", np.tile(2.0 * np.eye(32, dtype=np.float32), (4, 1)))
    add("eye1", np.tile(np.eye(32, dtype=np.float32), (4, 1)))
    add("ones", np.ones((128, 32), np.float32))
    for i, (nm, v) in enumerate(SCALARS.items()):
        add("s_" + nm, np.full((128, 1), v, np.float32))
    img = np.concatenate(cols, axis=1)
    return img, off



class L0:
    def __init__(self, ctx, coff, consts_w, dbg=()):
        self.ctx = ctx
        self.g = ctx.g
        self.coff = coff
        self.dbg = set(dbg)
        g = self.g
        T = g.TILE
        def get(name, shape):
            return ctx.t[name] if name in ctx.t else ctx.sbuf(name, shape)
        self.consts = get("consts_sb", [128, consts_w])
        self.x_tm = ctx.sbuf("x_tm", [128, g.NSUB * 1024])
        self.x_fm = ctx.sbuf("x_fm", [128, 8 * T])
        self.wbuf = get("wbuf", [128, 8 * 1024])
        self.bw_sb = ctx.sbuf("bw_sb", [128, 8 * 4])
        self.wconv = ctx.sbuf("wconv", [128, 3 * 8 * 4])
        self.raw = ctx.sbuf("raw", [128, 8 * (3 + T)])
        self.tail = ctx.sbuf("tail", [128, 3 * 8 * 3])
        self.k_fm = ctx.sbuf("k_fm", [128, 8 * T])
        self.VS = 63 + T
        self.v_fm = ctx.sbuf("v_fm", [128, 8 * self.VS])
        self.k_tm = ctx.sbuf("k_tm", [128, g.NSUB * 1024])
        self.v_tm = ctx.sbuf("v_tm", [128, g.NSUB * 1024])
        self.kb = ctx.sbuf("kb", [128, 1024])
        self.vb = ctx.sbuf("vb", [128, 1024])
        self.beta = ctx.sbuf("beta", [128, g.NSUB * 4])
        self.nrm = ctx.sbuf("nrm", [128, 3 * g.NSUB * 4])
        self.invT = ctx.sbuf("invT", [128, 16 * 32])
        self.nwt = ctx.sbuf("nwt", [128, 5 * 32])
        self.u_sb = ctx.sbuf("u_sb", [128, g.NSUB * 1024])
        self.w_sb = ctx.sbuf("w_sb", [128, g.NSUB * 4 * 256])
        self.uadj = ctx.sbuf("uadj", [128, 1024])
        self.S = ctx.sbuf("S", [128, 2048])
        self.scr = get("scr", [128, 512])
        self.pass2 = False
        self.P03 = ctx.t["P03"]
        self.P47 = ctx.t["P47"]
        self.PB = self.P03.banks
        c = ctx.counter
        self.dma = c("c_dma", dma=True)
        self.dmo = c("c_dmo", dma=True)
        self.mm = c("c_mm")
        self.vv = c("c_vv")
        self.ss = c("c_ss")
        self.gg = c("c_gg")
        self._tp_marks = []

    def C(self, name, w=32):
        o = self.coff[name]
        return self.consts[:, o:o + w]

    def transpose128(self, pairs, evac_engs=("vector", "scalar")):
        ctx = self.ctx
        ident = self.C("ident", 128)
        for idx0, (src, dst) in enumerate(pairs):
            idx = len(self._tp_marks)
            bo = (idx % 4) * 512
            waits = []
            if idx >= 4:
                cnt, mark = self._tp_marks[idx - 4]
                waits.append((cnt, mark))
            mmv = ctx.run("tensor",
                          lambda e, src=src, bo=bo:
                          e.transpose(self.P47[:, bo:bo + 128], src, ident),
                          waits=waits, inc=self.mm)
            eng = evac_engs[idx % len(evac_engs)]
            cnt = self.vv if eng == "vector" else self.ss
            if eng == "vector":
                mark = ctx.run(eng,
                               lambda e, dst=dst, bo=bo:
                               e.tensor_copy(dst, self.P47[:, bo:bo + 128]),
                               waits=[(self.mm, mmv)], inc=cnt)
            else:
                mark = ctx.run(eng,
                               lambda e, dst=dst, bo=bo:
                               e.copy(dst, self.P47[:, bo:bo + 128]),
                               waits=[(self.mm, mmv)], inc=cnt)
            self._tp_marks.append((cnt, mark))

    def load_w(self, wname, n_out=1024, buf=None, wait=True):
        ctx = self.ctx
        buf = self.wbuf if buf is None else buf
        for kc in range(8):
            ctx.dma("sync", self.dma,
                    buf[:, kc * n_out:(kc + 1) * n_out],
                    ctx.dram[wname].ap()[kc * 128:(kc + 1) * 128, :])
        if wait:
            ctx.wait("sync", self.dma)

    def proj_fm(self, evac, ntok=None, n_out=1024, buf=None):
        ctx, T = self.ctx, self.g.TILE
        nt = T if ntok is None else ntok
        buf = self.wbuf if buf is None else buf
        noc = n_out // 128
        marks = []
        for oc in range(noc):
            reg = 512 * (oc % 2)
            waits = [(self.vv, marks[oc - 2])] if oc >= 2 else []
            for kc in range(8):
                mmv = ctx.run("tensor",
                              lambda e, kc=kc, oc=oc, reg=reg, nt=nt,
                              n_out=n_out, buf=buf:
                              e.matmul(self.P03[:, reg:reg + nt],
                                       buf[:, kc * n_out + oc * 128:
                                           kc * n_out + oc * 128 + 128],
                                       self.x_fm[:, kc * T:kc * T + nt],
                                       start=(kc == 0), stop=(kc == 7)),
                              waits=(waits if kc == 0 else []), inc=self.mm)
            b = evac(oc)
            marks.append(ctx.run("vector", lambda e, b=b, reg=reg: b(e, reg),
                                 waits=[(self.mm, self.mm.total)], inc=self.vv))

    def conv4(self, out_fm, wcol_base, eng_name, ntok=None, out_stride=None,
              out_off=0):
        ctx, T = self.ctx, self.g.TILE
        nt = T if ntok is None else ntok
        ostr = T if out_stride is None else out_stride
        for fc in range(8):
            base = fc * (3 + T)
            for j in range(4):
                def c_fn(e, fc=fc, j=j, base=base, nt=nt, out_fm=out_fm,
                         wcol_base=wcol_base, ostr=ostr, out_off=out_off):
                    src = self.raw[:, base + j: base + j + nt]
                    dst = out_fm[:, fc * ostr + out_off:
                                 fc * ostr + out_off + nt]
                    wj = self.wconv[:, wcol_base + fc * 4 + j:
                                    wcol_base + fc * 4 + j + 1]
                    if j == 0:
                        e.tensor_scalar(out=dst, in0=src, scalar1=wj,
                                        scalar2=None, op0=ALU.mult)
                    else:
                        e.scalar_tensor_tensor(out=dst, in0=src, scalar=wj,
                                               in1=dst, op0=ALU.mult,
                                               op1=ALU.add)
                ctx.op(eng_name, c_fn)

    def margins_in(self, tail_base, eng="vector"):
        ctx, T = self.ctx, self.g.TILE
        for fc in range(8):
            ctx.op(eng, lambda e, fc=fc, tb=tail_base:
                   e.tensor_copy(self.raw[:, fc * (3 + T): fc * (3 + T) + 3],
                                 self.tail[:, tb + fc * 3: tb + fc * 3 + 3]))

    def tails_out(self, tail_base, ntok=None, eng="vector"):
        ctx, T = self.ctx, self.g.TILE
        nt = T if ntok is None else ntok
        for fc in range(8):
            ctx.op(eng, lambda e, fc=fc, tb=tail_base, nt=nt:
                   e.tensor_copy(self.tail[:, tb + fc * 3: tb + fc * 3 + 3],
                                 self.raw[:, fc * (3 + T) + nt:
                                          fc * (3 + T) + nt + 3]))

    def silu_fm(self, buf, ntok=None, stride=None, off=0):
        ctx, T = self.ctx, self.g.TILE
        nt = T if ntok is None else ntok
        st = T if stride is None else stride
        for fc in range(8):
            ctx.op("scalar", lambda e, fc=fc, nt=nt, st=st, off=off:
                   e.activation(out=buf[:, fc * st + off:fc * st + off + nt],
                                in_=buf[:, fc * st + off:fc * st + off + nt],
                                func=AF.Silu))

    def alloc_pass2(self):
        ctx, g = self.ctx, self.g
        T = g.TILE
        self.pass2 = True
        self.q_fm = ctx.sbuf("q_fm", [128, 8 * T])
        self.qnrm = ctx.sbuf("qnrm", [128, 2 * g.NSUB * 4])
        self.at_sb = ctx.sbuf("at_sb", [128, g.NSUB * 4 * 32])
        self.delta_tm = ctx.sbuf("delta_tm", [128, g.NSUB * 1024])
        self.fir_w = ctx.sbuf("fir_w", [128, 8 * 98])
        self.short_fm = ctx.sbuf("short_fm", [128, 8 * T])
        self.long_fm = ctx.sbuf("long_fm", [128, 8 * T])
        self.wide_fm = ctx.sbuf("wide_fm", [128, 8 * T])
        self.slw_tm = ctx.sbuf("slw_tm", [128, 3 * 1024])
        self.stat_tm = ctx.sbuf("stat_tm", [128, 64])
        self.gstat_fm = ctx.sbuf("gstat_fm", [128, T])
        self.hdn_fm = ctx.sbuf("hdn_fm", [128, g.NSUB * 2048])
        self.wsl = ctx.sbuf("wsl", [128, 2 * 1152])
        self.g2_sb = ctx.sbuf("g2_sb", [128, 16 * 20])
        self.g1b_sb = ctx.sbuf("g1b_sb", [128, 16])
        self.g2b_sb = ctx.sbuf("g2b_sb", [32, 4])
        self.ltexp_sb = ctx.sbuf("ltexp_sb", [32, 2])
        self.fsc_sb = ctx.sbuf("fsc_sb", [128, 12])
        self.onorm_sb = ctx.sbuf("onorm_sb", [128, 1024])
        self.wsm = ctx.sbuf("wsm", [128, 20])
        self.x1_tm = ctx.sbuf("x1_tm", [128, g.NSUB * 1024])

    def build_pass(self, with_outputs, s_init_name=None, s_init_mask=None):
        ctx, g, T = self.ctx, self.g, self.g.TILE
        if with_outputs and not self.pass2:
            self.alloc_pass2()
        ctx.stage("init" + ("_p2" if with_outputs else ""))
        ctx.dma("sync", self.dma, self.bw_sb[:], ctx.dram["bw_img"].ap())
        ctx.dma("sync", self.dma, self.wconv[:], ctx.dram["wconv_img"].ap())
        ctx.dma("sync", self.dma, self.consts[:], ctx.dram["consts_img"].ap())
        if s_init_name is not None:
            sap = ctx.dram[s_init_name]
            if list(sap.shape) == [128, 2048]:
                ctx.dma("sync", self.dma, self.S[:], sap.ap())
            else:
                ctx.dma("sync", self.dma,
                        self.S[:].rearrange("p (h c v) -> p h c v", h=4, c=2),
                        sap.ap().rearrange("h (c p) v -> p h c v", p=128))
        ctx.wait("sync", self.dma)
        if with_outputs:
            self._l0_init_post()
        if s_init_name is None:
            ctx.op("vector", lambda e: e.memset(self.S[:], 0.0))
        elif s_init_mask is not None:
            ctx.stage("sinit_mask" + ("_p2" if with_outputs else ""))
            ctx.op("vector", lambda e: e.tensor_scalar(
                out=self.S[:], in0=self.S[:], scalar1=s_init_mask,
                scalar2=None, op0=ALU.mult))
        ctx.op("vector", lambda e: e.memset(self.tail[:], 0.0))
        ctx.op("vector", lambda e: e.memset(self.raw[:], 0.0))
        ctx.op("vector", lambda e: e.memset(self.x_fm[:], 0.0))
        ctx.op("vector", lambda e: e.memset(self.uadj[:], 0.0))

        self.halo_pretile(with_outputs)
        for t in range(g.NT):
            self.tile_front(t, with_outputs)
            if with_outputs:
                self._l0_fir(t, defer=True)
            for s in range(g.NSUB):
                self.prep_subtile(t, s, with_outputs)
            for cj in range(g.NCH):
                self.scan_step(t, cj, with_outputs)
            if ctx.deferred:
                ctx.stage("fir_rem_t%d" % t)
                for eng_, fn_ in ctx.deferred:
                    ctx.op(eng_, fn_)
                ctx.deferred = []
            if with_outputs:
                self._l0_post(t)
            self.dump_dbg(t)

        ctx.stage("s_out")
        if "S_local" in ctx.dram:
            ctx.dma("sync", self.dmo,
                    ctx.dram["S_local"].ap().rearrange(
                        "h (c p) v -> p h c v", p=128),
                    self.S[:].rearrange("p (h c v) -> p h c v", h=4, c=2))
        ctx.wait("sync", self.dmo)

    def halo_pretile(self, with_outputs):
        ctx, g, T = self.ctx, self.g, self.g.TILE
        ctx.stage("halo_load")
        ctx.dma("sync", self.dma, self.x_tm[:, 0:1024],
                ctx.dram["x_halo"].ap())
        ctx.wait("sync", self.dma)
        ctx.stage("halo_tp")
        self.transpose128(
            [(self.x_tm[:, fc * 128:(fc + 1) * 128],
              self.x_fm[:, fc * T:fc * T + 128]) for fc in range(8)])
        plist = [("wk_img", 24), ("wv_img", 48)]
        if with_outputs:
            plist = [("wq_img", 0)] + plist
        for wname, tb in plist:
            ctx.stage("halo_w_" + wname)
            self.load_w(wname)
            ctx.stage("halo_" + wname)
            self.proj_fm(lambda oc: (lambda e, reg, oc=oc:
                                     e.tensor_copy(
                                         self.raw[:, oc * (3 + T) + 3:
                                                  oc * (3 + T) + 3 + 128],
                                         self.P03[:, reg:reg + 128])),
                         ntok=128)
            ctx.stage("halo_tail_" + wname)
            self.tails_out(tb, ntok=128)
            if wname == "wv_img" and with_outputs:
                ctx.stage("halo_vhist")
                self.conv4(self.v_fm, 64, "vector", ntok=128,
                           out_stride=self.VS, out_off=63)
                ctx.stage("halo_vhist_silu")
                self.silu_fm(self.v_fm, ntok=128, stride=self.VS, off=63)
                ctx.stage("halo_vhist2")
                for fc in range(8):
                    ctx.op("vector", lambda e, fc=fc:
                           e.tensor_copy(
                               self.v_fm[:, fc * self.VS:fc * self.VS + 63],
                               self.v_fm[:, fc * self.VS + 63 + 128 - 63:
                                         fc * self.VS + 63 + 128]))

    def tile_front(self, t, with_outputs):
        ctx, g, T = self.ctx, self.g, self.g.TILE
        x = ctx.dram["x"]
        for s in range(g.NSUB):
            ctx.stage("x_t%d_s%d" % (t, s))
            ctx.dma("sync", self.dma, self.x_tm[:, s * 1024:(s + 1) * 1024],
                    x.ap()[t * T + s * 128: t * T + s * 128 + 128, :])
            ctx.wait("sync", self.dma)
            ctx.stage("x_tp_t%d_s%d" % (t, s))
            self.transpose128(
                [(self.x_tm[:, s * 1024 + fc * 128:s * 1024 + (fc + 1) * 128],
                  self.x_fm[:, fc * T + s * 128:fc * T + s * 128 + 128])
                 for fc in range(8)])

        ctx.stage("beta_t%d" % t)
        for s in range(g.NSUB):
            reg = 1024 + 512 * (s % 2)
            for kc in range(8):
                ctx.run("tensor",
                        lambda e, s=s, kc=kc, reg=reg:
                        e.matmul(self.P03[:, reg:reg + 4],
                                 self.x_fm[:, kc * T + s * 128:
                                           kc * T + s * 128 + 128],
                                 self.bw_sb[:, kc * 4:(kc + 1) * 4],
                                 start=(kc == 0), stop=(kc == 7)),
                        inc=self.mm)
            ctx.run("scalar",
                    lambda e, s=s, reg=reg:
                    e.activation(out=self.beta[:, s * 4:(s + 1) * 4],
                                 in_=self.P03[:, reg:reg + 4],
                                 func=AF.Sigmoid),
                    waits=[(self.mm, self.mm.total)], inc=self.ss)

        plist = [("wk_img", self.k_fm, 24, 32, "vector", T, 0),
                 ("wv_img", self.v_fm, 48, 64, "vector", self.VS, 63)]
        if with_outputs:
            plist = [("wq_img", self.q_fm, 0, 0, "vector", T, 0)] + plist
        bufs = [self.wbuf, self.ctx.t["wbuf2"]]
        ctx.stage("ldw0_t%d" % t)
        self.load_w(plist[0][0], buf=bufs[0])
        for wi, (wname, out_buf, tb, wcol, conv_eng, ostr, ooff) in \
                enumerate(plist):
            ctx.stage("proj_%s_t%d" % (wname, t))
            if wi + 1 < len(plist):
                self.load_w(plist[wi + 1][0], buf=bufs[(wi + 1) % 2])
            self.proj_fm(lambda oc: (lambda e, reg, oc=oc:
                                     e.tensor_copy(
                                         self.raw[:, oc * (3 + T) + 3:
                                                  oc * (3 + T) + 3 + T],
                                         self.P03[:, reg:reg + T])),
                         buf=bufs[wi % 2])
            ctx.stage("conv_m_%s_t%d" % (wname, t))
            self.margins_in(tb)
            ctx.stage("conv_t_%s_t%d" % (wname, t))
            self.tails_out(tb)
            if wname == "wv_img" and t > 0:
                ctx.stage("vmargin_t%d" % t)
                for fc in range(8):
                    ctx.op("vector", lambda e, fc=fc:
                           e.tensor_copy(
                               self.v_fm[:, fc * self.VS:fc * self.VS + 63],
                               self.v_fm[:, fc * self.VS + T:
                                         fc * self.VS + T + 63]))
            ctx.stage("conv_%s_t%d" % (wname, t))
            self.conv4(out_buf, wcol, conv_eng, out_stride=ostr, out_off=ooff)
            ctx.stage("silu_%s_t%d" % (wname, t))
            self.silu_fm(out_buf, stride=ostr, off=ooff)

        ctx.stage("kv_tp_t%d" % t)
        pairs = []
        for buf_fm, buf_tm, st_, of_ in (
                (self.k_fm, self.k_tm, T, 0),
                (self.v_fm, self.v_tm, self.VS, 63)):
            for s in range(self.g.NSUB):
                for fc in range(8):
                    pairs.append((buf_fm[:, fc * st_ + of_ + s * 128:
                                         fc * st_ + of_ + s * 128 + 128],
                                  buf_tm[:, s * 1024 + fc * 128:
                                         s * 1024 + fc * 128 + 128]))
        self.transpose128(pairs)

        ctx.stage("norm_t%d" % t)
        NS4 = self.g.NSUB * 4
        for s in range(self.g.NSUB):
            for h in range(4):
                ctx.op("scalar", lambda e, s=s, h=h:
                       e.activation(out=self.scr[:, 0:256],
                                    in_=self.k_tm[:, s * 1024 + h * 256:
                                                  s * 1024 + h * 256 + 256],
                                    func=AF.Square,
                                    accum_out=self.nrm[:, s * 4 + h:
                                                       s * 4 + h + 1]))
        ctx.op("scalar", lambda e:
               e.activation(out=self.nrm[:, NS4:2 * NS4],
                            in_=self.nrm[:, 0:NS4], func=AF.Sqrt,
                            bias=self.C("s_eps6", 1)))
        if with_outputs:
            for s in range(self.g.NSUB):
                for h in range(4):
                    ctx.stage("qn_t%d_s%d_h%d" % (t, s, h))
                    for dkc in range(2):
                        fc = 2 * h + dkc
                        self.transpose128(
                            [(self.q_fm[:, fc * T + s * 128:
                                        fc * T + s * 128 + 128],
                              self.scr[:, dkc * 128:dkc * 128 + 128])],
                            evac_engs=("vector",))
                    vmark = self._tp_marks[-1][1]
                    ctx.run("scalar",
                            lambda e, s=s, h=h:
                            e.activation(out=self.scr[:, 256:512],
                                         in_=self.scr[:, 0:256],
                                         func=AF.Square,
                                         accum_out=self.qnrm[:, s * 4 + h:
                                                             s * 4 + h + 1]),
                            waits=[(self.vv, vmark)], inc=self.ss)
            ctx.stage("qn_fin_t%d" % t)
            ctx.op("scalar", lambda e:
                   e.activation(out=self.qnrm[:, NS4:2 * NS4],
                                in_=self.qnrm[:, 0:NS4], func=AF.Sqrt,
                                bias=self.C("s_eps6", 1)))
            ctx.stage("qn_fin2_t%d" % t)
            ctx.op("vector", lambda e:
                   e.reciprocal(self.qnrm[:, NS4:2 * NS4],
                                self.qnrm[:, NS4:2 * NS4]))
        ctx.stage("norm2_t%d" % t)
        def fin2(e):
            ctx.lock(e, "vector",
                     e.reciprocal(self.nrm[:, NS4:2 * NS4],
                                  self.nrm[:, NS4:2 * NS4]))
            e.tensor_mul(self.nrm[:, 2 * NS4:3 * NS4],
                         self.nrm[:, NS4:2 * NS4], self.beta[:, 0:NS4])
        ctx.op("vector", fin2)

    def prep_subtile(self, t, s, with_outputs):
        ctx, g, T = self.ctx, self.g, self.g.TILE
        NS4 = g.NSUB * 4
        ctx.stage("kscale_t%d_s%d" % (t, s))
        for h in range(4):
            def sc_fn(e, s=s, h=h, NS4=NS4):
                ksl = self.k_tm[:, s * 1024 + h * 256: s * 1024 + h * 256 + 256]
                e.tensor_scalar(out=self.kb[:, h * 256:(h + 1) * 256], in0=ksl,
                                scalar1=self.nrm[:, 2 * NS4 + s * 4 + h:
                                                 2 * NS4 + s * 4 + h + 1],
                                scalar2=None, op0=ALU.mult)
                e.tensor_scalar(out=ksl, in0=ksl,
                                scalar1=self.nrm[:, NS4 + s * 4 + h:
                                                 NS4 + s * 4 + h + 1],
                                scalar2=None, op0=ALU.mult)
                vsl = self.v_tm[:, s * 1024 + h * 256: s * 1024 + h * 256 + 256]
                e.tensor_scalar(out=self.vb[:, h * 256:(h + 1) * 256], in0=vsl,
                                scalar1=self.beta[:, s * 4 + h: s * 4 + h + 1],
                                scalar2=None, op0=ALU.mult)
            ctx.op("vector", sc_fn)

        ctx.stage("invTclr_t%d_s%d" % (t, s))
        ctx.op("vector", lambda e: e.memset(self.invT[:], 0.0))
        for h in range(4):
            ctx.stage("prep_t%d_s%d_h%d" % (t, s, h))
            self.prep_sh(t, s, h, with_outputs)

        ctx.stage("u_t%d_s%d" % (t, s))
        um = {}
        for h in range(4):
            um[h] = ctx.run("tensor",
                            lambda e, h=h:
                            e.matmul(self.PB[4 + h][:, 0:256],
                                     self.invT[:, h * 128:h * 128 + 128],
                                     self.vb[:, h * 256:(h + 1) * 256],
                                     start=True, stop=True),
                            inc=self.mm)
        for h in range(4):
            ctx.run("vector",
                    lambda e, h=h, s=s:
                    e.tensor_copy(self.u_sb[:, s * 1024 + h * 256:
                                            s * 1024 + h * 256 + 256],
                                  self.PB[4 + h][:, 0:256]),
                    waits=[(self.mm, um[h])], inc=self.vv)

    def prep_sh(self, t, s, h, with_outputs):
        ctx, g, T = self.ctx, self.g, self.g.TILE
        NS4 = g.NSUB * 4
        C = self.C
        nwt = self.nwt
        A = nwt[:, 0:32]
        AT = nwt[:, 32:64]
        X = nwt[:, 64:96]
        Z = nwt[:, 96:128]
        SC = nwt[:, 128:160]

        for j in range(4):
            for dkc in range(2):
                fc = 2 * h + dkc
                col = fc * T + s * 128 + j * 32
                ctx.run("tensor",
                        lambda e, j=j, col=col, dkc=dkc:
                        e.matmul(self.PB[4 + j][32 * j:32 * j + 32, 0:32],
                                 self.k_fm[:, col:col + 32],
                                 self.k_fm[:, col:col + 32],
                                 start=(dkc == 0), stop=(dkc == 1),
                                 tile_position=(0, 32 * j)),
                        inc=self.mm)
        gmm = self.mm.total

        def a_build(e, s=s, h=h, NS4=NS4):
            for j in range(4):
                e.tensor_scalar(out=A[32 * j:32 * j + 32, :],
                                in0=self.PB[4 + j][32 * j:32 * j + 32, 0:32],
                                scalar1=self.nrm[32 * j:32 * j + 32,
                                                 2 * NS4 + s * 4 + h:
                                                 2 * NS4 + s * 4 + h + 1],
                                scalar2=None, op0=ALU.mult)
            ctx.lock(e, "vector",
                     e.tensor_scalar(out=SC, in0=C("ones"),
                                     scalar1=self.nrm[:, NS4 + s * 4 + h:
                                                      NS4 + s * 4 + h + 1],
                                     scalar2=None, op0=ALU.mult))
            ctx.lock(e, "vector", e.transpose(Z, SC))
            ctx.lock(e, "vector", e.tensor_mul(A, A, Z))
            ctx.lock(e, "vector", e.tensor_mul(A, A, C("trilS")))
            e.transpose(AT, A)
            ctx.lock(e, "vector", e.tensor_sub(X, C("eye1"), A))
            return e.engine_nop()
        ctx.run("vector", a_build, waits=[(self.mm, gmm)], inc=self.vv)

        for it in range(4):
            yv = ctx.run("vector", lambda e: e.transpose(SC, X), inc=self.vv)
            for j in range(4):
                ctx.run("tensor",
                        lambda e, j=j:
                        e.matmul(self.PB[4 + j][32 * j:32 * j + 32, 0:32],
                                 AT[32 * j:32 * j + 32, :],
                                 X[32 * j:32 * j + 32, :],
                                 start=True, stop=True,
                                 tile_position=(32 * j, 32 * j)),
                        waits=([(self.vv, yv)] if j == 0 else []),
                        inc=self.mm)
            pv = self.mm.total

            def z_build(e):
                ctx.lock(e, "vector", e.tensor_sub(Z, C("eye2"), X))
                for j in range(4):
                    e.tensor_sub(Z[32 * j:32 * j + 32, :],
                                 Z[32 * j:32 * j + 32, :],
                                 self.PB[4 + j][32 * j:32 * j + 32, 0:32])
                return e.engine_nop()
            zv = ctx.run("vector", z_build, waits=[(self.mm, pv)], inc=self.vv)

            for j in range(4):
                ctx.run("tensor",
                        lambda e, j=j:
                        e.matmul(self.PB[4 + j][32 * j:32 * j + 32, 0:32],
                                 SC[32 * j:32 * j + 32, :],
                                 Z[32 * j:32 * j + 32, :],
                                 start=True, stop=True,
                                 tile_position=(32 * j, 32 * j)),
                        waits=([(self.vv, zv)] if j == 0 else []),
                        inc=self.mm)
            xv = self.mm.total
            last = (it == 3)

            def x_build(e, last=last, h=h):
                for j in range(4):
                    ins = e.tensor_copy(X[32 * j:32 * j + 32, :],
                                        self.PB[4 + j][32 * j:32 * j + 32,
                                                       0:32])
                ctx.lock(e, "vector", ins)
                if last:
                    ctx.lock(e, "vector", e.transpose(SC, X))
                    for j in range(4):
                        ins = e.tensor_copy(
                            self.invT[32 * j:32 * j + 32,
                                      (h * 4 + j) * 32:(h * 4 + j) * 32 + 32],
                            SC[32 * j:32 * j + 32, :])
                    ctx.lock(e, "vector", ins)
                return e.engine_nop()
            ctx.run("vector", x_build, waits=[(self.mm, xv)], inc=self.vv)

        ctx.stage("prep_w_t%d_s%d_h%d" % (t, s, h))
        wm = {}
        for dkc in range(2):
            wm[dkc] = ctx.run("tensor",
                    lambda e, dkc=dkc, h=h:
                    e.matmul(self.PB[4 + dkc][:, 0:128],
                             self.kb[:, h * 256 + dkc * 128:
                                     h * 256 + dkc * 128 + 128],
                             self.invT[:, h * 128:h * 128 + 128],
                             start=True, stop=True),
                    inc=self.mm)
        for dkc in range(2):
            ctx.run("vector",
                    lambda e, s=s, h=h, dkc=dkc:
                    e.tensor_copy(
                        self.w_sb[:, (s * 4 + h) * 256 + dkc * 128:
                                  (s * 4 + h) * 256 + dkc * 128 + 128],
                        self.PB[4 + dkc][:, 0:128]),
                    waits=[(self.mm, wm[dkc])], inc=self.vv)

        if with_outputs:
            self.prep_attnT(t, s, h)

    def prep_attnT(self, t, s, h):
        ctx, g, T = self.ctx, self.g, self.g.TILE
        NS4 = g.NSUB * 4
        ctx.stage("prep_at_t%d_s%d_h%d" % (t, s, h))
        for j in range(4):
            for dkc in range(2):
                fc = 2 * h + dkc
                col = fc * T + s * 128 + j * 32
                ctx.run("tensor",
                        lambda e, j=j, col=col, dkc=dkc:
                        e.matmul(self.PB[4 + j][32 * j:32 * j + 32, 0:32],
                                 self.k_fm[:, col:col + 32],
                                 self.q_fm[:, col:col + 32],
                                 start=(dkc == 0), stop=(dkc == 1),
                                 tile_position=(0, 32 * j)),
                        inc=self.mm)

        def at_build(e, s=s, h=h, NS4=NS4):
            for j in range(4):
                ins = e.tensor_scalar(
                    out=self.at_sb[32 * j:32 * j + 32,
                                   (s * 4 + h) * 32:(s * 4 + h) * 32 + 32],
                    in0=self.PB[4 + j][32 * j:32 * j + 32, 0:32],
                    scalar1=self.nrm[32 * j:32 * j + 32,
                                     NS4 + s * 4 + h:NS4 + s * 4 + h + 1],
                    scalar2=None, op0=ALU.mult)
            ctx.lock(e, "vector", ins)
            at = self.at_sb[:, (s * 4 + h) * 32:(s * 4 + h) * 32 + 32]
            return e.tensor_mul(at, at, self.C("triuI"))
        ctx.run("vector", at_build, waits=[(self.mm, self.mm.total)],
                inc=self.vv)

    def scan_step(self, t, cj, with_outputs):
        ctx, g, T = self.ctx, self.g, self.g.TILE
        s, jr = cj // 4, cj % 4
        p0 = 32 * jr
        ctx.stage("scan_t%d_c%d" % (t, cj))

        for h in range(4):
            for dkc in range(2):
                ctx.run("tensor",
                        lambda e, h=h, dkc=dkc, s=s, jr=jr, p0=p0:
                        e.matmul(self.PB[4 + h][p0:p0 + 32, 0:256],
                                 self.w_sb[:, (s * 4 + h) * 256 + dkc * 128 +
                                           jr * 32:
                                           (s * 4 + h) * 256 + dkc * 128 +
                                           jr * 32 + 32],
                                 self.S[:, (h * 2 + dkc) * 256:
                                        (h * 2 + dkc) * 256 + 256],
                                 start=(dkc == 0), stop=(dkc == 1),
                                 tile_position=(0, p0)),
                        inc=self.mm)
        pwsv = self.mm.total

        def ua(e, s=s, p0=p0):
            for h in range(4):
                e.tensor_sub(self.uadj[p0:p0 + 32, h * 256:(h + 1) * 256],
                             self.u_sb[p0:p0 + 32,
                                       s * 1024 + h * 256:
                                       s * 1024 + h * 256 + 256],
                             self.PB[4 + h][p0:p0 + 32, 0:256])
            return e.engine_nop()
        uav = ctx.run("vector", ua, waits=[(self.mm, pwsv)], inc=self.vv)

        if with_outputs:
            self.scan_outputs(t, cj, uav)
            supd_w = [(self.vv, self.vv.total)]
        else:
            supd_w = [(self.vv, uav)]

        first = True
        for h in range(4):
            for dkc in range(2):
                ctx.run("tensor",
                        lambda e, h=h, dkc=dkc, s=s:
                        e.matmul(self.PB[h * 2 + dkc][:, 0:256],
                                 self.k_tm[:, s * 1024 + h * 256 + dkc * 128:
                                           s * 1024 + h * 256 + dkc * 128 +
                                           128],
                                 self.uadj[:, h * 256:(h + 1) * 256],
                                 start=True, stop=True),
                        waits=(supd_w if first else []),
                        inc=self.mm)
                first = False
        summ = self.mm.total

        def s_add(e):
            for b in range(8):
                ins = e.tensor_add(self.S[:, b * 256:(b + 1) * 256],
                                   self.S[:, b * 256:(b + 1) * 256],
                                   self.PB[b][:, 0:256])
            return ins
        ctx.run("vector", s_add, waits=[(self.mm, summ)], inc=self.vv)
        ctx.run("gpsimd", lambda e, p0=p0:
                e.memset(self.uadj[p0:p0 + 32, :], 0.0),
                waits=[(self.mm, summ)], inc=self.gg)
        if ctx.deferred:
            left = g.NCH - cj
            chunk = (len(ctx.deferred) + left - 1) // left
            for eng_, fn_ in ctx.deferred[:chunk]:
                ctx.op(eng_, fn_)
            ctx.deferred = ctx.deferred[chunk:]

    def scan_outputs(self, t, cj, uav):
        ctx, g, T = self.ctx, self.g, self.g.TILE
        s, jr = cj // 4, cj % 4
        p0 = 32 * jr
        NS4 = g.NSUB * 4
        q_fm = self.q_fm
        marks = {}
        for h in range(4):
            for dkc in range(2):
                fc = 2 * h + dkc
                col = fc * T + s * 128 + jr * 32
                ctx.run("tensor",
                        lambda e, h=h, dkc=dkc, col=col, p0=p0:
                        e.matmul(self.PB[4 + h][p0:p0 + 32, 0:256],
                                 q_fm[:, col:col + 32],
                                 self.S[:, (h * 2 + dkc) * 256:
                                        (h * 2 + dkc) * 256 + 256],
                                 start=(dkc == 0), stop=False,
                                 tile_position=(0, p0)),
                        waits=([(self.vv, uav)] if h == 0 and dkc == 0
                               else []),
                        inc=self.mm)
            marks[h] = ctx.run("tensor",
                    lambda e, h=h, s=s, p0=p0:
                    e.matmul(self.PB[4 + h][p0:p0 + 32, 0:256],
                             self.at_sb[:, (s * 4 + h) * 32:
                                        (s * 4 + h) * 32 + 32],
                             self.uadj[:, h * 256:(h + 1) * 256],
                             start=False, stop=True,
                             tile_position=(0, p0)),
                    inc=self.mm)
        for h in range(4):
            ctx.run("vector",
                    lambda e, h=h, s=s, p0=p0, NS4=NS4:
                    e.tensor_scalar(
                        out=self.delta_tm[p0:p0 + 32,
                                          s * 1024 + h * 256:
                                          s * 1024 + h * 256 + 256],
                        in0=self.PB[4 + h][p0:p0 + 32, 0:256],
                        scalar1=self.qnrm[p0:p0 + 32,
                                          NS4 + s * 4 + h:
                                          NS4 + s * 4 + h + 1],
                        scalar2=None, op0=ALU.mult),
                    waits=[(self.mm, marks[h])], inc=self.vv)

    def dump_dbg(self, t):
        ctx, g = self.ctx, self.g
        if not self.dbg:
            return
        ctx.stage("dbg_t%d" % t)
        for name in sorted(self.dbg):
            buf = self.ctx.t.get(name)
            d = ctx.dram.get("dbg_" + name)
            if buf is None or d is None:
                continue
            w = list(d.shape)[2]
            ctx.dma("sync", self.dmo, d.ap()[t, :, :], buf[:, 0:w])
        ctx.wait("sync", self.dmo)



def _l0_init_post(self):
    ctx = self.ctx
    ctx.dma("sync", self.dma, self.fir_w[:], ctx.dram["fir_img"].ap())
    ctx.dma("sync", self.dma, self.g2_sb[:], ctx.dram["g2_img"].ap())
    ctx.dma("sync", self.dma, self.g1b_sb[:], ctx.dram["g1b_img"].ap())
    ctx.dma("sync", self.dma, self.g2b_sb[0:20, 0:2], ctx.dram["g2b_img"].ap())
    ctx.dma("sync", self.dma, self.ltexp_sb[0:20, 0:1], ctx.dram["lt_img"].ap())
    ctx.dma("sync", self.dma, self.fsc_sb[:, 0:4], ctx.dram["flr_img"].ap())
    ctx.dma("sync", self.dma, self.onorm_sb[:], ctx.dram["onorm_img"].ap())
    ctx.wait("sync", self.dma)
    ctx.stage("init_post")
    ctx.op("scalar", lambda e: e.activation(
        out=self.ltexp_sb[0:20, 1:2], in_=self.ltexp_sb[0:20, 0:1],
        func=AF.Exp, scale=-1.0))
    ctx.op("scalar", lambda e: e.activation(
        out=self.fsc_sb[:, 4:8], in_=self.fsc_sb[:, 0:4], func=AF.Sigmoid))
    ctx.stage("init_post2")
    def fsc2(e):
        e.tensor_scalar(out=self.fsc_sb[:, 8:12], in0=self.fsc_sb[:, 4:8],
                        scalar1=-4.0, scalar2=1.0, op0=ALU.mult, op1=ALU.add)
        ctx.lock(e, "vector",
                 e.tensor_add(self.g2b_sb[0:20, 2:3], self.g2b_sb[0:20, 0:1],
                              self.g2b_sb[0:20, 1:2]))
        return e.tensor_scalar(out=self.g2b_sb[0:20, 2:3],
                               in0=self.g2b_sb[0:20, 2:3],
                               scalar1=self.ltexp_sb[0:20, 1:2],
                               scalar2=None, op0=ALU.mult)
    ctx.op("vector", fsc2)
    ctx.op("vector", lambda e: e.memset(self.gstat_fm[:], 0.0))
    ctx.op("vector", lambda e: e.memset(self.stat_tm[:, 60:64], 0.0))


def _l0_fir(self, t, defer=False):
    ctx, T, VS = self.ctx, self.g.TILE, self.VS
    if not defer:
        ctx.stage("fir_t%d" % t)
    for K, woff, out_fm, eng in ((3, 0, self.short_fm, "vector"),
                                 (31, 3, self.long_fm, "vector"),
                                 (64, 34, self.wide_fm, "vector")):
        for fc in range(8):
            for j in range(K):
                def c_fn(e, fc=fc, j=j, K=K, woff=woff, out_fm=out_fm):
                    src = self.v_fm[:, fc * VS + 64 - K + j:
                                    fc * VS + 64 - K + j + T]
                    dst = out_fm[:, fc * T: fc * T + T]
                    wj = self.fir_w[:, fc * 98 + woff + j:
                                    fc * 98 + woff + j + 1]
                    if j == 0:
                        e.tensor_scalar(out=dst, in0=src, scalar1=wj,
                                        scalar2=None, op0=ALU.mult)
                    else:
                        e.scalar_tensor_tensor(out=dst, in0=src, scalar=wj,
                                               in1=dst, op0=ALU.mult,
                                               op1=ALU.add)
                if defer:
                    ctx.deferred.append((eng, c_fn))
                else:
                    ctx.op(eng, c_fn)


def _l0_stats_gate_sub(self, t, s):
    ctx, g, T = self.ctx, self.g, self.g.TILE
    ctx.stage("slwtp_t%d_s%d" % (t, s))
    pairs = []
    for b, buf in enumerate((self.short_fm, self.long_fm, self.wide_fm)):
        for fc in range(8):
            pairs.append((buf[:, fc * T + s * 128:fc * T + s * 128 + 128],
                          self.slw_tm[:, b * 1024 + fc * 128:
                                      b * 1024 + (fc + 1) * 128]))
    self.transpose128(pairs)

    ctx.stage("stats_t%d_s%d" % (t, s))
    branches = [self.slw_tm[:, 0:1024], self.slw_tm[:, 1024:2048],
                self.slw_tm[:, 2048:3072],
                self.delta_tm[:, s * 1024:(s + 1) * 1024],
                self.v_tm[:, s * 1024:(s + 1) * 1024]]
    for b, buf in enumerate(branches):
        for h in range(4):
            sl = buf[:, h * 256:(h + 1) * 256]
            ctx.op("scalar", lambda e, sl=sl, b=b, h=h:
                   e.activation(out=self.scr[:, 0:256], in_=sl, func=AF.Copy,
                                accum_out=self.stat_tm[:, b * 12 + h:
                                                       b * 12 + h + 1]))
            ctx.op("scalar", lambda e, sl=sl, b=b, h=h:
                   e.activation(out=self.scr[:, 256:512], in_=sl,
                                func=AF.Square,
                                accum_out=self.stat_tm[:, b * 12 + 4 + h:
                                                       b * 12 + 4 + h + 1]))
            ctx.op("vector", lambda e, sl=sl, b=b, h=h:
                   e.reduce_max(self.stat_tm[:, b * 12 + 8 + h:
                                             b * 12 + 8 + h + 1],
                                sl, axis=AX.X, apply_absolute_value=True))
    ctx.stage("statsf_t%d_s%d" % (t, s))
    sv = self.stat_tm[:, 0:60].rearrange("p (b k) -> p b k", b=5)

    def fin(e):
        e.tensor_scalar(out=sv[:, :, 0:4], in0=sv[:, :, 0:4],
                        scalar1=self.C("s_inv256", 1), scalar2=None,
                        op0=ALU.mult)
        ctx.lock(e, "vector",
                 e.tensor_scalar(out=sv[:, :, 4:8], in0=sv[:, :, 4:8],
                                 scalar1=self.C("s_inv256", 1), scalar2=None,
                                 op0=ALU.mult))
        return e.tensor_scalar(out=sv[:, :, 4:8], in0=sv[:, :, 4:8],
                               scalar1=self.C("s_eps8", 1), scalar2=None,
                               op0=ALU.max)
    ctx.op("vector", fin)
    ctx.stage("statsf2_t%d_s%d" % (t, s))
    ctx.op("scalar", lambda e:
           e.activation(out=sv[:, :, 4:8], in_=sv[:, :, 4:8], func=AF.Sqrt))
    ctx.stage("gstat_t%d_s%d" % (t, s))
    mmv = ctx.run("tensor",
                  lambda e, s=s:
                  e.transpose(self.P47[0:64, 0:128], self.stat_tm[:, 0:64],
                              self.C("ident", 128)),
                  inc=self.mm)
    ctx.run("vector",
            lambda e, s=s:
            e.tensor_copy(self.gstat_fm[0:64, s * 128:s * 128 + 128],
                          self.P47[0:64, 0:128]),
            waits=[(self.mm, mmv)], inc=self.vv)


def _l0_gate(self, t):
    ctx, g, T = self.ctx, self.g, self.g.TILE
    NSUB = g.NSUB
    ctx.stage("g1w_t%d" % t)
    ctx.dma("sync", self.dma, self.wsl[:, 0:1152],
            ctx.dram["g1_img"].ap()[0, :, :])
    ctx.wait("sync", self.dma)
    for oc in range(16):
        slab = self.wsl[:, (oc % 2) * 1152:(oc % 2) * 1152 + 1152]
        ctx.stage("g1_t%d_o%d" % (t, oc))
        if oc < 15:
            ctx.dma("sync", self.dma,
                    self.wsl[:, ((oc + 1) % 2) * 1152:
                             ((oc + 1) % 2) * 1152 + 1152],
                    ctx.dram["g1_img"].ap()[oc + 1, :, :])
            ctx.wait("sync", self.dma)
        for s in range(NSUB):
            for kc in range(9):
                rhs = (self.x_fm[:, kc * T + s * 128:kc * T + s * 128 + 128]
                       if kc < 8 else
                       self.gstat_fm[:, s * 128:s * 128 + 128])
                ctx.run("tensor",
                        lambda e, kc=kc, s=s, slab=slab, rhs=rhs:
                        e.matmul(self.P03[:, (s % 2) * 512:(s % 2) * 512 + 128],
                                 slab[:, kc * 128:(kc + 1) * 128],
                                 rhs, start=(kc == 0), stop=(kc == 8)),
                        inc=self.mm)
            ctx.run("scalar",
                    lambda e, oc=oc, s=s:
                    e.activation(out=self.hdn_fm[:, s * 2048 + oc * 128:
                                                 s * 2048 + oc * 128 + 128],
                                 in_=self.P03[:, (s % 2) * 512:
                                              (s % 2) * 512 + 128],
                                 func=AF.Gelu,
                                 bias=self.g1b_sb[:, oc:oc + 1]),
                    waits=[(self.mm, self.mm.total)], inc=self.ss)
    for s in range(NSUB):
        ctx.stage("g2_t%d_s%d" % (t, s))
        for kc in range(16):
            ctx.run("tensor",
                    lambda e, kc=kc, s=s:
                    e.matmul(self.P47[0:20, 256:384],
                             self.g2_sb[:, kc * 20:(kc + 1) * 20],
                             self.hdn_fm[:, s * 2048 + kc * 128:
                                         s * 2048 + (kc + 1) * 128],
                             start=(kc == 0), stop=(kc == 15)),
                    inc=self.mm)
        ctx.run("scalar",
                lambda e:
                e.activation(out=self.scr[0:20, 0:128],
                             in_=self.P47[0:20, 256:384], func=AF.Identity,
                             scale=self.ltexp_sb[0:20, 1:2],
                             bias=self.g2b_sb[0:20, 2:3]),
                waits=[(self.mm, self.mm.total)], inc=self.ss)
        ctx.stage("g2tp_t%d_s%d" % (t, s))
        mmv = ctx.run("tensor",
                      lambda e:
                      e.transpose(self.P47[0:128, 512:532],
                                  self.scr[0:20, 0:128],
                                  self.C("ident", 128)[0:20, 0:20]),
                      inc=self.mm)
        ctx.run("vector",
                lambda e:
                e.tensor_copy(self.wsm[:, 0:20], self.P47[0:128, 512:532]),
                waits=[(self.mm, mmv)], inc=self.vv)
        ctx.stage("sm_t%d_s%d" % (t, s))
        for h in range(4):
            def sm1(e, h=h):
                return e.reduce_max(self.scr[:, 0:1],
                                    self.wsm[:, 5 * h:5 * h + 5],
                                    axis=AX.X, negate=True)
            ctx.run("vector", sm1, inc=self.vv)
            ctx.run("scalar",
                    lambda e, h=h:
                    e.activation(out=self.wsm[:, 5 * h:5 * h + 5],
                                 in_=self.wsm[:, 5 * h:5 * h + 5],
                                 func=AF.Exp, bias=self.scr[:, 0:1],
                                 accum_out=self.scr[:, 1:2]),
                    waits=[(self.vv, self.vv.total)], inc=self.ss)
            def sm3(e, h=h):
                ctx.lock(e, "vector",
                         e.reciprocal(self.scr[:, 2:3], self.scr[:, 1:2]))
                ctx.lock(e, "vector",
                         e.tensor_scalar(out=self.wsm[:, 5 * h:5 * h + 5],
                                         in0=self.wsm[:, 5 * h:5 * h + 5],
                                         scalar1=self.scr[:, 2:3],
                                         scalar2=None, op0=ALU.mult))
                ctx.lock(e, "vector",
                         e.tensor_scalar(out=self.wsm[:, 5 * h:5 * h + 5],
                                         in0=self.wsm[:, 5 * h:5 * h + 5],
                                         scalar1=self.fsc_sb[:, 8 + h:9 + h],
                                         scalar2=None, op0=ALU.mult))
                return e.tensor_scalar(out=self.wsm[:, 5 * h:5 * h + 4],
                                       in0=self.wsm[:, 5 * h:5 * h + 4],
                                       scalar1=self.fsc_sb[:, 4 + h:5 + h],
                                       scalar2=None, op0=ALU.add)
            ctx.run("vector", sm3, waits=[(self.ss, self.ss.total)],
                    inc=self.vv)
        self._l0_mix(t, s)


def _l0_mix(self, t, s):
    ctx, g = self.ctx, self.g
    ctx.stage("mix_t%d_s%d" % (t, s))
    branches = [self.slw_tm[:, 0:1024], self.slw_tm[:, 1024:2048],
                self.slw_tm[:, 2048:3072],
                self.delta_tm[:, s * 1024:(s + 1) * 1024],
                self.v_tm[:, s * 1024:(s + 1) * 1024]]
    for h in range(4):
        for b in range(5):
            def mx(e, h=h, b=b):
                src = branches[b][:, h * 256:(h + 1) * 256]
                dst = self.slw_tm[:, h * 256:(h + 1) * 256]
                wcol = self.wsm[:, 5 * h + b:5 * h + b + 1]
                if b == 0:
                    return e.tensor_scalar(out=dst, in0=src, scalar1=wcol,
                                           scalar2=None, op0=ALU.mult)
                return e.scalar_tensor_tensor(out=dst, in0=src, scalar=wcol,
                                              in1=dst, op0=ALU.mult,
                                              op1=ALU.add)
            ctx.op("vector", mx)
    ctx.stage("rms_t%d_s%d" % (t, s))
    for h in range(4):
        ctx.op("scalar", lambda e, h=h:
               e.activation(out=self.scr[:, 0:256],
                            in_=self.slw_tm[:, h * 256:(h + 1) * 256],
                            func=AF.Square,
                            accum_out=self.scr[:, 256 + h:257 + h]))
    ctx.op("scalar", lambda e:
           e.activation(out=self.scr[:, 260:264], in_=self.scr[:, 256:260],
                        func=AF.Sqrt, scale=1.0 / 256,
                        bias=self.C("s_eps5", 1)))
    ctx.stage("rms2_t%d_s%d" % (t, s))
    def rmsf(e):
        return e.reciprocal(self.scr[:, 260:264], self.scr[:, 260:264])
    ctx.op("vector", rmsf)
    ctx.stage("rms3_t%d_s%d" % (t, s))
    for h in range(4):
        ctx.op("vector", lambda e, h=h:
               e.scalar_tensor_tensor(
                   out=self.slw_tm[:, h * 256:(h + 1) * 256],
                   in0=self.slw_tm[:, h * 256:(h + 1) * 256],
                   scalar=self.scr[:, 260 + h:261 + h],
                   in1=self.onorm_sb[:, h * 256:(h + 1) * 256],
                   op0=ALU.mult, op1=ALU.mult))
    ctx.stage("otp_t%d_s%d" % (t, s))
    T = g.TILE
    self.transpose128(
        [(self.slw_tm[:, fc * 128:(fc + 1) * 128],
          self.q_fm[:, fc * T + s * 128:fc * T + s * 128 + 128])
         for fc in range(8)])


def _l0_out(self, t, x1_name="x1"):
    ctx, g, T = self.ctx, self.g, self.g.TILE
    ctx.stage("wo_ld_t%d" % t)
    for och in range(2):
        ctx.dma("sync", self.dma,
                self.wbuf[:, och * 4096:och * 4096 + 4096]
                .rearrange("p (kc n) -> p kc n", kc=8),
                ctx.dram["wo_img"].ap()[:, och * 512:och * 512 + 512]
                .rearrange("(kc p) n -> p kc n", p=128))
    ctx.wait("sync", self.dma)
    for och in range(2):
        ctx.stage("wo_mm_t%d_o%d" % (t, och))
        for s in range(g.NSUB):
            for kc in range(8):
                ctx.run("tensor",
                        lambda e, s=s, kc=kc, och=och:
                        e.matmul(self.P03[:, (s % 2) * 512:(s % 2) * 512 + 512],
                                 self.q_fm[:, kc * T + s * 128:
                                           kc * T + s * 128 + 128],
                                 self.wbuf[:, och * 4096 + kc * 512:
                                           och * 4096 + (kc + 1) * 512],
                                 start=(kc == 0), stop=(kc == 7)),
                        inc=self.mm)
            ctx.run("vector",
                    lambda e, s=s, och=och:
                    e.tensor_add(self.x1_tm[:, s * 1024 + och * 512:
                                            s * 1024 + och * 512 + 512],
                                 self.P03[:, (s % 2) * 512:(s % 2) * 512 + 512],
                                 self.x_tm[:, s * 1024 + och * 512:
                                           s * 1024 + och * 512 + 512]),
                    waits=[(self.mm, self.mm.total)], inc=self.vv)
    for s in range(g.NSUB):
        ctx.stage("ln_t%d_s%d" % (t, s))
        ysl = self.x1_tm[:, s * 1024:(s + 1) * 1024]
        ctx.op("scalar", lambda e, ysl=ysl:
               e.activation(out=self.kb[:, 0:1024], in_=ysl, func=AF.Copy,
                            accum_out=self.scr[:, 270:271]))
        ctx.stage("ln2_t%d_s%d" % (t, s))
        ctx.op("vector", lambda e:
               e.tensor_scalar(out=self.scr[:, 271:272],
                               in0=self.scr[:, 270:271],
                               scalar1=self.C("s_inv1024", 1), scalar2=None,
                               op0=ALU.mult))
        ctx.stage("ln3_t%d_s%d" % (t, s))
        ctx.op("vector", lambda e, ysl=ysl:
               e.tensor_scalar(out=ysl, in0=ysl,
                               scalar1=self.scr[:, 271:272], scalar2=None,
                               op0=ALU.subtract))
        ctx.stage("ln4_t%d_s%d" % (t, s))
        ctx.op("scalar", lambda e, ysl=ysl:
               e.activation(out=self.kb[:, 0:1024], in_=ysl, func=AF.Square,
                            accum_out=self.scr[:, 272:273]))
        ctx.op("scalar", lambda e:
               e.activation(out=self.scr[:, 273:274], in_=self.scr[:, 272:273],
                            func=AF.Sqrt, scale=1.0 / 1024,
                            bias=self.C("s_eps5", 1)))
        ctx.stage("ln5_t%d_s%d" % (t, s))
        def lnf(e, ysl=ysl):
            ctx.lock(e, "vector",
                     e.reciprocal(self.scr[:, 273:274], self.scr[:, 273:274]))
            return e.tensor_scalar(out=ysl, in0=ysl,
                                   scalar1=self.scr[:, 273:274], scalar2=None,
                                   op0=ALU.mult)
        ctx.op("vector", lnf)
    ctx.stage("x1out_t%d" % t)
    for s in range(g.NSUB):
        ctx.dma("sync", self.dmo,
                ctx.dram[x1_name].ap()[t * T + s * 128:t * T + s * 128 + 128, :],
                self.x1_tm[:, s * 1024:(s + 1) * 1024])
    ctx.wait("sync", self.dmo)


def _l0_post(self, t):
    for s in range(self.g.NSUB):
        self._l0_stats_gate_sub(t, s)
    self._l0_gate(t)
    self._l0_out(t)


L0._l0_init_post = _l0_init_post
L0._l0_fir = _l0_fir
L0._l0_stats_gate_sub = _l0_stats_gate_sub
L0._l0_gate = _l0_gate
L0._l0_mix = _l0_mix
L0._l0_out = _l0_out
L0._l0_post = _l0_post



class FFN0:

    def __init__(self, ctx, coff, consts_w, reuse=None):
        self.ctx = ctx
        self.g = ctx.g
        self.ST = min(1024, self.g.TCORE)
        self.coff = coff
        r = reuse or {}
        def get(name, shape):
            if name in ctx.t:
                h = ctx.t[name]
                assert list(h.shape)[1] >= shape[1], (name, h.shape, shape)
                return h
            return ctx.sbuf(name, shape)
        self.consts = get("consts_sb", [128, consts_w])
        self.x1_fm = get("f_x1fm", [128, 8 * self.ST])
        self.slabs = [get("wbuf", [128, 8 * 1024]),
                      get("wbuf2", [128, 8 * 1024])]
        self.gh = get("f_gh", [128, 8 * self.ST])
        self.out = get("f_out", [128, (self.ST // 128) * 1024])
        self.gate = get("f_gate", [128, 8 * 8])
        self.cwb = get("f_cwb", [128, (self.ST // 128) * 8])
        self.prb = get("f_prb", [128, (self.ST // 128) * 8])
        self.xt = get("f_xt", [128, 1024])
        self.scr = get("scr", [128, 512])
        self.P03 = ctx.t["P03"]
        self.P47 = ctx.t["P47"]
        self.PB = self.P03.banks
        c = ctx.counter
        self.dma = c("c_dma", dma=True)
        self.dmo = c("c_dmo", dma=True)
        self.mm = c("c_mm")
        self.vv = c("c_vv")
        self.ss = c("c_ss")
        self._tpm = []

    def C(self, name, w=32):
        o = self.coff[name]
        return self.consts[:, o:o + w]

    def transpose_in(self, pairs):
        ctx = self.ctx
        ident = self.C("ident", 128)
        for (src, dst) in pairs:
            idx = len(self._tpm)
            bo = (idx % 4) * 512
            waits = [self._tpm[idx - 4]] if idx >= 4 else []
            mmv = ctx.run("tensor",
                          lambda e, src=src, bo=bo:
                          e.transpose(self.P47[:, bo:bo + 128], src, ident),
                          waits=waits, inc=self.mm)
            mark = ctx.run("vector",
                           lambda e, dst=dst, bo=bo:
                           e.tensor_copy(dst, self.P47[:, bo:bo + 128]),
                           waits=[(self.mm, mmv)], inc=self.vv)
            self._tpm.append((self.vv, mark))

    def build(self, x1_name="x1", x2_name="x2"):
        ctx, g = self.ctx, self.g
        ST = self.ST
        NSU = ST // 128
        NST = g.TCORE // ST
        ctx.stage("f_init")
        ctx.dma("sync", self.dma, self.gate[:], ctx.dram["gate8_img"].ap())
        ctx.wait("sync", self.dma)
        for st in range(NST):
            for s in range(NSU):
                ctx.stage("f_x_%d_%d" % (st, s))
                ctx.dma("sync", self.dma, self.xt[:],
                        ctx.dram[x1_name].ap()[st * ST + s * 128:
                                               st * ST + s * 128 + 128, :])
                ctx.wait("sync", self.dma)
                ctx.stage("f_xtp_%d_%d" % (st, s))
                self.transpose_in(
                    [(self.xt[:, fc * 128:(fc + 1) * 128],
                      self.x1_fm[:, fc * ST + s * 128:fc * ST + s * 128 + 128])
                     for fc in range(8)])
            ctx.stage("f_rt_%d" % st)
            rt_marks = []
            for s in range(NSU):
                for kc in range(8):
                    w = []
                    if kc == 0 and s >= 2:
                        w = [(self.vv, rt_marks[s - 2])]
                    ctx.run("tensor",
                            lambda e, s=s, kc=kc:
                            e.matmul(self.P03[:, (s % 2) * 512:(s % 2) * 512 + 8],
                                     self.x1_fm[:, kc * ST + s * 128:
                                                kc * ST + s * 128 + 128],
                                     self.gate[:, kc * 8:(kc + 1) * 8],
                                     start=(kc == 0), stop=(kc == 7)),
                            waits=w, inc=self.mm)
                rt_marks.append(ctx.run("vector",
                        lambda e, s=s:
                        e.tensor_copy(self.prb[:, s * 8:(s + 1) * 8],
                                      self.P03[:, (s % 2) * 512:
                                               (s % 2) * 512 + 8]),
                        waits=[(self.mm, self.mm.total)], inc=self.vv))
            ctx.stage("f_sm_%d" % st)
            for s in range(NSU):
                p = self.prb[:, s * 8:(s + 1) * 8]
                ctx.run("vector",
                        lambda e, p=p: e.reduce_max(self.scr[:, 0:1], p,
                                                    axis=AX.X, negate=True),
                        inc=self.vv)
                ctx.run("scalar",
                        lambda e, p=p:
                        e.activation(out=p, in_=p, func=AF.Exp,
                                     bias=self.scr[:, 0:1],
                                     accum_out=self.scr[:, 1:2]),
                        waits=[(self.vv, self.vv.total)], inc=self.ss)
                def smf(e, p=p, s=s):
                    ctx.lock(e, "vector",
                             e.reciprocal(self.scr[:, 2:3], self.scr[:, 1:2]))
                    ctx.lock(e, "vector",
                             e.tensor_scalar(out=p, in0=p,
                                             scalar1=self.scr[:, 2:3],
                                             scalar2=None, op0=ALU.mult))
                    cw = self.cwb[:, s * 8:(s + 1) * 8]
                    for ee in range(8):
                        if ee == 0:
                            ins = e.tensor_scalar(out=cw, in0=p,
                                                  scalar1=p[:, 0:1],
                                                  scalar2=None, op0=ALU.is_lt)
                        else:
                            ins = e.scalar_tensor_tensor(out=cw, in0=p,
                                                         scalar=p[:, ee:ee + 1],
                                                         in1=cw, op0=ALU.is_lt,
                                                         op1=ALU.add)
                        ctx.lock(e, "vector", ins)
                    ctx.lock(e, "vector",
                             e.tensor_scalar(out=cw, in0=cw, scalar1=2.0,
                                             scalar2=None, op0=ALU.is_lt))
                    return e.tensor_mul(cw, cw, p)
                ctx.run("vector", smf, waits=[(self.ss, self.ss.total)],
                        inc=self.vv)
            ctx.stage("f_clr_%d" % st)
            ctx.op("vector", lambda e: e.memset(self.out[:], 0.0))

            self._seq = []
            for ex in range(8):
                for half in range(2):
                    for mat in ("wg_img", "wu_img", "wd_img"):
                        self._seq.append(ctx.dram[mat].ap()[ex, half, :, :])
            self._li = 0
            ctx.stage("f_w0_%d" % st)
            self._issue_load()
            ctx.wait("sync", self.dma)
            for ex in range(8):
                for half in range(2):
                    self.expert_half(st, ex, half)

            for s in range(NSU):
                ctx.stage("f_res_%d_%d" % (st, s))
                ctx.dma("sync", self.dma, self.xt[:],
                        ctx.dram[x1_name].ap()[st * ST + s * 128:
                                               st * ST + s * 128 + 128, :])
                ctx.wait("sync", self.dma)
                ctx.stage("f_res2_%d_%d" % (st, s))
                ysl = self.out[:, s * 1024:(s + 1) * 1024]
                ctx.op("vector", lambda e, ysl=ysl:
                       e.tensor_add(ysl, ysl, self.xt[:, 0:1024]))
                ctx.stage("f_ln_%d_%d" % (st, s))
                ctx.op("scalar", lambda e, ysl=ysl:
                       e.activation(out=self.xt[:, 0:1024], in_=ysl,
                                    func=AF.Copy,
                                    accum_out=self.scr[:, 4:5]))
                ctx.stage("f_ln2_%d_%d" % (st, s))
                ctx.op("vector", lambda e, ysl=ysl:
                       e.tensor_scalar(out=self.scr[:, 5:6],
                                       in0=self.scr[:, 4:5],
                                       scalar1=self.C("s_inv1024", 1),
                                       scalar2=None, op0=ALU.mult))
                ctx.stage("f_ln3_%d_%d" % (st, s))
                ctx.op("vector", lambda e, ysl=ysl:
                       e.tensor_scalar(out=ysl, in0=ysl,
                                       scalar1=self.scr[:, 5:6], scalar2=None,
                                       op0=ALU.subtract))
                ctx.stage("f_ln4_%d_%d" % (st, s))
                ctx.op("scalar", lambda e, ysl=ysl:
                       e.activation(out=self.xt[:, 0:1024], in_=ysl,
                                    func=AF.Square,
                                    accum_out=self.scr[:, 6:7]))
                ctx.op("scalar", lambda e:
                       e.activation(out=self.scr[:, 7:8], in_=self.scr[:, 6:7],
                                    func=AF.Sqrt, scale=1.0 / 1024,
                                    bias=self.C("s_eps5", 1)))
                ctx.stage("f_ln5_%d_%d" % (st, s))
                def lnf(e, ysl=ysl):
                    ctx.lock(e, "vector",
                             e.reciprocal(self.scr[:, 7:8], self.scr[:, 7:8]))
                    return e.tensor_scalar(out=ysl, in0=ysl,
                                           scalar1=self.scr[:, 7:8],
                                           scalar2=None, op0=ALU.mult)
                ctx.op("vector", lnf)
            ctx.stage("f_out_%d" % st)
            for s in range(NSU):
                ctx.dma("sync", self.dmo,
                        ctx.dram[x2_name].ap()[st * ST + s * 128:
                                               st * ST + s * 128 + 128, :],
                        self.out[:, s * 1024:(s + 1) * 1024])
            ctx.wait("sync", self.dmo)

    def _issue_load(self):
        if self._li >= len(self._seq):
            return
        buf = self.slabs[self._li % 2]
        ap = self._seq[self._li]
        w = list(ap.shape)[-1]
        self.ctx.dma("sync", self.dma, buf[:, 0:w], ap)
        self._li += 1

    def expert_half(self, st, ex, half):
        ctx, g = self.ctx, self.g
        ST = self.ST
        NSU = ST // 128
        base = (ex * 2 + half) * 3
        sg = self.slabs[base % 2]
        su = self.slabs[(base + 1) % 2]
        sd = self.slabs[(base + 2) % 2]
        ctx.stage("f_g_%d_%d_%d" % (st, ex, half))
        self._issue_load()
        ctx.wait("sync", self.dma)
        g_marks = {}
        for oc8 in range(8):
            for nh in range(ST // 512):
                for kc in range(8):
                    w = []
                    if kc == 0 and oc8 > 0:
                        w = [(self.ss, g_marks[(oc8 - 1, nh)])]
                    ctx.run("tensor",
                            lambda e, oc8=oc8, nh=nh, kc=kc:
                            e.matmul(self.P03[:, (nh % 2) * 512:
                                              (nh % 2) * 512 + 512],
                                     sg[:, kc * 1024 + oc8 * 128:
                                               kc * 1024 + oc8 * 128 + 128],
                                     self.x1_fm[:, kc * ST + nh * 512:
                                                kc * ST + nh * 512 + 512],
                                     start=(kc == 0), stop=(kc == 7)),
                            waits=w, inc=self.mm)
                g_marks[(oc8, nh)] = ctx.run("scalar",
                        lambda e, oc8=oc8, nh=nh:
                        e.activation(out=self.gh[:, oc8 * ST + nh * 512:
                                                 oc8 * ST + nh * 512 + 512],
                                     in_=self.P03[:, (nh % 2) * 512:
                                                  (nh % 2) * 512 + 512],
                                     func=AF.Silu),
                        waits=[(self.mm, self.mm.total)], inc=self.ss)
        ctx.stage("f_u_%d_%d_%d" % (st, ex, half))
        self._issue_load()
        ctx.wait("sync", self.dma)
        u_marks = {}
        for oc8 in range(8):
            for nh in range(ST // 512):
                for kc in range(8):
                    w = []
                    if kc == 0 and oc8 > 0:
                        w = [(self.vv, u_marks[(oc8 - 1, nh)])]
                    ctx.run("tensor",
                            lambda e, oc8=oc8, nh=nh, kc=kc:
                            e.matmul(self.P03[:, (nh % 2) * 512:
                                              (nh % 2) * 512 + 512],
                                     su[:, kc * 1024 + oc8 * 128:
                                               kc * 1024 + oc8 * 128 + 128],
                                     self.x1_fm[:, kc * ST + nh * 512:
                                                kc * ST + nh * 512 + 512],
                                     start=(kc == 0), stop=(kc == 7)),
                            waits=w, inc=self.mm)
                u_marks[(oc8, nh)] = ctx.run("vector",
                        lambda e, oc8=oc8, nh=nh:
                        e.tensor_mul(self.gh[:, oc8 * ST + nh * 512:
                                             oc8 * ST + nh * 512 + 512],
                                     self.gh[:, oc8 * ST + nh * 512:
                                             oc8 * ST + nh * 512 + 512],
                                     self.P03[:, (nh % 2) * 512:
                                              (nh % 2) * 512 + 512]),
                        waits=[(self.mm, self.mm.total)], inc=self.vv)
        for s in range(NSU):
            ctx.stage("f_d_%d_%d_%d_%d" % (st, ex, half, s))
            if s == 0:
                self._issue_load()
                ctx.wait("sync", self.dma)
            for och in range(2):
                for oc8 in range(8):
                    ctx.run("tensor",
                            lambda e, s=s, och=och, oc8=oc8:
                            e.matmul(self.P47[:, 1024 + och * 512:
                                              1024 + och * 512 + 512],
                                     self.gh[:, oc8 * ST + s * 128:
                                             oc8 * ST + s * 128 + 128],
                                     sd[:, oc8 * 1024 + och * 512:
                                               oc8 * 1024 + och * 512 + 512],
                                     start=(oc8 == 0), stop=(oc8 == 7)),
                            inc=self.mm)
                ctx.run("vector",
                        lambda e, s=s, och=och, ex=ex:
                        e.scalar_tensor_tensor(
                            out=self.out[:, s * 1024 + och * 512:
                                         s * 1024 + och * 512 + 512],
                            in0=self.P47[:, 1024 + och * 512:
                                         1024 + och * 512 + 512],
                            scalar=self.cwb[:, s * 8 + ex:s * 8 + ex + 1],
                            in1=self.out[:, s * 1024 + och * 512:
                                         s * 1024 + och * 512 + 512],
                            op0=ALU.mult, op1=ALU.add),
                        waits=[(self.mm, self.mm.total)], inc=self.vv)



class L1A:

    def __init__(self, ctx, coff, consts_w):
        self.ctx = ctx
        self.g = ctx.g
        self.coff = coff
        self.T1 = 512
        T = self.T1
        def get(name, shape):
            return ctx.t[name] if name in ctx.t else ctx.sbuf(name, shape)
        self.consts = get("consts_sb", [128, consts_w])
        self.x_fm = get("a_xfm", [128, 8 * T])
        self.wbuf = get("wbuf", [128, 8 * 1024])
        self.xt = get("a_xt", [128, 1024])
        self.qkv_tm = get("a_qkv", [128, (T // 128) * 3 * 1024])
        self.kp_tm = get("a_kp", [128, 1024])
        self.bias = get("a_bias", [128, 3 * 1024])
        self.R_sb = get("a_R", [128, 64])
        self.sq_tm = get("a_sq", [128, 2 * 16])
        self.zrec = get("a_zrec", [128, 16])
        self.kv_sb = get("a_kvs", [64, 1024])
        self.scr = get("scr", [128, 512])
        self.P03 = ctx.t["P03"]
        self.P47 = ctx.t["P47"]
        self.PB = self.P03.banks
        c = ctx.counter
        self.dma = c("c_dma", dma=True)
        self.dmo = c("c_dmo", dma=True)
        self.mm = c("c_mm")
        self.vv = c("c_vv")
        self.ss = c("c_ss")
        self._tpm = []

    def C(self, name, w=32):
        o = self.coff[name]
        return self.consts[:, o:o + w]

    def transpose_in(self, pairs, banks=(0, 1)):
        ctx = self.ctx
        ident = self.C("ident", 128)
        nb = len(banks)
        for (src, dst) in pairs:
            idx = len(self._tpm)
            bo = banks[idx % nb] * 512
            waits = [self._tpm[idx - nb]] if idx >= nb else []
            mmv = ctx.run("tensor",
                          lambda e, src=src, bo=bo:
                          e.transpose(self.P47[:, bo:bo + 128], src, ident),
                          waits=waits, inc=self.mm)
            mark = ctx.run("vector",
                           lambda e, dst=dst, bo=bo:
                           e.tensor_copy(dst, self.P47[:, bo:bo + 128]),
                           waits=[(self.mm, mmv)], inc=self.vv)
            self._tpm.append((self.vv, mark))

    def build(self, x2_name="x2", q1_name="q1", kvp_name="kv_part"):
        ctx, g = self.ctx, self.g
        T = self.T1
        NSU = T // 128
        NT1 = g.TCORE // T
        ctx.stage("a_init")
        ctx.dma("sync", self.dma, self.bias[:], ctx.dram["bqkv_img"].ap())
        ctx.dma("sync", self.dma, self.R_sb[:], ctx.dram["r_img"].ap())
        ctx.wait("sync", self.dma)
        ctx.stage("a_init2")
        ctx.op("vector", lambda e: e.tensor_scalar(
            out=self.R_sb[:], in0=self.R_sb[:], scalar1=0.125, scalar2=None,
            op0=ALU.mult))
        ctx.op("vector", lambda e: e.memset(self.kv_sb[:], 0.0))

        kv_started = [False]
        for t in range(NT1):
            for s in range(NSU):
                ctx.stage("a_x_%d_%d" % (t, s))
                ctx.dma("sync", self.dma, self.xt[:],
                        ctx.dram[x2_name].ap()[t * T + s * 128:
                                               t * T + s * 128 + 128, :])
                ctx.wait("sync", self.dma)
                ctx.stage("a_xtp_%d_%d" % (t, s))
                self.transpose_in(
                    [(self.xt[:, fc * 128:(fc + 1) * 128],
                      self.x_fm[:, fc * T + s * 128:fc * T + s * 128 + 128])
                     for fc in range(8)])
            self.proj_tile(t)
            for s in range(NSU):
                self.subtile(t, s, kv_started, q1_name,
                             is_last=(t == NT1 - 1 and s == NSU - 1))
        ctx.stage("a_kvout")
        ctx.dma("sync", self.dmo, ctx.dram[kvp_name].ap(),
                self.kv_sb[0:64, 0:1024])
        ctx.wait("sync", self.dmo)

    def proj_tile(self, t):
        ctx, g, T = self.ctx, self.g, self.T1
        NSU = T // 128
        names = ("wq1_img", "wk1_img", "wv1_img")
        bufs = [self.wbuf, self.ctx.t["wbuf2"]]

        def ld(pi):
            for och in range(2):
                self.ctx.dma("sync", self.dma,
                             bufs[pi % 2][:, och * 4096:och * 4096 + 4096]
                             .rearrange("p (kc n) -> p kc n", kc=8),
                             self.ctx.dram[names[pi]]
                             .ap()[:, och * 512:och * 512 + 512]
                             .rearrange("(kc p) n -> p kc n", p=128))

        ctx.stage("a_w0_%d" % t)
        ld(0)
        ctx.wait("sync", self.dma)
        for pi, wname in enumerate(names):
            buf = bufs[pi % 2]
            ctx.stage("a_p_%d_%d" % (t, pi))
            if pi + 1 < 3:
                ld(pi + 1)
                ctx.wait("sync", self.dma)
            ev_marks = []
            for s in range(NSU):
                for och in range(2):
                    idx = s * 2 + och
                    w = [(self.vv, ev_marks[idx - 2])] if idx >= 2 else []
                    for kc in range(8):
                        ctx.run("tensor",
                                lambda e, och=och, kc=kc, s=s, buf=buf:
                                e.matmul(self.P03[:, och * 512:och * 512 + 512],
                                         self.x_fm[:, kc * T + s * 128:
                                                   kc * T + s * 128 + 128],
                                         buf[:, och * 4096 + kc * 512:
                                             och * 4096 + (kc + 1) * 512],
                                         start=(kc == 0), stop=(kc == 7)),
                                waits=(w if kc == 0 else []), inc=self.mm)
                    ev_marks.append(ctx.run(
                        "vector",
                        lambda e, och=och, pi=pi, s=s:
                        e.tensor_add(
                            self.qkv_tm[:, s * 3072 + pi * 1024 + och * 512:
                                        s * 3072 + pi * 1024 + och * 512 + 512],
                            self.P03[:, och * 512:och * 512 + 512],
                            self.bias[:, pi * 1024 + och * 512:
                                      pi * 1024 + och * 512 + 512]),
                        waits=[(self.mm, self.mm.total)], inc=self.vv))

    def subtile(self, t, s, kv_started, q1_name, is_last=False):
        ctx, g, T = self.ctx, self.g, self.T1
        if "dbg_aqkv" in ctx.dram:
            ctx.stage("a_dbg_%d_%d" % (t, s))
            ctx.dma("sync", self.dmo, ctx.dram["dbg_aqkv"].ap()[s, :, :],
                    self.qkv_tm[:, s * 3072:(s + 1) * 3072])
            ctx.wait("sync", self.dmo)
        ctx.stage("a_sq_%d_%d" % (t, s))
        for pi in range(2):
            for h in range(16):
                ctx.op("scalar", lambda e, pi=pi, h=h:
                       e.activation(out=self.scr[:, 0:64],
                                    in_=self.qkv_tm[:, s * 3072 + pi * 1024 +
                                                    h * 64:
                                                    s * 3072 + pi * 1024 +
                                                    h * 64 + 64],
                                    func=AF.Square, scale=0.125,
                                    accum_out=self.sq_tm[:, pi * 16 + h:
                                                         pi * 16 + h + 1]))
        ctx.stage("a_sq2_%d_%d" % (t, s))
        ctx.op("vector", lambda e: e.tensor_scalar(
            out=self.sq_tm[:], in0=self.sq_tm[:], scalar1=-0.5, scalar2=None,
            op0=ALU.mult))
        for pi in range(2):
            dst = self.qkv_tm if pi == 0 else self.kp_tm
            dcol = 0
            for h in range(16):
                ctx.stage("a_fm_%d_%d_%d_%d" % (t, s, pi, h))
                mmv = ctx.run("tensor",
                              lambda e, pi=pi, h=h, s=s:
                              e.transpose(self.P47[0:64, 0:128],
                                          self.qkv_tm[:, s * 3072 + pi * 1024 +
                                                      h * 64:
                                                      s * 3072 + pi * 1024 +
                                                      h * 64 + 64],
                                          self.C("ident", 128)),
                              inc=self.mm)
                vv1 = ctx.run("vector",
                              lambda e:
                              e.tensor_copy(self.scr[0:64, 0:128],
                                            self.P47[0:64, 0:128]),
                              waits=[(self.mm, mmv)], inc=self.vv)
                mmv2 = ctx.run("tensor",
                               lambda e:
                               e.matmul(self.P47[:, 256:320],
                                        self.scr[0:64, 0:128],
                                        self.R_sb[0:64, 0:64],
                                        start=True, stop=True),
                               waits=[(self.vv, vv1),
                                      (self.ss, self.ss.total)],
                               inc=self.mm)
                if pi == 0:
                    ctx.run("scalar",
                            lambda e, h=h, s=s:
                            e.activation(out=self.qkv_tm[:, s * 3072 + h * 64:
                                                         s * 3072 + h * 64 + 64],
                                         in_=self.P47[:, 256:320], func=AF.Exp,
                                         bias=self.sq_tm[:, h:h + 1],
                                         accum_out=self.zrec[:, h:h + 1]),
                            waits=[(self.mm, mmv2)], inc=self.ss)
                else:
                    ctx.run("scalar",
                            lambda e, h=h:
                            e.activation(out=self.kp_tm[:, h * 64:h * 64 + 64],
                                         in_=self.P47[:, 256:320], func=AF.Exp,
                                         bias=self.sq_tm[:, 16 + h:16 + h + 1]),
                            waits=[(self.mm, mmv2)], inc=self.ss)
        ctx.stage("a_z_%d_%d" % (t, s))
        def zf(e):
            ctx.lock(e, "vector",
                     e.tensor_scalar(out=self.zrec[:], in0=self.zrec[:],
                                     scalar1=self.C("s_eps8", 1), scalar2=None,
                                     op0=ALU.add))
            return e.reciprocal(self.zrec[:], self.zrec[:])
        ctx.op("vector", zf)
        ctx.stage("a_qz_%d_%d" % (t, s))
        for h in range(16):
            ctx.op("vector", lambda e, h=h, s=s:
                   e.tensor_scalar(out=self.qkv_tm[:, s * 3072 + h * 64:
                                                   s * 3072 + h * 64 + 64],
                                   in0=self.qkv_tm[:, s * 3072 + h * 64:
                                                   s * 3072 + h * 64 + 64],
                                   scalar1=self.zrec[:, h:h + 1], scalar2=None,
                                   op0=ALU.mult))
        ctx.stage("a_kv_%d_%d" % (t, s))
        kv_marks = {}
        for h in range(16):
            bank = 6 + h % 2
            w = [kv_marks[h - 2]] if h >= 2 else []
            mmv = ctx.run("tensor",
                          lambda e, h=h, bank=bank, s=s:
                          e.matmul(self.PB[bank][0:64, 0:64],
                                   self.kp_tm[:, h * 64:h * 64 + 64],
                                   self.qkv_tm[:, s * 3072 + 2 * 1024 + h * 64:
                                               s * 3072 + 2 * 1024 + h * 64 +
                                               64],
                                   start=True, stop=True),
                          waits=w, inc=self.mm)
            kv_marks[h] = (self.vv, ctx.run(
                "vector",
                lambda e, h=h, bank=bank:
                e.tensor_add(self.kv_sb[0:64, h * 64:h * 64 + 64],
                             self.kv_sb[0:64, h * 64:h * 64 + 64],
                             self.PB[bank][0:64, 0:64]),
                waits=[(self.mm, mmv)], inc=self.vv))
        kv_started[0] = True
        ctx.stage("a_spill_%d_%d" % (t, s))
        T1 = self.T1
        ctx.dma("sync", self.dmo,
                ctx.dram["q1"].ap()[t * T1 + s * 128:t * T1 + s * 128 + 128, :],
                self.qkv_tm[:, s * 3072:s * 3072 + 1024])
        ctx.wait("sync", self.dmo)


class L1B:

    def __init__(self, ctx, coff, consts_w):
        self.ctx = ctx
        self.g = ctx.g
        self.coff = coff
        self.T1 = 512
        T = self.T1
        def get(name, shape):
            return ctx.t[name] if name in ctx.t else ctx.sbuf(name, shape)
        self.consts = get("consts_sb", [128, consts_w])
        self.wbuf = get("wbuf", [128, 8 * 1024])
        self.xt = get("b_xt", [128, 1024])
        self.att = get("b_att", [128, (T // 128) * 1024])
        self.afm = get("b_afm", [128, 8 * T])
        self.kv2 = get("b_kv2", [64, 1024])
        self.bo = get("b_bo", [128, 1024])
        self.scr = get("scr", [128, 512])
        self.P03 = ctx.t["P03"]
        self.P47 = ctx.t["P47"]
        self.PB = self.P03.banks
        c = ctx.counter
        self.dma = c("c_dma", dma=True)
        self.dmo = c("c_dmo", dma=True)
        self.mm = c("c_mm")
        self.vv = c("c_vv")
        self.ss = c("c_ss")

    def C(self, name, w=32):
        o = self.coff[name]
        return self.consts[:, o:o + w]

    def build(self, kvt_name="kv_tot", x2_name="x2", x3_name="x3"):
        ctx, g = self.ctx, self.g
        T = self.T1
        NSU = T // 128
        NT1 = g.TCORE // T
        ctx.stage("b_init")
        ctx.dma("sync", self.dma, self.kv2[:], ctx.dram[kvt_name].ap())
        ctx.dma("sync", self.dma, self.bo[:], ctx.dram["bo_img"].ap())
        ctx.wait("sync", self.dma)
        for t in range(NT1):
            for s in range(NSU):
                ctx.stage("b_q_%d_%d" % (t, s))
                ctx.dma("sync", self.dma, self.xt[:],
                        ctx.dram["q1"].ap()[t * T + s * 128:
                                            t * T + s * 128 + 128, :])
                ctx.wait("sync", self.dma)
                out_marks = {}
                for h in range(16):
                    ctx.stage("b_h_%d_%d_%d" % (t, s, h))
                    mmv = ctx.run("tensor",
                                  lambda e, h=h:
                                  e.transpose(self.P47[0:64, 0:128],
                                              self.xt[:, h * 64:h * 64 + 64],
                                              self.C("ident", 128)),
                                  inc=self.mm)
                    vv1 = ctx.run("vector",
                                  lambda e:
                                  e.tensor_copy(self.scr[0:64, 0:128],
                                                self.P47[0:64, 0:128]),
                                  waits=[(self.mm, mmv)], inc=self.vv)
                    w = [(self.vv, vv1)]
                    if h >= 4:
                        w.append(out_marks[h - 4])
                    mmo = ctx.run("tensor",
                            lambda e, h=h:
                            e.matmul(self.PB[h % 4][:, 0:64],
                                     self.scr[0:64, 0:128],
                                     self.kv2[0:64, h * 64:h * 64 + 64],
                                     start=True, stop=True),
                            waits=w, inc=self.mm)
                    out_marks[h] = (self.vv, ctx.run(
                        "vector",
                        lambda e, h=h, s=s:
                        e.tensor_copy(self.att[:, s * 1024 + h * 64:
                                               s * 1024 + h * 64 + 64],
                                      self.PB[h % 4][:, 0:64]),
                        waits=[(self.mm, mmo)], inc=self.vv))
            ctx.stage("b_tp_%d" % t)
            prs = []
            for s in range(NSU):
                for fc in range(8):
                    prs.append((self.att[:, s * 1024 + fc * 128:
                                         s * 1024 + (fc + 1) * 128],
                                self.afm[:, fc * T + s * 128:
                                         fc * T + s * 128 + 128]))
            marks = []
            for idx, (srcp, dstp) in enumerate(prs):
                bo_ = (idx % 4) * 512
                w = [marks[idx - 4]] if idx >= 4 else []
                mmv = ctx.run("tensor",
                              lambda e, srcp=srcp, bo_=bo_:
                              e.transpose(self.P47[:, bo_:bo_ + 128], srcp,
                                          self.C("ident", 128)),
                              waits=w, inc=self.mm)
                marks.append((self.vv, ctx.run(
                    "vector",
                    lambda e, dstp=dstp, bo_=bo_:
                    e.tensor_copy(dstp, self.P47[:, bo_:bo_ + 128]),
                    waits=[(self.mm, mmv)], inc=self.vv)))
                marks[-1] = (self.vv, marks[-1][1])
            ctx.stage("b_wo_%d" % t)
            for och in range(2):
                ctx.dma("sync", self.dma,
                        self.wbuf[:, och * 4096:och * 4096 + 4096]
                        .rearrange("p (kc n) -> p kc n", kc=8),
                        ctx.dram["wo1_img"].ap()[:, och * 512:och * 512 + 512]
                        .rearrange("(kc p) n -> p kc n", p=128))
            ctx.wait("sync", self.dma)
            for och in range(2):
                ctx.stage("b_womm_%d_%d" % (t, och))
                wo_marks = []
                for s in range(NSU):
                    for kc in range(8):
                        w = []
                        if kc == 0 and s >= 2:
                            w = [(self.vv, wo_marks[s - 2])]
                        ctx.run("tensor",
                                lambda e, s=s, kc=kc, och=och:
                                e.matmul(self.P03[:, (s % 2) * 512:
                                                  (s % 2) * 512 + 512],
                                         self.afm[:, kc * T + s * 128:
                                                  kc * T + s * 128 + 128],
                                         self.wbuf[:, och * 4096 + kc * 512:
                                                   och * 4096 + (kc + 1) * 512],
                                         start=(kc == 0), stop=(kc == 7)),
                                waits=w, inc=self.mm)
                    wo_marks.append(ctx.run("vector",
                            lambda e, s=s, och=och:
                            e.tensor_add(self.att[:, s * 1024 + och * 512:
                                                 s * 1024 + och * 512 + 512],
                                         self.P03[:, (s % 2) * 512:
                                                  (s % 2) * 512 + 512],
                                         self.bo[:, och * 512:
                                                 och * 512 + 512]),
                            waits=[(self.mm, self.mm.total)], inc=self.vv))
            for s in range(NSU):
                ctx.stage("b_res_%d_%d" % (t, s))
                ctx.dma("sync", self.dma, self.xt[:],
                        ctx.dram[x2_name].ap()[t * T + s * 128:
                                               t * T + s * 128 + 128, :])
                ctx.wait("sync", self.dma)
                ctx.stage("b_res2_%d_%d" % (t, s))
                ysl = self.att[:, s * 1024:(s + 1) * 1024]
                ctx.op("vector", lambda e, ysl=ysl:
                       e.tensor_add(ysl, ysl, self.xt[:, 0:1024]))
                emit_ln(self, ysl, "b_ln_%d_%d" % (t, s))
            ctx.stage("b_out_%d" % t)
            for s in range(NSU):
                ctx.dma("sync", self.dmo,
                        ctx.dram[x3_name].ap()[t * T + s * 128:
                                               t * T + s * 128 + 128, :],
                        self.att[:, s * 1024:(s + 1) * 1024])
            ctx.wait("sync", self.dmo)


def emit_ln(ph, ysl, tag):
    ctx = ph.ctx
    ctx.stage(tag + "_a")
    ctx.op("scalar", lambda e: e.activation(out=ph.xt[:, 0:1024], in_=ysl,
                                            func=AF.Copy,
                                            accum_out=ph.scr[:, 4:5]))
    ctx.stage(tag + "_b")
    ctx.op("vector", lambda e: e.tensor_scalar(
        out=ph.scr[:, 5:6], in0=ph.scr[:, 4:5],
        scalar1=ph.C("s_inv1024", 1), scalar2=None, op0=ALU.mult))
    ctx.stage(tag + "_c")
    ctx.op("vector", lambda e: e.tensor_scalar(
        out=ysl, in0=ysl, scalar1=ph.scr[:, 5:6], scalar2=None,
        op0=ALU.subtract))
    ctx.stage(tag + "_d")
    ctx.op("scalar", lambda e: e.activation(out=ph.xt[:, 0:1024], in_=ysl,
                                            func=AF.Square,
                                            accum_out=ph.scr[:, 6:7]))
    ctx.op("scalar", lambda e: e.activation(
        out=ph.scr[:, 7:8], in_=ph.scr[:, 6:7], func=AF.Sqrt,
        scale=1.0 / 1024, bias=ph.C("s_eps5", 1)))
    ctx.stage(tag + "_e")
    def lnf(e):
        ctx.lock(e, "vector",
                 e.reciprocal(ph.scr[:, 7:8], ph.scr[:, 7:8]))
        return e.tensor_scalar(out=ysl, in0=ysl, scalar1=ph.scr[:, 7:8],
                               scalar2=None, op0=ALU.mult)
    ctx.op("vector", lnf)


class FFN1:

    def __init__(self, ctx, coff, consts_w):
        self.ctx = ctx
        self.g = ctx.g
        self.ST = min(1024, self.g.TCORE)
        self.coff = coff
        def get(name, shape):
            return ctx.t[name] if name in ctx.t else ctx.sbuf(name, shape)
        self.consts = get("consts_sb", [128, consts_w])
        self.x_fm = get("n_xfm", [128, 8 * self.ST])
        self.slabs = [get("wbuf", [128, 8 * 1024]),
                      get("wbuf2", [128, 8 * 1024])]
        self.gh = get("n_gh", [128, 4 * self.ST])
        self.out = get("n_out", [128, (self.ST // 128) * 1024])
        self.g16 = get("n_g16", [16, 1024])
        self.gwnT = get("n_gwnT", [128, 8 * 16])
        self.cwb = get("n_cwb", [128, (self.ST // 128) * 16])
        self.xt = get("n_xt", [128, 1024])
        self.scr = get("scr", [128, 512])
        self.P03 = ctx.t["P03"]
        self.P47 = ctx.t["P47"]
        self.PB = self.P03.banks
        c = ctx.counter
        self.dma = c("c_dma", dma=True)
        self.dmo = c("c_dmo", dma=True)
        self.mm = c("c_mm")
        self.vv = c("c_vv")
        self.ss = c("c_ss")
        self._tpm = []

    def C(self, name, w=32):
        o = self.coff[name]
        return self.consts[:, o:o + w]

    def transpose_in(self, pairs):
        ctx = self.ctx
        ident = self.C("ident", 128)
        for (src, dst) in pairs:
            idx = len(self._tpm)
            bo = (idx % 4) * 512
            waits = [self._tpm[idx - 4]] if idx >= 4 else []
            mmv = ctx.run("tensor",
                          lambda e, src=src, bo=bo:
                          e.transpose(self.P47[:, bo:bo + 128], src, ident),
                          waits=waits, inc=self.mm)
            mark = ctx.run("vector",
                           lambda e, dst=dst, bo=bo:
                           e.tensor_copy(dst, self.P47[:, bo:bo + 128]),
                           waits=[(self.mm, mmv)], inc=self.vv)
            self._tpm.append((self.vv, mark))

    def build(self, x3_name="x3", out_name="out"):
        ctx, g = self.ctx, self.g
        ST = self.ST
        NSU = ST // 128
        NST = g.TCORE // ST
        ctx.stage("n_init")
        ctx.dma("sync", self.dma, self.g16[:], ctx.dram["gate16_img"].ap())
        ctx.wait("sync", self.dma)
        ctx.stage("n_gwn")
        ctx.op("scalar", lambda e: e.activation(
            out=self.xt[0:16, 0:1024], in_=self.g16[:], func=AF.Square,
            accum_out=self.scr[0:16, 0:1]))
        ctx.stage("n_gwn2")
        ctx.op("scalar", lambda e: e.activation(
            out=self.scr[0:16, 1:2], in_=self.scr[0:16, 0:1], func=AF.Sqrt))
        ctx.stage("n_gwn3")
        def gwn3(e):
            ctx.lock(e, "vector",
                     e.tensor_scalar(out=self.scr[0:16, 1:2],
                                     in0=self.scr[0:16, 1:2],
                                     scalar1=self.C("s_eps12", 1)[0:16, :],
                                     scalar2=None, op0=ALU.max))
            ctx.lock(e, "vector",
                     e.reciprocal(self.scr[0:16, 1:2], self.scr[0:16, 1:2]))
            return e.tensor_scalar(out=self.g16[:], in0=self.g16[:],
                                   scalar1=self.scr[0:16, 1:2], scalar2=None,
                                   op0=ALU.mult)
        ctx.op("vector", gwn3)
        ctx.stage("n_gwn4")
        ev_marks = []
        for kc in range(8):
            bo = (kc % 4) * 512
            w = [(self.vv, ev_marks[kc - 4])] if kc >= 4 else []
            mmv = ctx.run("tensor",
                          lambda e, kc=kc, bo=bo:
                          e.transpose(self.P47[0:128, bo:bo + 16],
                                      self.g16[0:16, kc * 128:(kc + 1) * 128],
                                      self.C("ident", 128)[0:16, 0:16]),
                          waits=w, inc=self.mm)
            ev_marks.append(ctx.run(
                "vector",
                lambda e, kc=kc, bo=bo:
                e.tensor_copy(self.gwnT[:, kc * 16:(kc + 1) * 16],
                              self.P47[0:128, bo:bo + 16]),
                waits=[(self.mm, mmv)], inc=self.vv))

        if "dbg_ngwnT" in ctx.dram:
            ctx.stage("n_dbgg")
            ctx.dma("sync", self.dmo, ctx.dram["dbg_ngwnT"].ap(),
                    self.gwnT[:])
            ctx.wait("sync", self.dmo)
        for st in range(NST):
            for s in range(NSU):
                ctx.stage("n_x_%d_%d" % (st, s))
                ctx.dma("sync", self.dma, self.xt[:],
                        ctx.dram[x3_name].ap()[st * ST + s * 128:
                                               st * ST + s * 128 + 128, :])
                ctx.wait("sync", self.dma)
                ctx.stage("n_xtp_%d_%d" % (st, s))
                self.transpose_in(
                    [(self.xt[:, fc * 128:(fc + 1) * 128],
                      self.x_fm[:, fc * ST + s * 128:fc * ST + s * 128 + 128])
                     for fc in range(8)])
                ctx.stage("n_xn_%d_%d" % (st, s))
                ctx.run("scalar",
                        lambda e, s=s:
                        e.activation(out=self.xt[:, 0:1024],
                                     in_=self.xt[:, 0:1024], func=AF.Square,
                                     accum_out=self.scr[:, 8:9]),
                        inc=self.ss)
                ctx.stage("n_xn2_%d_%d" % (st, s))
                ctx.op("scalar", lambda e: e.activation(
                    out=self.scr[:, 9:10], in_=self.scr[:, 8:9], func=AF.Sqrt))
                ctx.stage("n_xn3_%d_%d" % (st, s))
                def xn3(e, s=s):
                    ctx.lock(e, "vector",
                             e.tensor_scalar(out=self.scr[:, 9:10],
                                             in0=self.scr[:, 9:10],
                                             scalar1=self.C("s_eps12", 1),
                                             scalar2=None, op0=ALU.max))
                    return e.reciprocal(self.scr[:, 40 + s:41 + s],
                                        self.scr[:, 9:10])
                ctx.op("vector", xn3)
            if "dbg_nxfm" in ctx.dram:
                ctx.stage("n_dbgx_%d" % st)
                ctx.dma("sync", self.dmo, ctx.dram["dbg_nxfm"].ap()[st, :, :],
                        self.x_fm[:])
                ctx.wait("sync", self.dmo)
            ctx.stage("n_rt_%d" % st)
            rt_marks = []
            for s in range(NSU):
                for kc in range(8):
                    w = []
                    if kc == 0 and s >= 2:
                        w = [(self.vv, rt_marks[s - 2])]
                    ctx.run("tensor",
                            lambda e, s=s, kc=kc:
                            e.matmul(self.P03[:, (s % 2) * 512:
                                              (s % 2) * 512 + 16],
                                     self.x_fm[:, kc * ST + s * 128:
                                                kc * ST + s * 128 + 128],
                                     self.gwnT[:, kc * 16:(kc + 1) * 16],
                                     start=(kc == 0), stop=(kc == 7)),
                            waits=w, inc=self.mm)
                def rtev(e, s=s):
                    if "dbg_nlg" in self.ctx.dram:
                        e.tensor_copy(self.gh[:, s * 16:(s + 1) * 16],
                                      self.P03[:, (s % 2) * 512:
                                               (s % 2) * 512 + 16])
                    e.tensor_scalar(out=self.cwb[:, s * 16:(s + 1) * 16],
                                    in0=self.P03[:, (s % 2) * 512:
                                                 (s % 2) * 512 + 16],
                                    scalar1=self.scr[:, 40 + s:41 + s],
                                    scalar2=None, op0=ALU.mult)
                    return e.engine_nop()
                rt_marks.append(ctx.run("vector", rtev,
                                        waits=[(self.mm, self.mm.total)],
                                        inc=self.vv))
            ctx.stage("n_sig_%d" % st)
            for s in range(NSU):
                ctx.op("scalar", lambda e, s=s: e.activation(
                    out=self.cwb[:, s * 16:(s + 1) * 16],
                    in_=self.cwb[:, s * 16:(s + 1) * 16], func=AF.Sigmoid))
            if "dbg_nlg" in ctx.dram:
                ctx.stage("n_dbgl_%d" % st)
                ctx.dma("sync", self.dmo, ctx.dram["dbg_nlg"].ap()[st, :, :],
                        self.gh[:, 0:128])
                ctx.wait("sync", self.dmo)
            if "dbg_nsc" in ctx.dram:
                ctx.stage("n_dbgs_%d" % st)
                ctx.dma("sync", self.dmo, ctx.dram["dbg_nsc"].ap()[st, :, :],
                        self.cwb[:])
                ctx.wait("sync", self.dmo)
            ctx.stage("n_topk_%d" % st)
            for s in range(NSU):
                def topk(e, s=s):
                    p = self.cwb[:, s * 16:(s + 1) * 16]
                    cnt = self.scr[:, 16:32]
                    for ee in range(16):
                        if ee == 0:
                            ins = e.tensor_scalar(out=cnt, in0=p,
                                                  scalar1=p[:, 0:1],
                                                  scalar2=None, op0=ALU.is_lt)
                        else:
                            ins = e.scalar_tensor_tensor(out=cnt, in0=p,
                                                         scalar=p[:, ee:ee + 1],
                                                         in1=cnt,
                                                         op0=ALU.is_lt,
                                                         op1=ALU.add)
                        ctx.lock(e, "vector", ins)
                    ctx.lock(e, "vector",
                             e.tensor_scalar(out=cnt, in0=cnt, scalar1=8.0,
                                             scalar2=None, op0=ALU.is_lt))
                    ctx.lock(e, "vector", e.tensor_mul(p, p, cnt))
                    ctx.lock(e, "vector",
                             e.reduce_sum(self.scr[:, 32:33], p, axis=AX.X))
                    ctx.lock(e, "vector",
                             e.reciprocal(self.scr[:, 33:34],
                                          self.scr[:, 32:33]))
                    return e.tensor_scalar(out=p, in0=p,
                                           scalar1=self.scr[:, 33:34],
                                           scalar2=None, op0=ALU.mult)
                ctx.op("vector", topk)
            if "dbg_ncwb" in ctx.dram:
                ctx.stage("n_dbgc_%d" % st)
                ctx.dma("sync", self.dmo, ctx.dram["dbg_ncwb"].ap()[st, :, :],
                        self.cwb[:])
                ctx.wait("sync", self.dmo)
            ctx.stage("n_clr_%d" % st)
            ctx.op("vector", lambda e: e.memset(self.out[:], 0.0))
            self._seq = []
            for ex in range(17):
                if ex == 16:
                    self._seq += [ctx.dram["swg_img"].ap(),
                                  ctx.dram["swu_img"].ap(),
                                  ctx.dram["swd_img"].ap()]
                else:
                    self._seq += [ctx.dram["wg1_img"].ap()[ex, :, :],
                                  ctx.dram["wu1_img"].ap()[ex, :, :],
                                  ctx.dram["wd1_img"].ap()[ex, :, :]]
            self._li = 0
            ctx.stage("n_w0_%d" % st)
            self._issue_load()
            ctx.wait("sync", self.dma)
            for ex in range(17):
                self.expert(st, ex)
            for s in range(NSU):
                ctx.stage("n_res_%d_%d" % (st, s))
                ctx.dma("sync", self.dma, self.xt[:],
                        ctx.dram[x3_name].ap()[st * ST + s * 128:
                                               st * ST + s * 128 + 128, :])
                ctx.wait("sync", self.dma)
                ctx.stage("n_res2_%d_%d" % (st, s))
                ysl = self.out[:, s * 1024:(s + 1) * 1024]
                ctx.op("vector", lambda e, ysl=ysl:
                       e.tensor_add(ysl, ysl, self.xt[:, 0:1024]))
                emit_ln(self, ysl, "n_ln_%d_%d" % (st, s))
            ctx.stage("n_out_%d" % st)
            for s in range(NSU):
                ctx.dma("sync", self.dmo,
                        ctx.dram[out_name].ap()[st * ST + s * 128:
                                                st * ST + s * 128 + 128, :],
                        self.out[:, s * 1024:(s + 1) * 1024])
            ctx.wait("sync", self.dmo)

    def expert(self, st, ex):
        ctx, g = self.ctx, self.g
        ST = self.ST
        NSU = ST // 128
        shared = (ex == 16)
        base = ex * 3
        sg = self.slabs[base % 2]
        su = self.slabs[(base + 1) % 2]
        sd = self.slabs[(base + 2) % 2]
        ctx.stage("n_g_%d_%d" % (st, ex))
        self._issue_load()
        ctx.wait("sync", self.dma)
        g_marks = {}
        for oc4 in range(4):
            for nh in range(ST // 512):
                for kc in range(8):
                    w = []
                    if kc == 0 and oc4 > 0:
                        w = [(self.ss, g_marks[(oc4 - 1, nh)])]
                    ctx.run("tensor",
                            lambda e, oc4=oc4, nh=nh, kc=kc:
                            e.matmul(self.P03[:, (nh % 2) * 512:
                                              (nh % 2) * 512 + 512],
                                     sg[:, kc * 512 + oc4 * 128:
                                               kc * 512 + oc4 * 128 + 128],
                                     self.x_fm[:, kc * ST + nh * 512:
                                                kc * ST + nh * 512 + 512],
                                     start=(kc == 0), stop=(kc == 7)),
                            waits=w, inc=self.mm)
                g_marks[(oc4, nh)] = ctx.run(
                    "scalar",
                    lambda e, oc4=oc4, nh=nh:
                    e.activation(out=self.gh[:, oc4 * ST + nh * 512:
                                             oc4 * ST + nh * 512 + 512],
                                 in_=self.P03[:, (nh % 2) * 512:
                                              (nh % 2) * 512 + 512],
                                 func=AF.Silu),
                    waits=[(self.mm, self.mm.total)], inc=self.ss)
        ctx.stage("n_u_%d_%d" % (st, ex))
        self._issue_load()
        ctx.wait("sync", self.dma)
        u_marks = {}
        for oc4 in range(4):
            for nh in range(ST // 512):
                for kc in range(8):
                    w = []
                    if kc == 0 and oc4 > 0:
                        w = [(self.vv, u_marks[(oc4 - 1, nh)])]
                    ctx.run("tensor",
                            lambda e, oc4=oc4, nh=nh, kc=kc:
                            e.matmul(self.P03[:, (nh % 2) * 512:
                                              (nh % 2) * 512 + 512],
                                     su[:, kc * 512 + oc4 * 128:
                                               kc * 512 + oc4 * 128 + 128],
                                     self.x_fm[:, kc * ST + nh * 512:
                                                kc * ST + nh * 512 + 512],
                                     start=(kc == 0), stop=(kc == 7)),
                            waits=w, inc=self.mm)
                u_marks[(oc4, nh)] = ctx.run(
                    "vector",
                    lambda e, oc4=oc4, nh=nh:
                    e.tensor_mul(self.gh[:, oc4 * ST + nh * 512:
                                         oc4 * ST + nh * 512 + 512],
                                 self.gh[:, oc4 * ST + nh * 512:
                                         oc4 * ST + nh * 512 + 512],
                                 self.P03[:, (nh % 2) * 512:
                                              (nh % 2) * 512 + 512]),
                    waits=[(self.mm, self.mm.total)], inc=self.vv)
        for s in range(NSU):
            ctx.stage("n_d_%d_%d_%d" % (st, ex, s))
            if s == 0:
                self._issue_load()
                ctx.wait("sync", self.dma)
            for och in range(2):
                for oc4 in range(4):
                    ctx.run("tensor",
                            lambda e, s=s, och=och, oc4=oc4:
                            e.matmul(self.P47[:, 1024 + och * 512:
                                              1024 + och * 512 + 512],
                                     self.gh[:, oc4 * ST + s * 128:
                                             oc4 * ST + s * 128 + 128],
                                     sd[:, oc4 * 1024 + och * 512:
                                               oc4 * 1024 + och * 512 + 512],
                                     start=(oc4 == 0), stop=(oc4 == 3)),
                            inc=self.mm)
                def dev(e, s=s, och=och, ex=ex, shared=shared):
                    o = self.out[:, s * 1024 + och * 512:
                                 s * 1024 + och * 512 + 512]
                    if shared:
                        return e.tensor_add(o, o,
                                            self.P47[:, 1024 + och * 512:
                                                     1024 + och * 512 + 512])
                    return e.scalar_tensor_tensor(
                        out=o, in0=self.P47[:, 1024 + och * 512:
                                            1024 + och * 512 + 512],
                        scalar=self.cwb[:, s * 16 + ex:s * 16 + ex + 1],
                        in1=o, op0=ALU.mult, op1=ALU.add)
                ctx.run("vector", dev, waits=[(self.mm, self.mm.total)],
                        inc=self.vv)


FFN1._issue_load = FFN0._issue_load



RG_PAIRS = [[0, 1], [2, 3], [4, 5], [6, 7]]



def g1_image(g1_w):
    g1T = np.zeros((1152, 2048), np.float32)
    g1T[:1084] = np.asarray(g1_w).T
    img = np.zeros((16, 128, 1152), np.float32)
    for oc in range(16):
        for kc in range(9):
            img[oc, :, kc * 128:(kc + 1) * 128] = \
                g1T[kc * 128:(kc + 1) * 128, oc * 128:(oc + 1) * 128]
    return img


def g2_image(g2_w):
    g2T = np.asarray(g2_w).T
    img = np.zeros((128, 16 * 20), np.float32)
    for kc in range(16):
        img[:, kc * 20:(kc + 1) * 20] = g2T[kc * 128:(kc + 1) * 128, :]
    return img


def fir_image(fs, fl, fw):
    f3 = np.asarray(fs).reshape(1024, 3)
    f31 = np.asarray(fl).reshape(1024, 31)
    f64 = np.asarray(fw).reshape(1024, 64)
    img = np.zeros((128, 8 * 98), np.float32)
    for fc in range(8):
        rows = slice(fc * 128, (fc + 1) * 128)
        img[:, fc * 98 + 0:fc * 98 + 3] = f3[rows]
        img[:, fc * 98 + 3:fc * 98 + 34] = f31[rows]
        img[:, fc * 98 + 34:fc * 98 + 98] = f64[rows]
    return img


def wgu_image(w):
    w = np.asarray(w)
    img = np.zeros((8, 2, 128, 8192), np.float32)
    for e in range(8):
        for half in range(2):
            for kc in range(8):
                for oc8 in range(8):
                    img[e, half, :, kc * 1024 + oc8 * 128:
                        kc * 1024 + oc8 * 128 + 128] = \
                        w[e][kc * 128:(kc + 1) * 128,
                             (half * 8 + oc8) * 128:(half * 8 + oc8) * 128 + 128]
    return img


def wd_image(w):
    w = np.asarray(w)
    img = np.zeros((8, 2, 128, 8192), np.float32)
    for e in range(8):
        for half in range(2):
            for oc8 in range(8):
                img[e, half, :, oc8 * 1024:(oc8 + 1) * 1024] = \
                    w[e][(half * 8 + oc8) * 128:(half * 8 + oc8) * 128 + 128, :]
    return img


def wgu1_image(w):
    w = np.asarray(w)
    img = np.zeros((16, 128, 4096), np.float32)
    for e in range(16):
        for kc in range(8):
            img[e, :, kc * 512:(kc + 1) * 512] = w[e][kc * 128:(kc + 1) * 128]
    return img


def wd1_image(w):
    w = np.asarray(w)
    img = np.zeros((16, 128, 4096), np.float32)
    for e in range(16):
        for oc4 in range(4):
            img[e, :, oc4 * 1024:(oc4 + 1) * 1024] = \
                w[e][oc4 * 128:(oc4 + 1) * 128]
    return img


def sw_image(w, down=False):
    w = np.asarray(w)
    img = np.zeros((128, 4096), np.float32)
    if down:
        for oc4 in range(4):
            img[:, oc4 * 1024:(oc4 + 1) * 1024] = w[oc4 * 128:(oc4 + 1) * 128]
    else:
        for kc in range(8):
            img[:, kc * 512:(kc + 1) * 512] = w[kc * 128:(kc + 1) * 128]
    return img


def build_images(I):
    cimg, coff = consts_image()
    vb20 = np.zeros(20, np.float32)
    vb20[4::5] = np.asarray(I["L0_attn_value_bias"])
    imgs = {
        "consts_img": cimg,
        "wq_img": lhsT_of(I["L0_attn_q_w"]),
        "wk_img": lhsT_of(I["L0_attn_k_w"]),
        "wv_img": lhsT_of(I["L0_attn_v_w"]),
        "bw_img": conv_w_layout(lhsT_of(I["L0_attn_b_w"])),
        "wconv_img": np.concatenate(
            [conv_w_layout(I["L0_attn_convq_w"]),
             conv_w_layout(I["L0_attn_convk_w"]),
             conv_w_layout(I["L0_attn_convv_w"])], axis=1),
        "fir_img": fir_image(I["L0_attn_fir_s"], I["L0_attn_fir_l"],
                             I["L0_attn_fir_wd"]),
        "g1_img": g1_image(I["L0_attn_g1_w"]),
        "g2_img": g2_image(I["L0_attn_g2_w"]),
        "g1b_img": np.ascontiguousarray(
            np.asarray(I["L0_attn_g1_b"]).reshape(16, 128).T),
        "g2b_img": np.ascontiguousarray(
            np.stack([np.asarray(I["L0_attn_g2_b"]), vb20], 1)),
        "lt_img": np.ascontiguousarray(
            np.repeat(np.asarray(I["L0_attn_log_temp"]), 5).reshape(20, 1)),
        "flr_img": np.ascontiguousarray(
            np.tile(np.asarray(I["L0_attn_floor_logit"]), (128, 1))),
        "onorm_img": np.ascontiguousarray(
            np.tile(np.asarray(I["L0_attn_onorm_w"]), (128, 4))),
        "wo_img": lhsT_of(I["L0_attn_o_w"]),
        "gate8_img": conv_w_layout(lhsT_of(I["L0_ffn_gate_w"])),
        "wg_img": wgu_image(I["L0_ffn_wg"]),
        "wu_img": wgu_image(I["L0_ffn_wu"]),
        "wd_img": wd_image(I["L0_ffn_wd"]),
        "wq1_img": lhsT_of(I["L1_attn_Wq"]),
        "wk1_img": lhsT_of(I["L1_attn_Wk"]),
        "wv1_img": lhsT_of(I["L1_attn_Wv"]),
        "wo1_img": lhsT_of(I["L1_attn_Wo"]),
        "bqkv_img": np.concatenate(
            [np.tile(np.asarray(I["L1_attn_b" + n]), (128, 1))
             for n in ("q", "k", "v")], 1).astype(np.float32),
        "bo_img": np.ascontiguousarray(
            np.tile(np.asarray(I["L1_attn_bo"]), (128, 1))).astype(np.float32),
        "r_img": np.concatenate([np.asarray(I["L1_attn_R"], np.float32),
                                 np.zeros((64, 64), np.float32)], 0),
        "gate16_img": np.asarray(I["L1_ffn_gate_w"], np.float32),
        "wg1_img": wgu1_image(I["L1_ffn_w_gate"]),
        "wu1_img": wgu1_image(I["L1_ffn_w_up"]),
        "wd1_img": wd1_image(I["L1_ffn_w_down"]),
        "swg_img": sw_image(I["L1_ffn_sw_gate"]),
        "swu_img": sw_image(I["L1_ffn_sw_up"]),
        "swd_img": sw_image(I["L1_ffn_sw_down"], down=True),
    }
    return imgs, coff, cimg.shape[1]



def build_program(geo, coff, consts_w, img_shapes, l0_tile=128,
                  rg=None):
    if rg is None:
        rg = RG_PAIRS
    nc = bass.Bass("TRN2", target_bir_lowering=False,
                   detect_race_conditions=False)
    ctx = Ctx(nc, geo)
    D = 1024
    ctx.din("x", [geo.TCORE, D])
    ctx.din("x_halo", [128, D])
    ctx.din("maskeo", [128, 2])
    for name, shp in img_shapes.items():
        ctx.din(name, list(shp))
    for name in ("x1", "x2", "q1", "x3"):
        ctx.dram[name] = nc.dram_tensor(name, [geo.TCORE, D], F32,
                                        kind="Internal")
    ctx.dram["cS_in"] = nc.dram_tensor("cS_in", [128, 2048], F32,
                                       kind="Internal")
    ctx.dram["cS_out"] = nc.dram_tensor("cS_out", [128, 2048], F32,
                                        kind="Internal")
    ctx.dram["kv_part"] = nc.dram_tensor("kv_part", [64, 1024], F32,
                                         kind="Internal")
    ctx.dram["kv_tot"] = nc.dram_tensor("kv_tot", [64, 1024], F32,
                                        kind="Internal")
    ctx.dout("out", [geo.TCORE, D])

    ctx.alloc_shared(consts_w)
    maskeo = ctx.sbuf("maskeo_sb", [128, 2])
    boot_dma = ctx.counter("c_dma", dma=True)
    cc = ctx.counter("c_cc")
    ctx.stage("boot")
    ctx.dma("sync", boot_dma, ctx.t["consts_sb"][:],
            ctx.dram["consts_img"].ap())
    ctx.dma("sync", boot_dma, maskeo[:], ctx.dram["maskeo"].ap())
    ctx.wait("sync", boot_dma)

    geo_l0 = Geo(TCORE=geo.TCORE, TILE=l0_tile)
    ctx.g = geo_l0
    ctx.phase_begin()
    l0 = L0(ctx, coff, consts_w)
    l0.build_pass(False)
    ctx.stage("scoll_1")
    ctx.op("vector", lambda e: e.tensor_scalar(
        out=l0.S[:], in0=l0.S[:], scalar1=maskeo[:, 0:1],
        scalar2=None, op0=ALU.mult))
    ctx.stage("scoll_2")
    ctx.dma("sync", l0.dmo, ctx.dram["cS_in"].ap(), l0.S[:])
    ctx.wait("sync", l0.dmo)
    ctx.stage("scoll_3")
    def coll_s(e):
        e.collective_compute(
            "AllReduce", ALU.add,
            ins=[ctx.dram["cS_in"].ap()], outs=[ctx.dram["cS_out"].ap()],
            replica_groups=rg).then_inc(cc.sem, 1)
        cc.total += 1
        e.wait_ge(cc.sem, cc.total)
    ctx.op("gpsimd", coll_s)
    l0.build_pass(True, s_init_name="cS_out", s_init_mask=maskeo[:, 1:2])
    ctx.phase_end()

    ctx.g = geo
    ctx.phase_begin()
    f0 = FFN0(ctx, coff, consts_w)
    f0.build("x1", "x2")
    ctx.phase_end()

    ctx.phase_begin()
    a = L1A(ctx, coff, consts_w)
    a.build("x2", "q1", "kv_part")
    ctx.stage("kvcoll")
    def coll_kv(e):
        e.collective_compute(
            "AllReduce", ALU.add,
            ins=[ctx.dram["kv_part"].ap()], outs=[ctx.dram["kv_tot"].ap()],
            replica_groups=rg).then_inc(cc.sem, 1)
        cc.total += 1
        e.wait_ge(cc.sem, cc.total)
    ctx.op("gpsimd", coll_kv)
    ctx.phase_end()
    ctx.phase_begin()
    b = L1B(ctx, coff, consts_w)
    b.build("kv_tot", "x2", "x3")
    ctx.phase_end()

    ctx.phase_begin()
    f1 = FFN1(ctx, coff, consts_w)
    f1.build("x3", "out")
    ctx.phase_end()

    ctx.emit()
    return nc, ctx


def make_in_maps(inputs, n_cores=8, tcore=4096):
    X = np.ascontiguousarray(np.asarray(inputs["L0_attn_hidden_states"],
                                        np.float32))
    imgs, coff, consts_w = build_images(inputs)
    in_maps = []
    for c in range(n_cores):
        b, h = c // 2, c % 2
        x = np.ascontiguousarray(X[b, h * tcore:(h + 1) * tcore])
        halo = (np.zeros((128, 1024), np.float32) if h == 0 else
                np.ascontiguousarray(X[b, tcore - 128:tcore]))
        mask = np.zeros((128, 2), np.float32)
        mask[:, h] = 1.0
        m = {"x": x, "x_halo": halo, "maskeo": mask}
        m.update(imgs)
        in_maps.append(m)
    return in_maps, imgs, coff, consts_w


def kernel(**inputs):
    tcore = 4096
    in_maps, imgs, coff, consts_w = make_in_maps(inputs, 8, tcore)
    geo = Geo(TCORE=tcore, TILE=512)
    img_shapes = {k: v.shape for k, v in imgs.items()}
    nc, ctx = build_program(geo, coff, consts_w, img_shapes, l0_tile=128)
    res = run_bass_kernel_spmd(nc, in_maps, core_ids=list(range(8)))
    out = np.zeros((4, 8192, 1024), np.float32)
    for c in range(8):
        b, h = c // 2, c % 2
        out[b, h * tcore:(h + 1) * tcore] = res.results[c]["out"]
    return out
